# version 20
# baseline (speedup 1.0000x reference)
import numpy as np
import ml_dtypes
import concourse.bass as bass
import concourse.mybir as mybir
from concourse.bass_utils import run_bass_kernel_spmd
from contextlib import ExitStack

F32 = mybir.dt.float32
BF16 = mybir.dt.bfloat16
I32 = mybir.dt.int32
AF = mybir.ActivationFunctionType
ALU = mybir.AluOpType
AXL = mybir.AxisListType

B, N, D, H, HD = 4, 4096, 1024, 16, 64
NT = N // 128
HALF = 2048
EPS_LN = 1e-5
EPS_RMS = 1e-6
CONST_VALS = (1e-5, 1.0, 1e-6, 1.0 / 1024.0, -0.5, -1.0)
SAME_ENGINE_WAITS = True

ENGINES = ("sync", "vector", "scalar", "tensor", "gpsimd")


class Builder:
    def __init__(self, nc):
        self.nc = nc
        self.streams = {e: [] for e in ENGINES}
        self.sem_handles = {}
        self.ctr = {}
        self.last_wait = {}
        self.free_ev = {}
        self.prod_ev = {}
        self.engine_sem = {"sync": "L", "vector": "V", "scalar": "A",
                           "tensor": "P", "gpsimd": "G"}

    DMA_SEMS = ("C", "wld0", "wld1", "wld2", "wld3", "x0", "x1", "wo0", "wo1", "wfA0", "wfA1",
                "hw0", "hw1", "w3l0", "w3l1", "xo0", "xo1", "xmr0", "xmr1",
                "hbr", "ogld", "og0", "og1", "hp0", "hp1", "xs0", "xs1",
                "os0", "os1")

    def init_sems(self, ctx):
        for s in ("V", "A", "P", "G", "O") + self.DMA_SEMS:
            self.sem_handles[s] = ctx.enter_context(self.nc.semaphore("sem_" + s))
            self.ctr[s] = 0
        for s in ("L", "W", "X", "S"):
            self.ctr.setdefault(s, 0)
        self.era = 0
        self.era_barriers = {0: []}
        self.engine_era = {}
        self.tensor_era = {}

    def new_era(self):
        self.era += 1
        self.era_barriers[self.era] = [(s, v) for s, v in self.ctr.items() if v > 0]

    def tag(self, handle):
        self.tensor_era[handle.name] = self.era
        return handle

    def totals(self, sems=("P", "V", "A")):
        return [(s, self.ctr[s]) for s in sems]

    def _wait(self, eng, ev):
        sem, val = ev
        if self.engine_sem[eng] == sem:
            assert val <= self.ctr[sem], f"{eng} waiting future own event {ev}"
            if not SAME_ENGINE_WAITS:
                return None
        key = (eng, sem)
        if self.last_wait.get(key, 0) >= val:
            return None
        self.last_wait[key] = val
        return (self.sem_handles[sem], val)

    def _prep(self, eng, waits, inc, inc_n, out_name=None):
        if out_name is not None:
            te = self.tensor_era.get(out_name, 0)
            if te > self.engine_era.get(eng, 0):
                self.engine_era[eng] = te
                waits = list(waits) + self.era_barriers[te]
        wl = []
        for ev in waits:
            if ev is None:
                continue
            w = self._wait(eng, ev)
            if w is not None:
                wl.append(w)
        sem_name = inc if isinstance(inc, str) else (self.engine_sem[eng] if inc else None)
        if sem_name is not None:
            self.ctr[sem_name] += inc_n
            return wl, (sem_name, self.ctr[sem_name]), self.sem_handles[sem_name], inc_n
        return wl, None, None, 0

    def op(self, eng, method, *args, waits=(), inc=None, inc_n=1, **kwargs):
        out_ap = kwargs.get("out")
        if out_ap is None and args:
            out_ap = args[0]
        out_name = getattr(out_ap, "name", None) if out_ap is not None else None
        wl, ev_out, sem_h, k = self._prep(eng, waits, inc, inc_n, out_name)

        def fn(engine, wl=wl, method=method, args=args, kwargs=kwargs, sem_h=sem_h, k=k):
            for sem, val in wl:
                engine.wait_ge(sem, val)
            inst = getattr(engine, method)(*args, **kwargs)
            if sem_h is not None:
                inst.then_inc(sem_h, k)
            return inst

        self.streams[eng].append(fn)
        return ev_out

    def raw(self, eng, fn, waits=(), inc=None, inc_n=1, out_name=None):
        wl, ev_out, sem_h, k = self._prep(eng, waits, inc, inc_n, out_name)

        def wrapped(engine, wl=wl, fn=fn, sem_h=sem_h, k=k):
            for sem, val in wl:
                engine.wait_ge(sem, val)

            def post(inst):
                if sem_h is not None:
                    inst.then_inc(sem_h, k)
            fn(engine, post)

        self.streams[eng].append(wrapped)
        return ev_out

    def wait_free(self, name):
        return self.free_ev.get(name)

    def set_prod(self, name, ev):
        if ev is not None:
            self.prod_ev[name] = ev

    def get_prod(self, name):
        return self.prod_ev.get(name)

    def set_free(self, name, ev):
        if ev is not None:
            self.free_ev[name] = ev


def build_nc():
    nc = bass.Bass("TRN2", target_bir_lowering=False, debug=False)
    dt = nc.dram_tensor

    xb = dt("xb", [N, D], F32, kind="ExternalInput").ap()
    x_own = dt("x_own", [HALF, D], F32, kind="ExternalInput").ap()
    half_ofs = dt("half_ofs", [1, 1], I32, kind="ExternalInput").ap()
    Wq_h = dt("Wq", [D, D], F32, kind="ExternalInput").ap()
    Wk_h = dt("Wk", [D, D], F32, kind="ExternalInput").ap()
    Wv_h = dt("Wv", [D, D], F32, kind="ExternalInput").ap()
    Wo_h = dt("Wo", [D, D], F32, kind="ExternalInput").ap()
    Wg1_h = dt("Wg1", [D, H], F32, kind="ExternalInput").ap()
    Wg2_h = dt("Wg2", [H, D], F32, kind="ExternalInput").ap()
    nwc_h = dt("nwc", [128, 8], F32, kind="ExternalInput").ap()
    cosim_h = dt("cosim", [128, NT * HD], F32, kind="ExternalInput").ap()
    sinim_h = dt("sinim", [128, NT * HD], F32, kind="ExternalInput").ap()
    tri_h = dt("tri", [128, 128], F32, kind="ExternalInput").ap()
    identb_h = dt("identb", [128, 128], BF16, kind="ExternalInput").ap()
    onesrow_h = dt("onesrow", [1, 128], F32, kind="ExternalInput").ap()
    onescol_h = dt("onescol", [128, 1], F32, kind="ExternalInput").ap()
    zrow_h = dt("zrow", [1, 512], F32, kind="ExternalInput").ap()
    w1i_h = dt("w1i", [32, 128, 1024], F32, kind="ExternalInput").ap()
    w2i_h = dt("w2i", [32, 128, 1024], F32, kind="ExternalInput").ap()
    h1i_h = dt("h1i", [32, 128, 4096], F32, kind="ExternalInput").ap()
    h2i_h = dt("h2i", [32, 128, 4096], F32, kind="ExternalInput").ap()
    w3i_h = dt("w3i", [32, 128, 1024], F32, kind="ExternalInput").ap()
    s1c_h = dt("s1c", [128, 32], F32, kind="ExternalInput").ap()
    s2c_h = dt("s2c", [128, 32], F32, kind="ExternalInput").ap()
    h1c_h = dt("h1c", [128, 32], F32, kind="ExternalInput").ap()
    h2c_h = dt("h2c", [128, 32], F32, kind="ExternalInput").ap()
    s3bc_h = dt("s3bc", [128, 1024], F32, kind="ExternalInput").ap()
    cvals_h = dt("cvals", [128, 6], F32, kind="ExternalInput").ap()

    out_h = dt("out", [HALF, D], F32, kind="ExternalOutput").ap()

    og_dram = dt("og_dram", [N, D], BF16, kind="Internal").ap()
    hp_dram = dt("hp_dram", [128, 32 * 1024], BF16, kind="Internal").ap()
    hp2_dram = dt("hp2_dram", [128, 32 * 1024], BF16, kind="Internal").ap()
    xm_dram = dt("xm_dram", [HALF, D], F32, kind="Internal").ap()

    bld = Builder(nc)

    with ExitStack() as gctx:
        sb = lambda name, shape, dtype=F32: bld.tag(gctx.enter_context(nc.sbuf_tensor(name, shape, dtype)))
        bld.init_sems(gctx)

        S_ps = gctx.enter_context(nc.psum_tensor("S_ps", [128, 1024], F32))
        o_ps = gctx.enter_context(nc.psum_tensor("o_ps", [128, 1024], F32))
        tpA = gctx.enter_context(nc.psum_tensor("tpA", [128, 512], F32))
        tpB = gctx.enter_context(nc.psum_tensor("tpB", [128, 512], F32))
        pj0 = gctx.enter_context(nc.psum_tensor("pj0", [128, 512], F32))
        pj1 = gctx.enter_context(nc.psum_tensor("pj1", [128, 512], F32))
        PJ = [("pj0", pj0), ("pj1", pj1)]
        TP = [("tpA", tpA), ("tpB", tpB)]

        tri = sb("tri_s", [128, 128])
        identb = sb("identb_s", [128, 128], BF16)
        onesrow = sb("onesrow_s", [1, 128])
        onescol = sb("onescol_s", [128, 1])
        zrow = sb("zrow_s", [1, 512])
        nwc = sb("nwc_s", [128, 8])
        s1c = sb("s1c_s", [128, 32]); s2c = sb("s2c_s", [128, 32])
        h1c = sb("h1c_s", [128, 32]); h2c = sb("h2c_s", [128, 32])
        cvals = sb("cvals_s", [128, 6])
        for ci, cv in enumerate(CONST_VALS):
            nc.const_aps.register(F32, cv, cvals[:, ci:ci + 1])

        s3bc = sb("s3bc_s", [128, 1024])
        ofs_sb = sb("ofs_sb", [1, 1], I32)
        for dst, src in [(tri, tri_h), (identb, identb_h), (onesrow, onesrow_h),
                         (onescol, onescol_h), (zrow, zrow_h), (nwc, nwc_h),
                         (s1c, s1c_h), (s2c, s2c_h), (h1c, h1c_h), (h2c, h2c_h),
                         (cvals, cvals_h), (s3bc, s3bc_h), (ofs_sb, half_ofs)]:
            bld.op("sync", "dma_start", out=dst[:], in_=src[:], inc="C", inc_n=16)

        with ExitStack() as actx:
            sba = lambda name, shape, dtype=F32: bld.tag(actx.enter_context(nc.sbuf_tensor(name, shape, dtype)))
            cosim = sba("cosim_s", [128, NT * HD])
            sinim = sba("sinim_s", [128, NT * HD])
            Wq_bf = sba("Wq_bf", [128, 8 * 1024], BF16)
            Wk_bf = sba("Wk_bf", [128, 8 * 1024], BF16)
            Wv_bf = sba("Wv_bf", [128, 8 * 1024], BF16)
            Wg1_bf = sba("Wg1_bf", [128, 8 * 16], BF16)
            Wg2_bf = sba("Wg2_bf", [16, 1024], BF16)
            t1f = sba("t1f", [16, N], BF16)
            carry = sba("carry", [1, 1024])
            S_bf = sba("S_bf", [128, 1024], BF16)

            bld.op("sync", "dma_start", out=cosim[:], in_=cosim_h[:], inc="C", inc_n=16)
            bld.op("sync", "dma_start", out=sinim[:], in_=sinim_h[:], inc="C", inc_n=16)
            ev_tabs = ("C", bld.ctr["C"])
            ev_consts = ev_tabs

            with ExitStack() as wctx:
                wld = [bld.tag(wctx.enter_context(nc.sbuf_tensor(f"wld{q}", [128, 1024], F32)))
                       for q in range(4)]
                for wi, (Wh, Wb) in enumerate([(Wq_h, Wq_bf), (Wk_h, Wk_bf), (Wv_h, Wv_bf)]):
                    for c in range(8):
                        q = (wi * 8 + c) % 4
                        evl = bld.op("sync", "dma_start", out=wld[q][:],
                                     in_=Wh[c * 128:(c + 1) * 128, :],
                                     waits=[bld.wait_free(f"wld{q}")], inc=f"wld{q}", inc_n=16)
                        evc = bld.op("vector", "tensor_copy",
                                     out=Wb[:, c * 1024:(c + 1) * 1024], in_=wld[q][:],
                                     waits=[evl], inc=True)
                        bld.set_free(f"wld{q}", evc)
                evl = bld.op("sync", "dma_start",
                             out=wld[0][:, 0:128].rearrange("p (c g) -> p c g", g=16),
                             in_=Wg1_h[:].rearrange("(c p) g -> p c g", p=128),
                             waits=[bld.wait_free("wld0")], inc="wld0", inc_n=16)
                evc = bld.op("vector", "tensor_copy", out=Wg1_bf[:], in_=wld[0][:, 0:128],
                             waits=[evl], inc=True)
                bld.set_free("wld0", evc)
                evl = bld.op("sync", "dma_start", out=wld[1][0:16, :], in_=Wg2_h[:],
                             waits=[bld.wait_free("wld1")], inc="wld1", inc_n=16)
                evc = bld.op("vector", "tensor_copy", out=Wg2_bf[:], in_=wld[1][0:16, :],
                             waits=[evl], inc=True)
                bld.set_free("wld1", evc)
            bld.new_era()
            ev_attw = evc

            x_t = [sba(f"x_t{p}", [128, 1024]) for p in range(2)]
            x1t = [sba(f"x1t{p}", [128, 1024], BF16) for p in range(2)]
            x1f = [sba(f"x1f{p}", [128, 1024], BF16) for p in range(2)]
            sqscr = sba("sqscr", [128, 1024], BF16)
            st = {nm: [sba(f"st_{nm}_{p}", [128, 1]) for p in range(2)]
                  for nm in ("s1", "sq", "m", "t0", "v", "r", "nmr", "oss", "orr")}
            qraw = [sba(f"qraw{p}", [128, 1024]) for p in range(2)]
            escr = [sba(f"escr{p}", [128, 1024]) for p in range(2)]
            E_tm = [sba(f"E_tm{p}", [128, 1024]) for p in range(2)]
            sig = [sba(f"sig{p}", [128, 1024]) for p in range(2)]
            qh_tm = [sba(f"qh_tm{p}", [128, 2048], BF16) for p in range(2)]
            eh_tm = [sba(f"eh_tm{p}", [128, 2048], BF16) for p in range(2)]
            qh_fm = [sba(f"qh_fm{p}", [128, 2048], BF16) for p in range(2)]
            eh_fm = [sba(f"eh_fm{p}", [128, 2048], BF16) for p in range(2)]
            v_tm = [sba(f"v_tm{p}", [128, 1024], BF16) for p in range(2)]
            A_sb = [sba(f"A_sb{p}", [128, 128], BF16) for p in range(2)]
            og = sba("og", [128, 1024])
            ogn = [sba(f"ogn{p}", [128, 1024], BF16) for p in range(2)]

            ev_z1 = bld.op("tensor", "matmul", out=S_ps[:, 0:512], lhsT=onesrow[:],
                           rhs=zrow[:], start=True, stop=True, waits=[ev_consts], inc=True)
            ev_z2 = bld.op("tensor", "matmul", out=S_ps[:, 512:1024], lhsT=onesrow[:],
                           rhs=zrow[:], start=True, stop=True, inc=True)
            bld.set_prod("S_ps_init", ev_z2)
            ev_cz = bld.op("vector", "memset", carry[:], 0.0, waits=[ev_consts], inc=True)
            bld.set_prod("carry_h0", ev_cz)
            bld.set_prod("carry_h1", ev_cz)

            def tile_pre1(i):
                p = i % 2
                evx = bld.op("sync", "dma_start", out=x_t[p][:],
                             in_=xb[i * 128:(i + 1) * 128, :],
                             waits=[bld.wait_free(f"x_t{p}"), ev_attw if i < 2 else None],
                             inc=f"x{p}", inc_n=16)
                ev_s1 = bld.op("vector", "reduce_sum", out=st["s1"][p][:], in_=x_t[p][:],
                               axis=AXL.X, waits=[evx, bld.wait_free(f"stats{p}")], inc=True)
                ev_sq = bld.op("scalar", "activation", out=sqscr[:], in_=x_t[p][:],
                               func=AF.Square, accum_out=st["sq"][p][:],
                               waits=[evx, bld.wait_free("sqscr")], inc=True)
                bld.set_free("sqscr", ev_sq)
                ev_m = bld.op("vector", "tensor_scalar_mul", out=st["m"][p][:], in0=st["s1"][p][:],
                              scalar1=1.0 / D, waits=[ev_s1], inc=True)
                ev_t0 = bld.op("vector", "tensor_mul", out=st["t0"][p][:], in0=st["m"][p][:],
                               in1=st["m"][p][:], waits=[ev_m], inc=True)
                ev_v = bld.op("vector", "scalar_tensor_tensor", out=st["v"][p][:], in0=st["sq"][p][:],
                              scalar=1.0 / D, in1=st["t0"][p][:], op0=ALU.mult, op1=ALU.subtract,
                              waits=[ev_sq, ev_t0], inc=True)
                ev_ln = bld.op("scalar", "activation", out=st["v"][p][:], in_=st["v"][p][:],
                               func=AF.Ln, bias=EPS_LN,
                               waits=[ev_v, ev_consts if i < 2 else None], inc=True)
                ev_r = bld.op("scalar", "activation", out=st["r"][p][:], in_=st["v"][p][:],
                              func=AF.Exp, scale=-0.5, waits=[ev_ln], inc=True)
                ev_nmr = bld.op("vector", "scalar_tensor_tensor", out=st["nmr"][p][:],
                                in0=st["m"][p][:], scalar=-1.0, in1=st["r"][p][:],
                                op0=ALU.mult, op1=ALU.mult, waits=[ev_m, ev_r], inc=True)
                bld.set_free(f"stats{p}", ev_nmr)
                ev_x1t = bld.op("vector", "tensor_scalar", out=x1t[p][:], in0=x_t[p][:],
                                scalar1=st["m"][p][:], scalar2=st["r"][p][:],
                                op0=ALU.subtract, op1=ALU.mult,
                                waits=[ev_r, bld.wait_free(f"x1t{p}")], inc=True)
                bld.set_free(f"x_t{p}", ev_x1t)
                bld.set_prod(f"x1t_{i}", ev_x1t)

            def tile_x1tp(i):
                p = i % 2
                for half in range(2):
                    tpn, tp = TP[half]
                    tpv = tp[:].bitcast(BF16)
                    ev_tp = None
                    for c in range(4):
                        cc = half * 4 + c
                        ev_tp = bld.op("tensor", "transpose",
                                       out=tpv[:, c * 128:(c + 1) * 128],
                                       in_=x1t[p][:, cc * 128:(cc + 1) * 128],
                                       identity=identb[:],
                                       waits=[bld.get_prod(f"x1t_{i}"), bld.wait_free(tpn)] if c == 0 else [],
                                       inc=(c == 3))
                    ev_ev = bld.op("vector", "tensor_copy",
                                   out=x1f[p][:, half * 512:(half + 1) * 512], in_=tpv[:, 0:512],
                                   waits=[ev_tp, bld.wait_free(f"x1f{p}_h{half}")], inc=True)
                    bld.set_free(tpn, ev_ev)
                    bld.set_prod(f"x1f{p}_h{half}", ev_ev)
                bld.set_free(f"x1t{p}", bld.get_prod(f"x1f{p}_h1"))

            def tile_pre2(i):
                p = i % 2

                def proj(Wb):
                    evs = []
                    for half in range(2):
                        pjn, pj = PJ[half]
                        ev_mm = None
                        for c in range(8):
                            ev_mm = bld.op("tensor", "matmul", out=pj[:],
                                           lhsT=x1f[p][:, c * 128:(c + 1) * 128],
                                           rhs=Wb[:, c * 1024 + half * 512: c * 1024 + half * 512 + 512],
                                           start=(c == 0), stop=(c == 7),
                                           waits=[bld.get_prod(f"x1f{p}_h{c // 4}"),
                                                  bld.wait_free(pjn)] if c in (0, 4) else [],
                                           inc=(c == 7))
                        evs.append((pjn, pj, ev_mm))
                    return evs

                for half, (pjn, pj, ev_mm) in enumerate(proj(Wq_bf)):
                    sl = slice(half * 512, half * 512 + 512)
                    ev_qr = bld.op("vector", "tensor_copy", out=qraw[p][:, sl], in_=pj[:],
                                   waits=[ev_mm, bld.wait_free(f"qraw{p}")], inc=True)
                    ev_e = bld.op("scalar", "activation", out=escr[p][:, sl], in_=pj[:],
                                  func=AF.Exp, scale=-1.0,
                                  waits=[ev_mm, ev_qr, bld.wait_free(f"escr{p}")], inc=True)
                    bld.set_free(pjn, ev_e)
                    ev_l = bld.op("scalar", "activation", out=escr[p][:, sl], in_=escr[p][:, sl],
                                  func=AF.Ln, bias=1.0, waits=[ev_e], inc=True)
                    ev_g = bld.op("scalar", "activation", out=escr[p][:, sl], in_=escr[p][:, sl],
                                  func=AF.Exp, scale=-1.0, waits=[ev_l], inc=True)
                    bld.set_prod(f"gq{p}_h{half}", ev_g)
                    bld.set_prod(f"qraw{p}_h{half}", ev_qr)
                for half, (pjn, pj, ev_mm) in enumerate(proj(Wk_bf)):
                    sl = slice(half * 512, half * 512 + 512)
                    ev_e = bld.op("scalar", "activation", out=E_tm[p][:, sl], in_=pj[:],
                                  func=AF.Exp, waits=[ev_mm, bld.wait_free(f"E_tm{p}")], inc=True)
                    bld.set_free(pjn, ev_e)
                    bld.set_prod(f"E_tm{p}_h{half}", ev_e)
                for half, (pjn, pj, ev_mm) in enumerate(proj(Wv_bf)):
                    sl = slice(half * 512, half * 512 + 512)
                    ev_v2 = bld.op("vector", "tensor_copy", out=v_tm[p][:, sl], in_=pj[:],
                                   waits=[ev_mm, bld.wait_free(f"v_tm{p}")], inc=True)
                    bld.set_free(pjn, ev_v2)
                    bld.set_prod(f"v_tm{p}_h{half}", ev_v2)
                pjn, pj = PJ[0]
                ev_mm = None
                for c in range(8):
                    ev_mm = bld.op("tensor", "matmul", out=pj[0:16, 0:128],
                                   lhsT=Wg1_bf[:, c * 16:(c + 1) * 16],
                                   rhs=x1f[p][:, c * 128:(c + 1) * 128],
                                   start=(c == 0), stop=(c == 7),
                                   waits=[bld.wait_free(pjn)] if c == 0 else [],
                                   inc=(c == 7))
                ev_t1 = bld.op("vector", "tensor_copy", out=t1f[:, i * 128:(i + 1) * 128],
                               in_=pj[0:16, 0:128], waits=[ev_mm], inc=True)
                bld.set_free(pjn, ev_t1)
                bld.set_prod(f"t1_{i}", ev_t1)
                bld.set_free(f"x1f{p}_h0", ev_mm)
                bld.set_free(f"x1f{p}_h1", ev_mm)
                ev_lnps = []
                for half in range(2):
                    pjn, pj = PJ[half]
                    sl = slice(half * 512, half * 512 + 512)
                    bld.op("tensor", "matmul", out=pj[:], lhsT=tri[:],
                           rhs=E_tm[p][:, sl], start=True, stop=False,
                           waits=[bld.get_prod(f"E_tm{p}_h{half}"), bld.get_prod(f"carry_h{half}"),
                                  bld.wait_free(pjn)], inc=False)
                    ev_p2 = bld.op("tensor", "matmul", out=pj[:], lhsT=onesrow[:],
                                   rhs=carry[:, sl], start=False, stop=True, inc=True)
                    ev_lnp = bld.op("scalar", "activation", out=sig[p][:, sl], in_=pj[:],
                                    func=AF.Ln, bias=1.0,
                                    waits=[ev_p2, bld.wait_free(f"sig{p}")], inc=True)
                    ev_sig = bld.op("scalar", "activation", out=sig[p][:, sl], in_=sig[p][:, sl],
                                    func=AF.Exp, scale=-1.0, waits=[ev_lnp], inc=True)
                    bld.set_prod(f"sig{p}_h{half}", ev_sig)
                    ev_lnps.append(ev_lnp)
                for half in range(2):
                    pjn, pj = PJ[half]
                    sl = slice(half * 512, half * 512 + 512)
                    ev_tt = bld.op("tensor", "matmul", out=pj[0:1, 0:512], lhsT=onescol[:],
                                   rhs=E_tm[p][:, sl], start=True, stop=True,
                                   waits=[ev_lnps[half]], inc=True)
                    ev_cu = bld.op("vector", "tensor_add", out=carry[:, sl], in0=carry[:, sl],
                                   in1=pj[0:1, 0:512], waits=[ev_tt], inc=True)
                    bld.set_free(pjn, ev_cu)
                    bld.set_prod(f"carry_h{half}", ev_cu)
            def tile_lrpe(i):
                p = i % 2
                ev_gs = bld.op("vector", "tensor_mul", out=escr[p][:], in0=escr[p][:], in1=sig[p][:],
                               waits=[bld.get_prod(f"gq{p}_h0"), bld.get_prod(f"gq{p}_h1"),
                                      bld.get_prod(f"sig{p}_h0"), bld.get_prod(f"sig{p}_h1")],
                               inc=True)
                bld.set_free(f"sig{p}", ev_gs)
                ev_tq = bld.op("vector", "tensor_mul", out=qraw[p][:], in0=qraw[p][:], in1=escr[p][:],
                               waits=[ev_gs], inc=True)
                bld.set_free(f"escr{p}", ev_tq)
                ct = cosim[:, i * HD:(i + 1) * HD]
                stt_ = sinim[:, i * HD:(i + 1) * HD]
                ct_b = ct.unsqueeze(1).broadcast_to((128, 16, 64))
                st_b = stt_.unsqueeze(1).broadcast_to((128, 16, 64))
                tq3 = qraw[p][:].rearrange("p (h d) -> p h d", d=64)
                E3 = E_tm[p][:].rearrange("p (h d) -> p h d", d=64)
                qh4 = qh_tm[p][:].rearrange("p (h t d) -> p h t d", h=16, t=2)
                eh4 = eh_tm[p][:].rearrange("p (h t d) -> p h t d", h=16, t=2)
                bld.op("vector", "tensor_mul", out=qh4[:, :, 0, :], in0=tq3, in1=ct_b,
                       waits=[ev_tq, ev_tabs, bld.wait_free(f"qh_tm{p}")], inc=True)
                ev_q2 = bld.op("vector", "tensor_mul", out=qh4[:, :, 1, :], in0=tq3, in1=st_b,
                               inc=True)
                bld.set_free(f"qraw{p}", ev_q2)
                ev_e1 = bld.op("gpsimd", "tensor_mul", out=eh4[:, :, 0, :], in0=E3, in1=ct_b,
                               waits=[bld.get_prod(f"E_tm{p}_h0"), bld.get_prod(f"E_tm{p}_h1"),
                                      ev_tabs, bld.wait_free(f"eh_tm{p}")], inc="O")
                ev_e2 = bld.op("gpsimd", "tensor_mul", out=eh4[:, :, 1, :], in0=E3, in1=st_b,
                               inc="O")
                bld.set_free(f"E_tm{p}", ev_e2)
                bld.set_prod(f"qh_tm{p}", ev_q2)
                bld.set_prod(f"eh_tm{p}", ev_e2)

            def tile_qe_tp(i):
                p = i % 2
                ev_q2 = bld.get_prod(f"qh_tm{p}")
                ev_e2 = bld.get_prod(f"eh_tm{p}")
                for (src, sname, dst, dname, pev) in (
                        (qh_tm, f"qh_tm{p}", qh_fm, f"qh_fm{p}", ev_q2),
                        (eh_tm, f"eh_tm{p}", eh_fm, f"eh_fm{p}", ev_e2)):
                    ev_tp_last = None
                    for grp in range(4):
                        tpn, tp = TP[grp % 2]
                        tpv = tp[:].bitcast(BF16)
                        ev_tp = None
                        for c in range(4):
                            h = grp * 4 + c
                            ev_tp = bld.op("tensor", "transpose",
                                           out=tpv[:, c * 128:(c + 1) * 128],
                                           in_=src[p][:, h * 128:(h + 1) * 128],
                                           identity=identb[:],
                                           waits=[pev, bld.wait_free(tpn)] if c == 0 else [],
                                           inc=(c == 3))
                        ev_tp_last = ev_tp
                        eng_ev = "vector" if grp % 2 == 0 else "scalar"
                        ev_ev = bld.op(eng_ev, "tensor_copy" if grp % 2 == 0 else "copy",
                                       **({"out": dst[p][:, grp * 512:(grp + 1) * 512],
                                           "in_": tpv[:, 0:512]}),
                                       waits=[ev_tp, bld.wait_free(dname)] if grp == 0 else [ev_tp],
                                       inc=True)
                        bld.set_free(tpn, ev_ev)
                        bld.set_prod(f"{dname}_g{grp}", ev_ev)
                    bld.set_prod(dname, bld.get_prod(f"{dname}_g3"))
                    if sname.startswith("qh_tm"):
                        bld.set_free(sname, ev_tp_last)

            def tile_gla(i):
                p = i % 2
                if i > 0:
                    ev_sbf = bld.op("vector", "tensor_copy", out=S_bf[:], in_=S_ps[:],
                                    waits=[bld.get_prod("S_upd"), bld.wait_free("S_bf")],
                                    inc=True)
                    bld.set_prod("S_bf", ev_sbf)
                    bld.set_free("S_ps_rd", ev_sbf)
                for half in range(2):
                    pjn, pj = PJ[half]
                    sl = slice(half * 512, half * 512 + 512)
                    ev_mm = bld.op("tensor", "matmul", out=pj[:],
                                   lhsT=t1f[:, i * 128:(i + 1) * 128],
                                   rhs=Wg2_bf[:, half * 512:half * 512 + 512],
                                   start=True, stop=True,
                                   waits=[bld.get_prod(f"t1_{i}"), bld.wait_free(pjn)],
                                   inc=True)
                    ev_e = bld.op("scalar", "activation", out=escr[p][:, sl], in_=pj[:],
                                  func=AF.Exp, scale=-1.0,
                                  waits=[ev_mm, bld.wait_free(f"escr{p}")], inc=True)
                    bld.set_free(pjn, ev_e)
                    ev_l = bld.op("scalar", "activation", out=escr[p][:, sl], in_=escr[p][:, sl],
                                  func=AF.Ln, bias=1.0, waits=[ev_e], inc=True)
                    ev_g = bld.op("scalar", "activation", out=escr[p][:, sl], in_=escr[p][:, sl],
                                  func=AF.Exp, scale=-1.0, waits=[ev_l], inc=True)
                    bld.set_prod(f"gate{p}_h{half}", ev_g)
                def emit_A(h):
                    qsl = slice(h * 128, (h + 1) * 128)
                    pa = h % 2
                    tpn, tp = TP[pa]
                    ev_a = bld.op("tensor", "matmul", out=tp[:, 0:128],
                                  lhsT=eh_fm[p][:, qsl], rhs=qh_fm[p][:, qsl],
                                  start=True, stop=True,
                                  waits=[bld.get_prod(f"eh_fm{p}_g{h // 4}"),
                                         bld.get_prod(f"qh_fm{p}_g{h // 4}"),
                                         bld.wait_free(tpn)],
                                  inc=True)
                    ev_am = bld.op("vector", "tensor_mul", out=A_sb[pa][:], in0=tp[:, 0:128],
                                   in1=tri[:], waits=[ev_a, bld.wait_free(f"A_sb{pa}")],
                                   inc=True)
                    bld.set_free(tpn, ev_am)
                    bld.set_prod(f"am_{h}", ev_am)

                def emit_intra_S(h):
                    hb = h // 8
                    hsl = slice(h * 64, (h + 1) * 64)
                    qsl = slice(h * 128, (h + 1) * 128)
                    pa = h % 2
                    if i > 0:
                        bld.op("tensor", "matmul", out=o_ps[:, hsl],
                               lhsT=qh_fm[p][:, qsl], rhs=S_bf[:, hsl],
                               start=True, stop=True,
                               waits=[bld.get_prod("S_bf"),
                                      bld.get_prod(f"qh_fm{p}_g{h // 4}"),
                                      bld.wait_free(f"o_ps{hb}")],
                               inc=False)
                    bld.op("tensor", "matmul", out=o_ps[:, hsl],
                           lhsT=A_sb[pa][:], rhs=v_tm[p][:, hsl],
                           start=(i == 0), stop=True,
                           waits=[bld.get_prod(f"am_{h}"), bld.get_prod(f"v_tm{p}_h{h // 8}"),
                                  bld.wait_free(f"o_ps{hb}") if i == 0 else None],
                           inc=False)
                    ev_mark = bld.op("tensor", "matmul", out=S_ps[:, hsl],
                                     lhsT=eh_tm[p][:, qsl], rhs=v_tm[p][:, hsl],
                                     start=False, stop=True,
                                     waits=[bld.get_prod(f"eh_tm{p}"),
                                            bld.get_prod("S_ps_init") if i == 0 and h == 0 else None,
                                            bld.wait_free("S_ps_rd") if h == 0 else None],
                                     inc=True)
                    bld.set_free(f"A_sb{pa}", ev_mark)
                    if h == 15:
                        bld.set_prod("S_upd", ev_mark)
                        bld.set_prod(f"o_done_{i}", ev_mark)
                        bld.set_free("S_bf", ev_mark)
                        bld.set_free(f"qh_fm{p}", ev_mark)
                        bld.set_free(f"eh_fm{p}", ev_mark)
                        bld.set_free(f"v_tm{p}", ev_mark)
                        bld.set_free(f"eh_tm{p}", ev_mark)

                emit_A(0)
                for h in range(16):
                    if h + 1 < 16:
                        emit_A(h + 1)
                    emit_intra_S(h)
            def tile_og(i):
                p = i % 2
                ev_od = bld.get_prod(f"o_done_{i}")
                for half in range(2):
                    sl = slice(half * 512, half * 512 + 512)
                    ev_og2 = bld.op("vector", "tensor_mul", out=og[:, sl],
                                    in0=o_ps[:, sl], in1=escr[p][:, sl],
                                    waits=[ev_od, bld.get_prod(f"gate{p}_h{half}"),
                                           bld.wait_free("og")] if half == 0 else [
                                           bld.get_prod(f"gate{p}_h1")], inc=True)
                bld.set_free(f"escr{p}", ev_og2)
                bld.set_free("o_ps0", ev_og2)
                bld.set_free("o_ps1", ev_og2)
                ev_oss = bld.op("scalar", "activation", out=sqscr[:], in_=og[:],
                                func=AF.Square, accum_out=st["oss"][p][:],
                                waits=[ev_og2, bld.wait_free("sqscr")], inc=True)
                bld.set_free("sqscr", ev_oss)
                ev_lno = bld.op("scalar", "activation", out=st["oss"][p][:], in_=st["oss"][p][:],
                                func=AF.Ln, scale=1.0 / D, bias=EPS_RMS, waits=[ev_oss], inc=True)
                ev_ro = bld.op("scalar", "activation", out=st["orr"][p][:], in_=st["oss"][p][:],
                               func=AF.Exp, scale=-0.5, waits=[ev_lno], inc=True)
                ev_ogn = bld.op("scalar", "activation", out=ogn[p][:], in_=og[:],
                                func=AF.Copy, scale=st["orr"][p][:],
                                waits=[ev_ro, bld.wait_free(f"ogn{p}")], inc=True)
                bld.set_free("og", ev_ogn)
                ev_st2 = bld.op("gpsimd", "dma_start", out=og_dram[i * 128:(i + 1) * 128, :],
                                in_=ogn[p][:], waits=[ev_ogn], inc=f"og{p}", inc_n=16)
                bld.set_free(f"ogn{p}", ev_st2)
                bld.set_prod(f"og_store_{i}", ev_st2)
                og_counts[i] = (bld.ctr["og0"], bld.ctr["og1"])

            og_counts = {}
            tile_pre1(0)
            tile_pre1(1)
            tile_x1tp(0)
            tile_x1tp(1)
            tile_pre2(0)
            tile_lrpe(0)
            for i in range(NT):
                if i + 2 < NT:
                    tile_pre1(i + 2)
                if i + 1 < NT:
                    tile_pre2(i + 1)
                if i + 2 < NT:
                    tile_x1tp(i + 2)
                tile_qe_tp(i)
                tile_gla(i)
                if i + 1 < NT:
                    tile_lrpe(i + 1)
                tile_og(i)

            ev_last_sbf = bld.get_prod("S_bf")
            ev_last_og = bld.get_prod("og_h1")
            bld.set_free("Sps_nt0", ev_last_sbf)
            bld.set_free("Sps_nt1", ev_last_sbf)
            bld.set_free("ops_nt0", ev_last_og)
            bld.set_free("ops_nt1", ev_last_og)
            bld.og_counts = og_counts
            phaseA_done = [("P", bld.ctr["P"]), ("V", bld.ctr["V"]),
                           ("A", bld.ctr["A"]),
                           ("og0", bld.ctr["og0"]), ("og1", bld.ctr["og1"]),
                           ("x0", bld.ctr["x0"]), ("x1", bld.ctr["x1"])]

        bld.new_era()
        with ExitStack() as bctx:
            sbb = lambda name, shape, dtype=F32: bld.tag(bctx.enter_context(nc.sbuf_tensor(name, shape, dtype)))
            Wo_bf = sbb("Wo_bf", [128, 8 * 1024], BF16)
            ogf = [sbb(f"ogf{p}", [128, 1024], BF16) for p in range(2)]
            xo_t = [sbb(f"xo_t{p}", [128, 1024]) for p in range(2)]
            xm_t = [sbb(f"xm_t{p}", [128, 1024]) for p in range(2)]
            yraw = [sbb(f"yraw{p}", [128, 1024]) for p in range(2)]
            x2t = sbb("x2t", [128, 1024], BF16)
            yln = sbb("yln", [128, 1024])
            sqb = sbb("sqb", [128, 1024], BF16)
            bs = {nm: [sbb(f"bs_{nm}_{p}", [128, 1]) for p in range(2)]
                  for nm in ("s1", "sq", "sqB", "m", "t0", "v", "r", "nmr")}
            hbuf = sbb("hbuf", [128, 32 * 1024], BF16)
            outt = [sbb(f"outt{p}", [128, 1024]) for p in range(2)]
            ev_ofsl = ev_consts

            with ExitStack() as wctx:
                wldb = [bld.tag(wctx.enter_context(nc.sbuf_tensor(f"wldb{q}", [128, 1024], F32)))
                        for q in range(2)]
                for c in range(8):
                    q = c % 2
                    evl = bld.op("sync", "dma_start", out=wldb[q][:],
                                 in_=Wo_h[c * 128:(c + 1) * 128, :],
                                 waits=[bld.wait_free(f"wldb{q}")], inc=f"wo{q}", inc_n=16)
                    evc = bld.op("vector", "tensor_scalar_mul",
                                 out=Wo_bf[:, c * 1024:(c + 1) * 1024], in0=wldb[q][:],
                                 scalar1=nwc[:, c:c + 1],
                                 waits=[evl, ev_consts] + (phaseA_done if c == 0 else []),
                                 inc=True)
                    bld.set_free(f"wldb{q}", evc)
            bld.new_era()
            ev_wo = ("V", bld.ctr["V"])

            regcell = {}

            def setup_reg(engine, post):
                reg = engine.alloc_register("hofs")
                engine.reg_load(reg, ofs_sb[0:1, 0:1])
                regcell["ofs"] = engine.snap(reg)
                inst = engine.nop()
                post(inst)

            ev_reg = bld.raw("gpsimd", setup_reg, waits=[ev_ofsl] + phaseA_done, inc="O")

            for hf in range(2):
                bld.new_era()
                bhctx = ExitStack()
                x2f = bld.tag(bhctx.enter_context(nc.sbuf_tensor(f"x2f{hf}", [128, 8 * 1024], BF16)))
                with ExitStack() as hctx:
                    og_sb = bld.tag(hctx.enter_context(nc.sbuf_tensor(f"og_sb{hf}", [128, 8 * 1024], BF16)))

                    def og_load(engine, post, hf=hf):
                        ofs = regcell["ofs"]
                        src = og_dram[bass.ds(ofs, HALF), :][hf * 1024: hf * 1024 + 1024, :]
                        src = src.rearrange("(t p) m -> p t m", p=128)
                        inst = engine.dma_start(
                            out=og_sb[:].rearrange("p (t m) -> p t m", t=8), in_=src)
                        post(inst)

                    ev_ogl = bld.raw("gpsimd", og_load,
                                     waits=[ev_reg,
                                            ("og0", bld.ctr["og0"]), ("og1", bld.ctr["og1"]),
                                            bld.wait_free("og_sb")],
                                     inc="ogld", inc_n=16, out_name=og_sb.name)
                    bld.set_prod("og_sb", ev_ogl)

                    def B_pe(t):
                        p = t % 2
                        gtok = hf * 1024 + t * 128
                        evx = bld.op("sync", "dma_start", out=xo_t[p][:],
                                     in_=x_own[gtok:gtok + 128, :],
                                     waits=[bld.wait_free(f"xo_t{p}")], inc=f"xo{p}", inc_n=16)
                        bld.set_prod(f"xo_{t}", evx)
                        for half in range(2):
                            tpn, tp = TP[half]
                            tpv = tp[:].bitcast(BF16)
                            ev_tp = None
                            for c in range(4):
                                cc = half * 4 + c
                                ev_tp = bld.op("tensor", "transpose",
                                               out=tpv[:, c * 128:(c + 1) * 128],
                                               in_=og_sb[:, t * 1024 + cc * 128: t * 1024 + cc * 128 + 128],
                                               identity=identb[:],
                                               waits=[ev_ogl, ev_wo, bld.wait_free(tpn)] if c == 0 else [],
                                               inc=(c == 3))
                            ev_ev = bld.op("vector", "tensor_copy",
                                           out=ogf[p][:, half * 512:(half + 1) * 512],
                                           in_=tpv[:, 0:512],
                                           waits=[ev_tp, bld.wait_free(f"ogf{p}")], inc=True)
                            bld.set_free(tpn, ev_ev)
                            bld.set_prod(f"ogf{p}_h{half}", ev_ev)
                        for half in range(2):
                            pjn, pj = PJ[half]
                            ev_mm = None
                            for c in range(8):
                                ev_mm = bld.op("tensor", "matmul", out=pj[:],
                                               lhsT=ogf[p][:, c * 128:(c + 1) * 128],
                                               rhs=Wo_bf[:, c * 1024 + half * 512: c * 1024 + half * 512 + 512],
                                               start=(c == 0), stop=(c == 7),
                                               waits=[bld.get_prod(f"ogf{p}_h{c // 4}"),
                                                      bld.wait_free(pjn)] if c in (0, 4) else [],
                                               inc=(c == 7))
                            ev_yev = bld.op("vector", "tensor_copy",
                                            out=yraw[p][:, half * 512:(half + 1) * 512], in_=pj[:],
                                            waits=[ev_mm, bld.wait_free(f"yraw{p}")], inc=True)
                            bld.set_free(pjn, ev_yev)
                            bld.set_prod(f"yraw_{t}_h{half}", ev_yev)
                        bld.set_free(f"ogf{p}", bld.get_prod("y_ps_h1") if False else ev_mm)

                    def B_ln(t):
                        p = t % 2
                        gtok = hf * 1024 + t * 128
                        evy = [bld.get_prod(f"yraw_{t}_h0"), bld.get_prod(f"yraw_{t}_h1")]
                        ev_sa = bld.op("vector", "reduce_sum", out=bs["s1"][p][:], in_=yraw[p][:],
                                       axis=AXL.X, waits=evy + [bld.wait_free(f"bstats{p}")], inc=True)
                        ev_qa = bld.op("scalar", "activation", out=sqb[:], in_=yraw[p][:],
                                       func=AF.Square, accum_out=bs["sq"][p][:],
                                       waits=evy + [bld.wait_free("sqb")], inc=True)
                        bld.set_free("sqb", ev_qa)
                        ev_m = bld.op("vector", "tensor_scalar_mul", out=bs["m"][p][:],
                                      in0=bs["s1"][p][:], scalar1=1.0 / D, waits=[ev_sa], inc=True)
                        ev_t0 = bld.op("vector", "tensor_mul", out=bs["t0"][p][:], in0=bs["m"][p][:],
                                       in1=bs["m"][p][:], waits=[ev_m], inc=True)
                        ev_v = bld.op("vector", "scalar_tensor_tensor", out=bs["v"][p][:],
                                      in0=bs["sq"][p][:], scalar=1.0 / D, in1=bs["t0"][p][:],
                                      op0=ALU.mult, op1=ALU.subtract, waits=[ev_qa, ev_t0], inc=True)
                        ev_ln = bld.op("scalar", "activation", out=bs["v"][p][:], in_=bs["v"][p][:],
                                       func=AF.Ln, bias=EPS_LN, waits=[ev_v], inc=True)
                        ev_r = bld.op("scalar", "activation", out=bs["r"][p][:], in_=bs["v"][p][:],
                                      func=AF.Exp, scale=-0.5, waits=[ev_ln], inc=True)
                        ev_yl = bld.op("vector", "tensor_scalar", out=yln[:], in0=yraw[p][:],
                                       scalar1=bs["m"][p][:], scalar2=bs["r"][p][:],
                                       op0=ALU.subtract, op1=ALU.mult,
                                       waits=[ev_r, bld.wait_free("yln")], inc=True)
                        ev_xm = bld.op("vector", "tensor_add", out=xm_t[p][:],
                                       in0=yln[:], in1=xo_t[p][:],
                                       waits=[ev_yl, bld.get_prod(f"xo_{t}"),
                                              bld.wait_free(f"xm_t{p}"),
                                              bld.wait_free(f"xm_tst{p}")], inc=True)
                        bld.set_free(f"yraw{p}", ev_yl)
                        bld.set_free(f"xo_t{p}", ev_xm)
                        bld.set_free("yln", ev_xm)
                        ev_xst = bld.op("gpsimd", "dma_start", out=xm_dram[gtok:gtok + 128, :],
                                        in_=xm_t[p][:], waits=[ev_xm], inc=f"xs{p}", inc_n=16)
                        bld.set_prod(f"xm_store_{t}", ev_xst)
                        bld.set_free(f"xm_tst{p}", ev_xst)
                        ev_s3 = bld.op("vector", "reduce_sum", out=bs["s1"][p][:], in_=xm_t[p][:],
                                       axis=AXL.X, waits=[ev_xm], inc=True)
                        ev_q3 = bld.op("scalar", "activation", out=sqb[:], in_=xm_t[p][:],
                                       func=AF.Square, accum_out=bs["sq"][p][:],
                                       waits=[ev_xm, bld.wait_free("sqb")], inc=True)
                        bld.set_free("sqb", ev_q3)
                        ev_m3 = bld.op("vector", "tensor_scalar_mul", out=bs["m"][p][:],
                                       in0=bs["s1"][p][:], scalar1=1.0 / D, waits=[ev_s3], inc=True)
                        ev_t3 = bld.op("vector", "tensor_mul", out=bs["t0"][p][:], in0=bs["m"][p][:],
                                       in1=bs["m"][p][:], waits=[ev_m3], inc=True)
                        ev_v3 = bld.op("vector", "scalar_tensor_tensor", out=bs["v"][p][:],
                                       in0=bs["sq"][p][:], scalar=1.0 / D, in1=bs["t0"][p][:],
                                       op0=ALU.mult, op1=ALU.subtract, waits=[ev_q3, ev_t3], inc=True)
                        ev_l3 = bld.op("scalar", "activation", out=bs["v"][p][:], in_=bs["v"][p][:],
                                       func=AF.Ln, bias=EPS_LN, waits=[ev_v3], inc=True)
                        ev_r3 = bld.op("scalar", "activation", out=bs["r"][p][:], in_=bs["v"][p][:],
                                       func=AF.Exp, scale=-0.5, waits=[ev_l3], inc=True)
                        ev_x2 = bld.op("vector", "tensor_scalar", out=x2t[:], in0=xm_t[p][:],
                                       scalar1=bs["m"][p][:], scalar2=bs["r"][p][:],
                                       op0=ALU.subtract, op1=ALU.mult,
                                       waits=[ev_r3, bld.wait_free("x2t")], inc=True)
                        bld.set_free(f"bstats{p}", ev_x2)
                        bld.set_free(f"xm_t{p}", ev_x2)
                        bld.set_prod(f"x2t_{t}", ev_x2)

                    def B_x2tp(t):
                        ev_x2 = bld.get_prod(f"x2t_{t}")
                        ev_e2 = None
                        for half in range(2):
                            tpn, tp = TP[half]
                            tpv = tp[:].bitcast(BF16)
                            ev_tp = None
                            for c in range(4):
                                cc = half * 4 + c
                                ev_tp = bld.op("tensor", "transpose",
                                               out=tpv[:, c * 128:(c + 1) * 128],
                                               in_=x2t[:, cc * 128:(cc + 1) * 128],
                                               identity=identb[:],
                                               waits=[ev_x2, bld.wait_free(tpn)] if c == 0 else [],
                                               inc=(c == 3))
                            for c in range(4):
                                cc = half * 4 + c
                                ev_e2 = bld.op("vector", "tensor_copy",
                                               out=x2f[:, cc * 1024 + t * 128: cc * 1024 + t * 128 + 128],
                                               in_=tpv[:, c * 128:(c + 1) * 128],
                                               waits=[ev_tp, bld.wait_free("x2f")] if c == 0 else [ev_tp],
                                               inc=(c == 3))
                            bld.set_free(tpn, ev_e2)
                        bld.set_free("x2t", ev_e2)
                        bld.set_prod(f"x2f_t{t}", ev_e2)

                    B_pe(0)
                    for t in range(8):
                        if t + 1 < 8:
                            B_pe(t + 1)
                        B_ln(t)
                        B_x2tp(t)
                    ev_x2f_all = bld.get_prod("x2f_t7")
                    bld.set_free("og_sb", ev_x2f_all)

                bld.new_era()
                ffn_barrier = bld.totals()
                with ExitStack() as c1ctx:
                    wfA = [bld.tag(c1ctx.enter_context(nc.sbuf_tensor(f"wfA{hf}_{q}", [128, 2048], F32)))
                           for q in range(2)]
                    wbA = [bld.tag(c1ctx.enter_context(nc.sbuf_tensor(f"wbA{hf}_{q}", [128, 1024], BF16)))
                           for q in range(2)]
                    wbB = [bld.tag(c1ctx.enter_context(nc.sbuf_tensor(f"wbB{hf}_{q}", [128, 1024], BF16)))
                           for q in range(2)]
                    sA = [bld.tag(c1ctx.enter_context(nc.sbuf_tensor(f"sA{hf}_{q}", [128, 512], F32)))
                          for q in range(2)]
                    for oc in range(32):
                        q = oc % 2
                        evl1 = bld.op("sync", "dma_start", out=wfA[q][:, 0:1024],
                                      in_=w1i_h[oc],
                                      waits=(ffn_barrier if oc < 2 else []) +
                                            [bld.wait_free(f"wfA{q}")], inc=f"wfA{q}", inc_n=16)
                        evl2 = bld.op("sync", "dma_start", out=wfA[q][:, 1024:2048],
                                      in_=w2i_h[oc], inc=f"wfA{q}", inc_n=16)
                        evc1 = bld.op("vector", "tensor_copy", out=wbA[q][:], in_=wfA[q][:, 0:1024],
                                      waits=[evl2, bld.wait_free(f"wbA{q}")], inc=True)
                        evc2 = bld.op("vector", "tensor_copy", out=wbB[q][:], in_=wfA[q][:, 1024:2048],
                                      waits=[evl2, bld.wait_free(f"wbB{q}")], inc=True)
                        bld.set_free(f"wfA{q}", evc2)
                        for nt in range(2):
                            psa = S_ps[:, nt * 512: nt * 512 + 512]
                            psb = o_ps[:, nt * 512: nt * 512 + 512]
                            ev_x2f_nt = ev_x2f_all
                            ev_ma = None
                            for kc in range(8):
                                ev_ma = bld.op("tensor", "matmul", out=psa,
                                               lhsT=wbA[q][:, kc * 128:(kc + 1) * 128],
                                               rhs=x2f[:, kc * 1024 + nt * 512: kc * 1024 + nt * 512 + 512],
                                               start=(kc == 0), stop=(kc == 7),
                                               waits=[evc1, ev_x2f_nt,
                                                      bld.wait_free(f"Sps_nt{nt}")] if kc == 0 else [],
                                               inc=(kc == 7))
                            ev_mb = None
                            for kc in range(8):
                                ev_mb = bld.op("tensor", "matmul", out=psb,
                                               lhsT=wbB[q][:, kc * 128:(kc + 1) * 128],
                                               rhs=x2f[:, kc * 1024 + nt * 512: kc * 1024 + nt * 512 + 512],
                                               start=(kc == 0), stop=(kc == 7),
                                               waits=[evc2, ev_x2f_nt,
                                                      bld.wait_free(f"ops_nt{nt}")] if kc == 0 else [],
                                               inc=(kc == 7))
                            if nt == 1:
                                bld.set_free(f"wbA{q}", ev_ma)
                                bld.set_free(f"wbB{q}", ev_mb)
                            ev_sl = bld.op("scalar", "activation", out=sA[nt][:], in_=psa,
                                           func=AF.Silu, scale=s1c[:, oc:oc + 1],
                                           waits=[ev_ma, bld.wait_free(f"sA{nt}")], inc=True)
                            bld.set_free(f"Sps_nt{nt}", ev_sl)
                            ev_hv = bld.op("vector", "scalar_tensor_tensor",
                                           out=hbuf[:, oc * 1024 + nt * 512: oc * 1024 + nt * 512 + 512],
                                           in0=psb, scalar=s2c[:, oc:oc + 1], in1=sA[nt][:],
                                           op0=ALU.mult, op1=ALU.mult,
                                           waits=[ev_mb, ev_sl, bld.wait_free("hbuf")], inc=True)
                            bld.set_free(f"ops_nt{nt}", ev_hv)
                            bld.set_free(f"sA{nt}", ev_hv)
                            bld.set_prod(f"h_oc{oc}", ev_hv)
                    ev_h_all = bld.get_prod("h_oc31")
                    bld.set_free("x2f", bld.get_prod("h_oc31"))

                bhctx.close()
                bld.new_era()
                c2_barrier = bld.totals()
                with ExitStack() as c2ctx:
                    hwf = [bld.tag(c2ctx.enter_context(nc.sbuf_tensor(f"hwf{hf}_{q}", [128, 2048], F32)))
                           for q in range(2)]
                    hwb = [bld.tag(c2ctx.enter_context(nc.sbuf_tensor(f"hwb{hf}_{q}", [128, 4096], BF16)))
                           for q in range(2)]
                    hp_t = [bld.tag(c2ctx.enter_context(nc.sbuf_tensor(f"hp_t{hf}_{q}", [128, 512], BF16)))
                            for q in range(2)]

                    def hlayer(img_h, scol, dst_dram, src_prod_ev, first_barrier, bg=None):
                        oc_counts = []
                        for oc in range(32):
                            if bg is not None:
                                bg(oc)
                            q = oc % 2
                            evl1 = bld.op("sync", "dma_start", out=hwf[q][:],
                                          in_=img_h[oc][:, 0:2048],
                                          waits=(first_barrier if oc < 2 else []) +
                                                [bld.wait_free(f"hwf{q}")], inc=f"hw{q}", inc_n=16)
                            evc1 = bld.op("vector", "tensor_copy", out=hwb[q][:, 0:2048],
                                          in_=hwf[q][:],
                                          waits=[evl1, bld.wait_free(f"hwb{q}")], inc=True)
                            evl2 = bld.op("sync", "dma_start", out=hwf[q][:],
                                          in_=img_h[oc][:, 2048:4096],
                                          waits=[evc1], inc=f"hw{q}", inc_n=16)
                            evc2 = bld.op("vector", "tensor_copy", out=hwb[q][:, 2048:4096],
                                          in_=hwf[q][:], waits=[evl2], inc=True)
                            bld.set_free(f"hwf{q}", evc2)
                            for nt in range(2):
                                psx = S_ps[:, nt * 512: nt * 512 + 512]
                                ev_mm = None
                                for kc in range(32):
                                    ev_mm = bld.op("tensor", "matmul", out=psx,
                                                   lhsT=hwb[q][:, kc * 128:(kc + 1) * 128],
                                                   rhs=hbuf[:, kc * 1024 + nt * 512: kc * 1024 + nt * 512 + 512],
                                                   start=(kc == 0), stop=(kc == 31),
                                                   waits=[evc1, evc2, src_prod_ev,
                                                          bld.wait_free(f"Sps_nt{nt}")] if kc == 0 else [],
                                                   inc=(kc == 31))
                                if nt == 1:
                                    bld.set_free(f"hwb{q}", ev_mm)
                                pp = nt
                                ev_g = bld.op("scalar", "activation", out=hp_t[pp][:], in_=psx,
                                              func=AF.Gelu, scale=scol[:, oc:oc + 1],
                                              waits=[ev_mm, bld.wait_free(f"hp_t{pp}")], inc=True)
                                bld.set_free(f"Sps_nt{nt}", ev_g)
                                ev_hs = bld.op("gpsimd", "dma_start",
                                               out=dst_dram[:, (oc * 2 + nt) * 512: (oc * 2 + nt) * 512 + 512],
                                               in_=hp_t[pp][:], waits=[ev_g], inc=f"hp{pp}", inc_n=16)
                                bld.set_free(f"hp_t{pp}", ev_hs)
                            oc_counts.append((bld.ctr["hp0"], bld.ctr["hp1"]))
                        return [("hp0", bld.ctr["hp0"]), ("hp1", bld.ctr["hp1"])], oc_counts

                    ev_hp, h1_counts = hlayer(h1i_h, h1c, hp_dram, ev_h_all, c2_barrier)
                    ev_rl = bld.op("gpsimd", "dma_start", out=hbuf[:], in_=hp_dram[:],
                                   waits=ev_hp + [ev_h_all, bld.wait_free("hbuf")],
                                   inc="hbr", inc_n=16)
                    bld.new_era()
                    c6ctx = ExitStack()
                    w3ld = [bld.tag(c6ctx.enter_context(nc.sbuf_tensor(f"w3ld{hf}_{q}", [128, 512], F32, side="right")))
                            for q in range(2)]
                    w3b = bld.tag(c6ctx.enter_context(nc.sbuf_tensor(f"w3b{hf}", [128, 32 * 512], BF16, side="right")))

                    def w3_pair(kc, oh):
                        q = kc % 2
                        evl = bld.op("sync", "dma_start", out=w3ld[q][:],
                                     in_=w3i_h[kc][:, oh * 512: oh * 512 + 512],
                                     waits=[bld.wait_free(f"w3ld{q}")], inc=f"w3l{q}", inc_n=16)
                        evc = bld.op("vector", "tensor_copy",
                                     out=w3b[:, kc * 512:(kc + 1) * 512], in_=w3ld[q][:],
                                     waits=[evl, bld.wait_free("w3b") if kc == 0 else None],
                                     inc=True)
                        bld.set_free(f"w3ld{q}", evc)
                        bld.set_prod(f"w3b_{kc}", evc)

                    def w3_stream(oh):
                        for kc in range(32):
                            w3_pair(kc, oh)

                    ev_hp2, h2_counts = hlayer(h2i_h, h2c, hp2_dram, ev_rl,
                                               [], bg=lambda oc: w3_pair(oc, 0))
                    ev_rl2 = bld.op("gpsimd", "dma_start", out=hbuf[:], in_=hp2_dram[:],
                                    waits=ev_hp2, inc="hbr", inc_n=16)

                bld.new_era()
                if True:
                    y3 = bld.tag(c6ctx.enter_context(nc.sbuf_tensor(f"y3_{hf}", [128, 8 * 1024], F32, side="right")))
                    for oh in range(2):
                        if oh == 1:
                            w3_stream(1)
                        ev_mm_last = None
                        for t in range(8):
                            pjn, pj = PJ[t % 2]
                            ev_mm = None
                            for kc in range(32):
                                ev_mm = bld.op("tensor", "matmul", out=pj[:],
                                               lhsT=hbuf[:, kc * 1024 + t * 128: kc * 1024 + t * 128 + 128],
                                               rhs=w3b[:, kc * 512:(kc + 1) * 512],
                                               start=(kc == 0), stop=(kc == 31),
                                               waits=[ev_rl2, bld.get_prod("w3b_31"),
                                                      bld.wait_free(pjn)] if kc == 0 else [],
                                               inc=(kc == 31))
                            ev_mm_last = ev_mm
                            ev_y3 = bld.op("vector", "tensor_mul",
                                           out=y3[:, t * 1024 + oh * 512: t * 1024 + oh * 512 + 512],
                                           in0=pj[:], in1=s3bc[:, oh * 512: oh * 512 + 512],
                                           waits=[ev_mm, bld.wait_free("y3") if t == 0 and oh == 0 else None],
                                           inc=True)
                            bld.set_free(pjn, ev_y3)
                            bld.set_prod(f"y3_t{t}_oh{oh}", ev_y3)
                        bld.set_free("w3b", ev_mm_last)
                        if oh == 1:
                            bld.set_free("hbuf", ev_mm_last)
                    for t in range(8):
                        p = t % 2
                        gtok = hf * 1024 + t * 128
                        y3t = y3[:, t * 1024:(t + 1) * 1024]
                        evy = [bld.get_prod(f"y3_t{t}_oh0"), bld.get_prod(f"y3_t{t}_oh1")]
                        ev_xl = bld.op("sync", "dma_start", out=xm_t[p][:],
                                       in_=xm_dram[gtok:gtok + 128, :],
                                       waits=[bld.get_prod(f"xm_store_{t}"),
                                              bld.wait_free(f"xm_t{p}"),
                                              bld.wait_free(f"xm_tst{p}")], inc=f"xmr{p}", inc_n=16)
                        ev_s4 = bld.op("vector", "reduce_sum", out=bs["s1"][p][:], in_=y3t,
                                       axis=AXL.X, waits=evy + [bld.wait_free(f"bstats{p}")], inc=True)
                        ev_q4 = bld.op("scalar", "activation", out=sqb[:], in_=y3t,
                                       func=AF.Square, accum_out=bs["sq"][p][:],
                                       waits=evy + [bld.wait_free("sqb")], inc=True)
                        bld.set_free("sqb", ev_q4)
                        ev_m4 = bld.op("vector", "tensor_scalar_mul", out=bs["m"][p][:],
                                       in0=bs["s1"][p][:], scalar1=1.0 / D, waits=[ev_s4], inc=True)
                        ev_t4 = bld.op("vector", "tensor_mul", out=bs["t0"][p][:], in0=bs["m"][p][:],
                                       in1=bs["m"][p][:], waits=[ev_m4], inc=True)
                        ev_v4 = bld.op("vector", "scalar_tensor_tensor", out=bs["v"][p][:],
                                       in0=bs["sq"][p][:], scalar=1.0 / D, in1=bs["t0"][p][:],
                                       op0=ALU.mult, op1=ALU.subtract, waits=[ev_q4, ev_t4], inc=True)
                        ev_l4 = bld.op("scalar", "activation", out=bs["v"][p][:], in_=bs["v"][p][:],
                                       func=AF.Ln, bias=EPS_LN, waits=[ev_v4], inc=True)
                        ev_r4 = bld.op("scalar", "activation", out=bs["r"][p][:], in_=bs["v"][p][:],
                                       func=AF.Exp, scale=-0.5, waits=[ev_l4], inc=True)
                        ev_n4 = bld.op("vector", "scalar_tensor_tensor", out=bs["nmr"][p][:],
                                       in0=bs["m"][p][:], scalar=-1.0, in1=bs["r"][p][:],
                                       op0=ALU.mult, op1=ALU.mult, waits=[ev_m4, ev_r4], inc=True)
                        bld.set_free(f"bstats{p}", ev_n4)
                        ev_o4 = bld.op("vector", "tensor_scalar", out=outt[p][:], in0=y3t,
                                       scalar1=bs["m"][p][:], scalar2=bs["r"][p][:],
                                       op0=ALU.subtract, op1=ALU.mult,
                                       waits=[ev_r4, bld.wait_free(f"outt{p}")], inc=True)
                        ev_f = bld.op("vector", "tensor_add", out=outt[p][:], in0=outt[p][:],
                                      in1=xm_t[p][:], waits=[ev_o4, ev_xl], inc=True)
                        bld.set_free(f"xm_t{p}", ev_f)
                        if t == 7:
                            bld.set_free("y3", ev_f)
                        ev_os = bld.op("sync", "dma_start", out=out_h[gtok:gtok + 128, :],
                                       in_=outt[p][:], waits=[ev_f], inc=f"os{p}", inc_n=16)
                        bld.set_free(f"outt{p}", ev_os)
                    c6ctx.close()

            bld.op("sync", "wait_ge", bld.sem_handles["os0"], bld.ctr["os0"])
            bld.op("sync", "wait_ge", bld.sem_handles["os1"], bld.ctr["os1"])

            with nc.Block() as block:
                @block.sync
                def _(eng):
                    for f in bld.streams["sync"]:
                        f(eng)

                @block.vector
                def _(eng):
                    for f in bld.streams["vector"]:
                        f(eng)

                @block.scalar
                def _(eng):
                    for f in bld.streams["scalar"]:
                        f(eng)

                @block.tensor
                def _(eng):
                    for f in bld.streams["tensor"]:
                        f(eng)

                @block.gpsimd
                def _(eng):
                    for f in bld.streams["gpsimd"]:
                        f(eng)

    return nc


def _host_inputs(inputs):
    x = np.ascontiguousarray(inputs["L0_attn_x"])
    t = np.arange(N, dtype=np.float64)[:, None]
    j = np.arange(HD, dtype=np.float64)[None, :]
    th = t * (10000.0 ** (-2.0 * j / HD))
    cos_t = np.cos(th).astype(np.float32)
    sin_t = np.sin(th).astype(np.float32)
    cosim = np.ascontiguousarray(cos_t.reshape(NT, 128, HD).transpose(1, 0, 2).reshape(128, NT * HD))
    sinim = np.ascontiguousarray(sin_t.reshape(NT, 128, HD).transpose(1, 0, 2).reshape(128, NT * HD))
    common = dict(
        Wq=np.ascontiguousarray(inputs["L0_attn_Wq"]),
        Wk=np.ascontiguousarray(inputs["L0_attn_Wk"]),
        Wv=np.ascontiguousarray(inputs["L0_attn_Wv"]),
        Wo=np.ascontiguousarray(inputs["L0_attn_Wo"]),
        Wg1=np.ascontiguousarray(inputs["L0_attn_Wg1"]),
        Wg2=np.ascontiguousarray(inputs["L0_attn_Wg2"]),
        nwc=np.ascontiguousarray(inputs["L0_attn_norm_w"].reshape(8, 128).T),
        cosim=cosim, sinim=sinim,
        tri=np.triu(np.ones((128, 128), np.float32)),
        identb=np.eye(128, dtype=np.float32).astype(ml_dtypes.bfloat16),
        onesrow=np.ones((1, 128), np.float32),
        onescol=np.ones((128, 1), np.float32),
        zrow=np.zeros((1, 512), np.float32),
        w1i=np.ascontiguousarray(inputs["L0_ffn_w1_w"].T.reshape(8, 128, 32, 128)
                                 .transpose(2, 1, 0, 3).reshape(32, 128, 1024)),
        w2i=np.ascontiguousarray(inputs["L0_ffn_w2_w"].T.reshape(8, 128, 32, 128)
                                 .transpose(2, 1, 0, 3).reshape(32, 128, 1024)),
        h1i=np.ascontiguousarray(inputs["L0_ffn_h1_w"].T.reshape(32, 128, 32, 128)
                                 .transpose(2, 1, 0, 3).reshape(32, 128, 4096)),
        h2i=np.ascontiguousarray(inputs["L0_ffn_h2_w"].T.reshape(32, 128, 32, 128)
                                 .transpose(2, 1, 0, 3).reshape(32, 128, 4096)),
        w3i=np.ascontiguousarray(inputs["L0_ffn_w3_w"].T.reshape(32, 128, 1024)),
        s1c=np.ascontiguousarray(inputs["L0_ffn_w1_s"].reshape(32, 128).T),
        s2c=np.ascontiguousarray(inputs["L0_ffn_w2_s"].reshape(32, 128).T),
        h1c=np.ascontiguousarray(inputs["L0_ffn_h1_s"].reshape(32, 128).T),
        h2c=np.ascontiguousarray(inputs["L0_ffn_h2_s"].reshape(32, 128).T),
        s3bc=np.ascontiguousarray(np.broadcast_to(inputs["L0_ffn_w3_s"][None, :], (128, 1024))),
        cvals=np.ascontiguousarray(np.broadcast_to(np.array(CONST_VALS, np.float32)[None, :],
                                                   (128, len(CONST_VALS)))),
    )
    maps = []
    for core in range(8):
        b, hf = core // 2, core % 2
        m = dict(common)
        m["xb"] = np.ascontiguousarray(x[b])
        m["x_own"] = np.ascontiguousarray(x[b, hf * HALF:(hf + 1) * HALF])
        m["half_ofs"] = np.array([[hf * HALF]], np.int32)
        maps.append(m)
    return maps


def kernel(**inputs):
    nc = build_nc()
    in_maps = _host_inputs(inputs)
    res = run_bass_kernel_spmd(nc, in_maps, core_ids=list(range(8)))
    out = np.empty((B, N, D), np.float32)
    for core in range(8):
        b, hf = core // 2, core % 2
        out[b, hf * HALF:(hf + 1) * HALF] = res.results[core]["out"]
    return out


# version 21
# speedup vs baseline: 1.0450x; 1.0450x over previous
import numpy as np
import ml_dtypes
import concourse.bass as bass
import concourse.mybir as mybir
from concourse.bass_utils import run_bass_kernel_spmd
from contextlib import ExitStack

F32 = mybir.dt.float32
BF16 = mybir.dt.bfloat16
I32 = mybir.dt.int32
AF = mybir.ActivationFunctionType
ALU = mybir.AluOpType
AXL = mybir.AxisListType

B, N, D, H, HD = 4, 4096, 1024, 16, 64
NT = N // 128
HALF = 2048
EPS_LN = 1e-5
EPS_RMS = 1e-6
CONST_VALS = (1e-5, 1.0, 1e-6, 1.0 / 1024.0, -0.5, -1.0)
SAME_ENGINE_WAITS = True

ENGINES = ("sync", "vector", "scalar", "tensor", "gpsimd")


class Builder:
    def __init__(self, nc):
        self.nc = nc
        self.streams = {e: [] for e in ENGINES}
        self.sem_handles = {}
        self.ctr = {}
        self.last_wait = {}
        self.free_ev = {}
        self.prod_ev = {}
        self.engine_sem = {"sync": "L", "vector": "V", "scalar": "A",
                           "tensor": "P", "gpsimd": "G"}

    DMA_SEMS = ("C", "wld0", "wld1", "wld2", "wld3", "x0", "x1", "wo0", "wo1", "wfA0", "wfA1",
                "hw0", "hw1", "w3l0", "w3l1", "xo0", "xo1", "xmr0", "xmr1",
                "hbr", "ogld", "og0", "og1", "hp0", "hp1", "xs0", "xs1",
                "os0", "os1")

    def init_sems(self, ctx):
        for s in ("V", "A", "P", "G", "O") + self.DMA_SEMS:
            self.sem_handles[s] = ctx.enter_context(self.nc.semaphore("sem_" + s))
            self.ctr[s] = 0
        for s in ("L", "W", "X", "S"):
            self.ctr.setdefault(s, 0)
        self.era = 0
        self.era_barriers = {0: []}
        self.engine_era = {}
        self.tensor_era = {}

    def new_era(self):
        self.era += 1
        self.era_barriers[self.era] = [(s, v) for s, v in self.ctr.items() if v > 0]

    def tag(self, handle):
        self.tensor_era[handle.name] = self.era
        return handle

    def totals(self, sems=("P", "V", "A")):
        return [(s, self.ctr[s]) for s in sems]

    def _wait(self, eng, ev):
        sem, val = ev
        if self.engine_sem[eng] == sem:
            assert val <= self.ctr[sem], f"{eng} waiting future own event {ev}"
            if not SAME_ENGINE_WAITS:
                return None
        key = (eng, sem)
        if self.last_wait.get(key, 0) >= val:
            return None
        self.last_wait[key] = val
        return (self.sem_handles[sem], val)

    def _prep(self, eng, waits, inc, inc_n, out_name=None):
        if out_name is not None:
            te = self.tensor_era.get(out_name, 0)
            if te > self.engine_era.get(eng, 0):
                self.engine_era[eng] = te
                waits = list(waits) + self.era_barriers[te]
        wl = []
        for ev in waits:
            if ev is None:
                continue
            w = self._wait(eng, ev)
            if w is not None:
                wl.append(w)
        sem_name = inc if isinstance(inc, str) else (self.engine_sem[eng] if inc else None)
        if sem_name is not None:
            self.ctr[sem_name] += inc_n
            return wl, (sem_name, self.ctr[sem_name]), self.sem_handles[sem_name], inc_n
        return wl, None, None, 0

    def op(self, eng, method, *args, waits=(), inc=None, inc_n=1, **kwargs):
        out_ap = kwargs.get("out")
        if out_ap is None and args:
            out_ap = args[0]
        out_name = getattr(out_ap, "name", None) if out_ap is not None else None
        wl, ev_out, sem_h, k = self._prep(eng, waits, inc, inc_n, out_name)

        def fn(engine, wl=wl, method=method, args=args, kwargs=kwargs, sem_h=sem_h, k=k):
            for sem, val in wl:
                engine.wait_ge(sem, val)
            inst = getattr(engine, method)(*args, **kwargs)
            if sem_h is not None:
                inst.then_inc(sem_h, k)
            return inst

        self.streams[eng].append(fn)
        return ev_out

    def raw(self, eng, fn, waits=(), inc=None, inc_n=1, out_name=None):
        wl, ev_out, sem_h, k = self._prep(eng, waits, inc, inc_n, out_name)

        def wrapped(engine, wl=wl, fn=fn, sem_h=sem_h, k=k):
            for sem, val in wl:
                engine.wait_ge(sem, val)

            def post(inst):
                if sem_h is not None:
                    inst.then_inc(sem_h, k)
            fn(engine, post)

        self.streams[eng].append(wrapped)
        return ev_out

    def wait_free(self, name):
        return self.free_ev.get(name)

    def set_prod(self, name, ev):
        if ev is not None:
            self.prod_ev[name] = ev

    def get_prod(self, name):
        return self.prod_ev.get(name)

    def set_free(self, name, ev):
        if ev is not None:
            self.free_ev[name] = ev


def build_nc():
    nc = bass.Bass("TRN2", target_bir_lowering=False, debug=False)
    dt = nc.dram_tensor

    xb = dt("xb", [N, D], F32, kind="ExternalInput").ap()
    x_own = dt("x_own", [HALF, D], F32, kind="ExternalInput").ap()
    half_ofs = dt("half_ofs", [1, 1], I32, kind="ExternalInput").ap()
    Wq_h = dt("Wq", [D, D], F32, kind="ExternalInput").ap()
    Wk_h = dt("Wk", [D, D], F32, kind="ExternalInput").ap()
    Wv_h = dt("Wv", [D, D], F32, kind="ExternalInput").ap()
    Wo_h = dt("Wo", [D, D], F32, kind="ExternalInput").ap()
    Wg1_h = dt("Wg1", [D, H], F32, kind="ExternalInput").ap()
    Wg2_h = dt("Wg2", [H, D], F32, kind="ExternalInput").ap()
    nwc_h = dt("nwc", [128, 8], F32, kind="ExternalInput").ap()
    cosim_h = dt("cosim", [128, NT * HD], F32, kind="ExternalInput").ap()
    sinim_h = dt("sinim", [128, NT * HD], F32, kind="ExternalInput").ap()
    tri_h = dt("tri", [128, 128], F32, kind="ExternalInput").ap()
    identb_h = dt("identb", [128, 128], BF16, kind="ExternalInput").ap()
    onesrow_h = dt("onesrow", [1, 128], F32, kind="ExternalInput").ap()
    onescol_h = dt("onescol", [128, 1], F32, kind="ExternalInput").ap()
    zrow_h = dt("zrow", [1, 512], F32, kind="ExternalInput").ap()
    w1i_h = dt("w1i", [32, 128, 1024], F32, kind="ExternalInput").ap()
    w2i_h = dt("w2i", [32, 128, 1024], F32, kind="ExternalInput").ap()
    h1i_h = dt("h1i", [32, 128, 4096], F32, kind="ExternalInput").ap()
    h2i_h = dt("h2i", [32, 128, 4096], F32, kind="ExternalInput").ap()
    w3i_h = dt("w3i", [32, 128, 1024], F32, kind="ExternalInput").ap()
    s1c_h = dt("s1c", [128, 32], F32, kind="ExternalInput").ap()
    s2c_h = dt("s2c", [128, 32], F32, kind="ExternalInput").ap()
    h1c_h = dt("h1c", [128, 32], F32, kind="ExternalInput").ap()
    h2c_h = dt("h2c", [128, 32], F32, kind="ExternalInput").ap()
    s3bc_h = dt("s3bc", [128, 1024], F32, kind="ExternalInput").ap()
    cvals_h = dt("cvals", [128, 6], F32, kind="ExternalInput").ap()

    out_h = dt("out", [HALF, D], F32, kind="ExternalOutput").ap()

    og_dram = dt("og_dram", [N, D], BF16, kind="Internal").ap()
    hp_dram = dt("hp_dram", [128, 32 * 1024], BF16, kind="Internal").ap()
    hp2_dram = dt("hp2_dram", [128, 32 * 1024], BF16, kind="Internal").ap()
    xm_dram = dt("xm_dram", [HALF, D], F32, kind="Internal").ap()

    bld = Builder(nc)

    with ExitStack() as gctx:
        sb = lambda name, shape, dtype=F32: bld.tag(gctx.enter_context(nc.sbuf_tensor(name, shape, dtype)))
        bld.init_sems(gctx)

        S_ps = gctx.enter_context(nc.psum_tensor("S_ps", [128, 1024], F32))
        o_ps = gctx.enter_context(nc.psum_tensor("o_ps", [128, 1024], F32))
        tpA = gctx.enter_context(nc.psum_tensor("tpA", [128, 512], F32))
        tpB = gctx.enter_context(nc.psum_tensor("tpB", [128, 512], F32))
        pj0 = gctx.enter_context(nc.psum_tensor("pj0", [128, 512], F32))
        pj1 = gctx.enter_context(nc.psum_tensor("pj1", [128, 512], F32))
        PJ = [("pj0", pj0), ("pj1", pj1)]
        TP = [("tpA", tpA), ("tpB", tpB)]

        tri = sb("tri_s", [128, 128])
        identb = sb("identb_s", [128, 128], BF16)
        onesrow = sb("onesrow_s", [1, 128])
        onescol = sb("onescol_s", [128, 1])
        zrow = sb("zrow_s", [1, 512])
        nwc = sb("nwc_s", [128, 8])
        s1c = sb("s1c_s", [128, 32]); s2c = sb("s2c_s", [128, 32])
        h1c = sb("h1c_s", [128, 32]); h2c = sb("h2c_s", [128, 32])
        cvals = sb("cvals_s", [128, 6])
        for ci, cv in enumerate(CONST_VALS):
            nc.const_aps.register(F32, cv, cvals[:, ci:ci + 1])

        s3bc = sb("s3bc_s", [128, 1024])
        ofs_sb = sb("ofs_sb", [1, 1], I32)
        for dst, src in [(tri, tri_h), (identb, identb_h), (onesrow, onesrow_h),
                         (onescol, onescol_h), (zrow, zrow_h), (nwc, nwc_h),
                         (s1c, s1c_h), (s2c, s2c_h), (h1c, h1c_h), (h2c, h2c_h),
                         (cvals, cvals_h), (s3bc, s3bc_h), (ofs_sb, half_ofs)]:
            bld.op("sync", "dma_start", out=dst[:], in_=src[:], inc="C", inc_n=16)

        with ExitStack() as actx:
            sba = lambda name, shape, dtype=F32: bld.tag(actx.enter_context(nc.sbuf_tensor(name, shape, dtype)))
            cosim = sba("cosim_s", [128, NT * HD])
            sinim = sba("sinim_s", [128, NT * HD])
            Wq_bf = sba("Wq_bf", [128, 8 * 1024], BF16)
            Wk_bf = sba("Wk_bf", [128, 8 * 1024], BF16)
            Wv_bf = sba("Wv_bf", [128, 8 * 1024], BF16)
            Wg1_bf = sba("Wg1_bf", [128, 8 * 16], BF16)
            Wg2_bf = sba("Wg2_bf", [16, 1024], BF16)
            t1f = sba("t1f", [16, N], BF16)
            carry = sba("carry", [1, 1024])
            S_bf = sba("S_bf", [128, 1024], BF16)

            bld.op("sync", "dma_start", out=cosim[:], in_=cosim_h[:], inc="C", inc_n=16)
            bld.op("sync", "dma_start", out=sinim[:], in_=sinim_h[:], inc="C", inc_n=16)
            ev_tabs = ("C", bld.ctr["C"])
            ev_consts = ev_tabs

            with ExitStack() as wctx:
                wld = [bld.tag(wctx.enter_context(nc.sbuf_tensor(f"wld{q}", [128, 1024], F32)))
                       for q in range(4)]
                for wi, (Wh, Wb) in enumerate([(Wq_h, Wq_bf), (Wk_h, Wk_bf), (Wv_h, Wv_bf)]):
                    for c in range(8):
                        q = (wi * 8 + c) % 4
                        evl = bld.op("sync", "dma_start", out=wld[q][:],
                                     in_=Wh[c * 128:(c + 1) * 128, :],
                                     waits=[bld.wait_free(f"wld{q}")], inc=f"wld{q}", inc_n=16)
                        evc = bld.op("vector", "tensor_copy",
                                     out=Wb[:, c * 1024:(c + 1) * 1024], in_=wld[q][:],
                                     waits=[evl], inc=True)
                        bld.set_free(f"wld{q}", evc)
                evl = bld.op("sync", "dma_start",
                             out=wld[0][:, 0:128].rearrange("p (c g) -> p c g", g=16),
                             in_=Wg1_h[:].rearrange("(c p) g -> p c g", p=128),
                             waits=[bld.wait_free("wld0")], inc="wld0", inc_n=16)
                evc = bld.op("vector", "tensor_copy", out=Wg1_bf[:], in_=wld[0][:, 0:128],
                             waits=[evl], inc=True)
                bld.set_free("wld0", evc)
                evl = bld.op("sync", "dma_start", out=wld[1][0:16, :], in_=Wg2_h[:],
                             waits=[bld.wait_free("wld1")], inc="wld1", inc_n=16)
                evc = bld.op("vector", "tensor_copy", out=Wg2_bf[:], in_=wld[1][0:16, :],
                             waits=[evl], inc=True)
                bld.set_free("wld1", evc)
            bld.new_era()
            ev_attw = evc

            x_t = [sba(f"x_t{p}", [128, 1024]) for p in range(2)]
            x1t = [sba(f"x1t{p}", [128, 1024], BF16) for p in range(2)]
            x1f = [sba(f"x1f{p}", [128, 1024], BF16) for p in range(2)]
            sqscr = sba("sqscr", [128, 1024], BF16)
            st = {nm: [sba(f"st_{nm}_{p}", [128, 1]) for p in range(2)]
                  for nm in ("s1", "sq", "m", "t0", "v", "r", "nmr", "oss", "orr")}
            qraw = [sba(f"qraw{p}", [128, 1024]) for p in range(2)]
            escr = [sba(f"escr{p}", [128, 1024]) for p in range(2)]
            E_tm = [sba(f"E_tm{p}", [128, 1024]) for p in range(2)]
            sig = [sba(f"sig{p}", [128, 1024]) for p in range(2)]
            qh_tm = [sba(f"qh_tm{p}", [128, 2048], BF16) for p in range(2)]
            eh_tm = [sba(f"eh_tm{p}", [128, 2048], BF16) for p in range(2)]
            qh_fm = [sba(f"qh_fm{p}", [128, 2048], BF16) for p in range(2)]
            eh_fm = [sba(f"eh_fm{p}", [128, 2048], BF16) for p in range(2)]
            v_tm = [sba(f"v_tm{p}", [128, 1024], BF16) for p in range(2)]
            A_sb = [sba(f"A_sb{p}", [128, 128], BF16) for p in range(2)]
            og = sba("og", [128, 1024])
            ogn = [sba(f"ogn{p}", [128, 1024], BF16) for p in range(2)]

            ev_z1 = bld.op("tensor", "matmul", out=S_ps[:, 0:512], lhsT=onesrow[:],
                           rhs=zrow[:], start=True, stop=True, waits=[ev_consts], inc=True)
            ev_z2 = bld.op("tensor", "matmul", out=S_ps[:, 512:1024], lhsT=onesrow[:],
                           rhs=zrow[:], start=True, stop=True, inc=True)
            bld.set_prod("S_ps_init", ev_z2)
            ev_cz = bld.op("vector", "memset", carry[:], 0.0, waits=[ev_consts], inc=True)
            bld.set_prod("carry_h0", ev_cz)
            bld.set_prod("carry_h1", ev_cz)

            def tile_pre1(i):
                p = i % 2
                evx = bld.op("sync", "dma_start", out=x_t[p][:],
                             in_=xb[i * 128:(i + 1) * 128, :],
                             waits=[bld.wait_free(f"x_t{p}"), ev_attw if i < 2 else None],
                             inc=f"x{p}", inc_n=16)
                ev_s1 = bld.op("vector", "reduce_sum", out=st["s1"][p][:], in_=x_t[p][:],
                               axis=AXL.X, waits=[evx, bld.wait_free(f"stats{p}")], inc=True)
                ev_sq = bld.op("scalar", "activation", out=sqscr[:], in_=x_t[p][:],
                               func=AF.Square, accum_out=st["sq"][p][:],
                               waits=[evx, bld.wait_free("sqscr")], inc=True)
                bld.set_free("sqscr", ev_sq)
                ev_m = bld.op("vector", "tensor_scalar_mul", out=st["m"][p][:], in0=st["s1"][p][:],
                              scalar1=1.0 / D, waits=[ev_s1], inc=True)
                ev_t0 = bld.op("vector", "tensor_mul", out=st["t0"][p][:], in0=st["m"][p][:],
                               in1=st["m"][p][:], waits=[ev_m], inc=True)
                ev_v = bld.op("vector", "scalar_tensor_tensor", out=st["v"][p][:], in0=st["sq"][p][:],
                              scalar=1.0 / D, in1=st["t0"][p][:], op0=ALU.mult, op1=ALU.subtract,
                              waits=[ev_sq, ev_t0], inc=True)
                ev_ln = bld.op("scalar", "activation", out=st["v"][p][:], in_=st["v"][p][:],
                               func=AF.Ln, bias=EPS_LN,
                               waits=[ev_v, ev_consts if i < 2 else None], inc=True)
                ev_r = bld.op("scalar", "activation", out=st["r"][p][:], in_=st["v"][p][:],
                              func=AF.Exp, scale=-0.5, waits=[ev_ln], inc=True)
                ev_nmr = bld.op("vector", "scalar_tensor_tensor", out=st["nmr"][p][:],
                                in0=st["m"][p][:], scalar=-1.0, in1=st["r"][p][:],
                                op0=ALU.mult, op1=ALU.mult, waits=[ev_m, ev_r], inc=True)
                bld.set_free(f"stats{p}", ev_nmr)
                ev_x1t = bld.op("vector", "tensor_scalar", out=x1t[p][:], in0=x_t[p][:],
                                scalar1=st["m"][p][:], scalar2=st["r"][p][:],
                                op0=ALU.subtract, op1=ALU.mult,
                                waits=[ev_r, bld.wait_free(f"x1t{p}")], inc=True)
                bld.set_free(f"x_t{p}", ev_x1t)
                bld.set_prod(f"x1t_{i}", ev_x1t)

            def tile_x1tp(i):
                p = i % 2
                for half in range(2):
                    tpn, tp = TP[half]
                    tpv = tp[:].bitcast(BF16)
                    ev_tp = None
                    for c in range(4):
                        cc = half * 4 + c
                        ev_tp = bld.op("tensor", "transpose",
                                       out=tpv[:, c * 128:(c + 1) * 128],
                                       in_=x1t[p][:, cc * 128:(cc + 1) * 128],
                                       identity=identb[:],
                                       waits=[bld.get_prod(f"x1t_{i}"), bld.wait_free(tpn)] if c == 0 else [],
                                       inc=(c == 3))
                    ev_ev = bld.op("vector", "tensor_copy",
                                   out=x1f[p][:, half * 512:(half + 1) * 512], in_=tpv[:, 0:512],
                                   waits=[ev_tp, bld.wait_free(f"x1f{p}_h{half}")], inc=True)
                    bld.set_free(tpn, ev_ev)
                    bld.set_prod(f"x1f{p}_h{half}", ev_ev)
                bld.set_free(f"x1t{p}", bld.get_prod(f"x1f{p}_h1"))

            def tile_pre2(i):
                p = i % 2

                def proj(Wb):
                    evs = []
                    for half in range(2):
                        pjn, pj = PJ[half]
                        ev_mm = None
                        for c in range(8):
                            ev_mm = bld.op("tensor", "matmul", out=pj[:],
                                           lhsT=x1f[p][:, c * 128:(c + 1) * 128],
                                           rhs=Wb[:, c * 1024 + half * 512: c * 1024 + half * 512 + 512],
                                           start=(c == 0), stop=(c == 7),
                                           waits=[bld.get_prod(f"x1f{p}_h{c // 4}"),
                                                  bld.wait_free(pjn)] if c in (0, 4) else [],
                                           inc=(c == 7))
                        evs.append((pjn, pj, ev_mm))
                    return evs

                for half, (pjn, pj, ev_mm) in enumerate(proj(Wq_bf)):
                    sl = slice(half * 512, half * 512 + 512)
                    ev_qr = bld.op("vector", "tensor_copy", out=qraw[p][:, sl], in_=pj[:],
                                   waits=[ev_mm, bld.wait_free(f"qraw{p}")], inc=True)
                    ev_e = bld.op("scalar", "activation", out=escr[p][:, sl], in_=pj[:],
                                  func=AF.Exp, scale=-1.0,
                                  waits=[ev_mm, ev_qr, bld.wait_free(f"escr{p}")], inc=True)
                    bld.set_free(pjn, ev_e)
                    ev_l = bld.op("scalar", "activation", out=escr[p][:, sl], in_=escr[p][:, sl],
                                  func=AF.Ln, bias=1.0, waits=[ev_e], inc=True)
                    ev_g = bld.op("scalar", "activation", out=escr[p][:, sl], in_=escr[p][:, sl],
                                  func=AF.Exp, scale=-1.0, waits=[ev_l], inc=True)
                    bld.set_prod(f"gq{p}_h{half}", ev_g)
                    bld.set_prod(f"qraw{p}_h{half}", ev_qr)
                for half, (pjn, pj, ev_mm) in enumerate(proj(Wk_bf)):
                    sl = slice(half * 512, half * 512 + 512)
                    ev_e = bld.op("scalar", "activation", out=E_tm[p][:, sl], in_=pj[:],
                                  func=AF.Exp, waits=[ev_mm, bld.wait_free(f"E_tm{p}")], inc=True)
                    bld.set_free(pjn, ev_e)
                    bld.set_prod(f"E_tm{p}_h{half}", ev_e)
                for half, (pjn, pj, ev_mm) in enumerate(proj(Wv_bf)):
                    sl = slice(half * 512, half * 512 + 512)
                    ev_v2 = bld.op("vector", "tensor_copy", out=v_tm[p][:, sl], in_=pj[:],
                                   waits=[ev_mm, bld.wait_free(f"v_tm{p}")], inc=True)
                    bld.set_free(pjn, ev_v2)
                    bld.set_prod(f"v_tm{p}_h{half}", ev_v2)
                pjn, pj = PJ[0]
                ev_mm = None
                for c in range(8):
                    ev_mm = bld.op("tensor", "matmul", out=pj[0:16, 0:128],
                                   lhsT=Wg1_bf[:, c * 16:(c + 1) * 16],
                                   rhs=x1f[p][:, c * 128:(c + 1) * 128],
                                   start=(c == 0), stop=(c == 7),
                                   waits=[bld.wait_free(pjn)] if c == 0 else [],
                                   inc=(c == 7))
                ev_t1 = bld.op("vector", "tensor_copy", out=t1f[:, i * 128:(i + 1) * 128],
                               in_=pj[0:16, 0:128], waits=[ev_mm], inc=True)
                bld.set_free(pjn, ev_t1)
                bld.set_prod(f"t1_{i}", ev_t1)
                bld.set_free(f"x1f{p}_h0", ev_mm)
                bld.set_free(f"x1f{p}_h1", ev_mm)
                ev_lnps = []
                for half in range(2):
                    pjn, pj = PJ[half]
                    sl = slice(half * 512, half * 512 + 512)
                    bld.op("tensor", "matmul", out=pj[:], lhsT=tri[:],
                           rhs=E_tm[p][:, sl], start=True, stop=False,
                           waits=[bld.get_prod(f"E_tm{p}_h{half}"), bld.get_prod(f"carry_h{half}"),
                                  bld.wait_free(pjn)], inc=False)
                    ev_p2 = bld.op("tensor", "matmul", out=pj[:], lhsT=onesrow[:],
                                   rhs=carry[:, sl], start=False, stop=True, inc=True)
                    ev_lnp = bld.op("scalar", "activation", out=sig[p][:, sl], in_=pj[:],
                                    func=AF.Ln, bias=1.0,
                                    waits=[ev_p2, bld.wait_free(f"sig{p}")], inc=True)
                    ev_sig = bld.op("scalar", "activation", out=sig[p][:, sl], in_=sig[p][:, sl],
                                    func=AF.Exp, scale=-1.0, waits=[ev_lnp], inc=True)
                    bld.set_prod(f"sig{p}_h{half}", ev_sig)
                    ev_lnps.append(ev_lnp)
                for half in range(2):
                    pjn, pj = PJ[half]
                    sl = slice(half * 512, half * 512 + 512)
                    ev_tt = bld.op("tensor", "matmul", out=pj[0:1, 0:512], lhsT=onescol[:],
                                   rhs=E_tm[p][:, sl], start=True, stop=True,
                                   waits=[ev_lnps[half]], inc=True)
                    ev_cu = bld.op("vector", "tensor_add", out=carry[:, sl], in0=carry[:, sl],
                                   in1=pj[0:1, 0:512], waits=[ev_tt], inc=True)
                    bld.set_free(pjn, ev_cu)
                    bld.set_prod(f"carry_h{half}", ev_cu)
            def tile_lrpe(i):
                p = i % 2
                ev_gs = bld.op("vector", "tensor_mul", out=escr[p][:], in0=escr[p][:], in1=sig[p][:],
                               waits=[bld.get_prod(f"gq{p}_h0"), bld.get_prod(f"gq{p}_h1"),
                                      bld.get_prod(f"sig{p}_h0"), bld.get_prod(f"sig{p}_h1")],
                               inc=True)
                bld.set_free(f"sig{p}", ev_gs)
                ev_tq = bld.op("vector", "tensor_mul", out=qraw[p][:], in0=qraw[p][:], in1=escr[p][:],
                               waits=[ev_gs], inc=True)
                bld.set_free(f"escr{p}", ev_tq)
                ct = cosim[:, i * HD:(i + 1) * HD]
                stt_ = sinim[:, i * HD:(i + 1) * HD]
                ct_b = ct.unsqueeze(1).broadcast_to((128, 16, 64))
                st_b = stt_.unsqueeze(1).broadcast_to((128, 16, 64))
                tq3 = qraw[p][:].rearrange("p (h d) -> p h d", d=64)
                E3 = E_tm[p][:].rearrange("p (h d) -> p h d", d=64)
                qh4 = qh_tm[p][:].rearrange("p (h t d) -> p h t d", h=16, t=2)
                eh4 = eh_tm[p][:].rearrange("p (h t d) -> p h t d", h=16, t=2)
                bld.op("vector", "tensor_mul", out=qh4[:, :, 0, :], in0=tq3, in1=ct_b,
                       waits=[ev_tq, ev_tabs, bld.wait_free(f"qh_tm{p}")], inc=True)
                ev_q2 = bld.op("vector", "tensor_mul", out=qh4[:, :, 1, :], in0=tq3, in1=st_b,
                               inc=True)
                bld.set_free(f"qraw{p}", ev_q2)
                ev_e1 = bld.op("gpsimd", "tensor_mul", out=eh4[:, :, 0, :], in0=E3, in1=ct_b,
                               waits=[bld.get_prod(f"E_tm{p}_h0"), bld.get_prod(f"E_tm{p}_h1"),
                                      ev_tabs, bld.wait_free(f"eh_tm{p}")], inc="O")
                ev_e2 = bld.op("gpsimd", "tensor_mul", out=eh4[:, :, 1, :], in0=E3, in1=st_b,
                               inc="O")
                bld.set_free(f"E_tm{p}", ev_e2)
                bld.set_prod(f"qh_tm{p}", ev_q2)
                bld.set_prod(f"eh_tm{p}", ev_e2)

            def tile_qe_tp(i):
                p = i % 2
                ev_q2 = bld.get_prod(f"qh_tm{p}")
                ev_e2 = bld.get_prod(f"eh_tm{p}")
                for (src, sname, dst, dname, pev) in (
                        (qh_tm, f"qh_tm{p}", qh_fm, f"qh_fm{p}", ev_q2),
                        (eh_tm, f"eh_tm{p}", eh_fm, f"eh_fm{p}", ev_e2)):
                    ev_tp_last = None
                    for grp in range(4):
                        tpn, tp = TP[grp % 2]
                        tpv = tp[:].bitcast(BF16)
                        ev_tp = None
                        for c in range(4):
                            h = grp * 4 + c
                            ev_tp = bld.op("tensor", "transpose",
                                           out=tpv[:, c * 128:(c + 1) * 128],
                                           in_=src[p][:, h * 128:(h + 1) * 128],
                                           identity=identb[:],
                                           waits=[pev, bld.wait_free(tpn)] if c == 0 else [],
                                           inc=(c == 3))
                        ev_tp_last = ev_tp
                        eng_ev = "vector" if grp % 2 == 0 else "scalar"
                        ev_ev = bld.op(eng_ev, "tensor_copy" if grp % 2 == 0 else "copy",
                                       **({"out": dst[p][:, grp * 512:(grp + 1) * 512],
                                           "in_": tpv[:, 0:512]}),
                                       waits=[ev_tp, bld.wait_free(dname)] if grp == 0 else [ev_tp],
                                       inc=True)
                        bld.set_free(tpn, ev_ev)
                        bld.set_prod(f"{dname}_g{grp}", ev_ev)
                    bld.set_prod(dname, bld.get_prod(f"{dname}_g3"))
                    if sname.startswith("qh_tm"):
                        bld.set_free(sname, ev_tp_last)

            def tile_gla(i):
                p = i % 2
                if i > 0:
                    ev_sbf = bld.op("vector", "tensor_copy", out=S_bf[:], in_=S_ps[:],
                                    waits=[bld.get_prod("S_upd"), bld.wait_free("S_bf")],
                                    inc=True)
                    bld.set_prod("S_bf", ev_sbf)
                    bld.set_free("S_ps_rd", ev_sbf)
                for half in range(2):
                    pjn, pj = PJ[half]
                    sl = slice(half * 512, half * 512 + 512)
                    ev_mm = bld.op("tensor", "matmul", out=pj[:],
                                   lhsT=t1f[:, i * 128:(i + 1) * 128],
                                   rhs=Wg2_bf[:, half * 512:half * 512 + 512],
                                   start=True, stop=True,
                                   waits=[bld.get_prod(f"t1_{i}"), bld.wait_free(pjn)],
                                   inc=True)
                    ev_e = bld.op("scalar", "activation", out=escr[p][:, sl], in_=pj[:],
                                  func=AF.Exp, scale=-1.0,
                                  waits=[ev_mm, bld.wait_free(f"escr{p}")], inc=True)
                    bld.set_free(pjn, ev_e)
                    ev_l = bld.op("scalar", "activation", out=escr[p][:, sl], in_=escr[p][:, sl],
                                  func=AF.Ln, bias=1.0, waits=[ev_e], inc=True)
                    ev_g = bld.op("scalar", "activation", out=escr[p][:, sl], in_=escr[p][:, sl],
                                  func=AF.Exp, scale=-1.0, waits=[ev_l], inc=True)
                    bld.set_prod(f"gate{p}_h{half}", ev_g)
                def emit_A(h):
                    qsl = slice(h * 128, (h + 1) * 128)
                    pa = h % 2
                    tpn, tp = TP[pa]
                    ev_a = bld.op("tensor", "matmul", out=tp[:, 0:128],
                                  lhsT=eh_fm[p][:, qsl], rhs=qh_fm[p][:, qsl],
                                  start=True, stop=True,
                                  waits=[bld.get_prod(f"eh_fm{p}_g{h // 4}"),
                                         bld.get_prod(f"qh_fm{p}_g{h // 4}"),
                                         bld.wait_free(tpn)],
                                  inc=True)
                    ev_am = bld.op("vector", "tensor_mul", out=A_sb[pa][:], in0=tp[:, 0:128],
                                   in1=tri[:], waits=[ev_a, bld.wait_free(f"A_sb{pa}")],
                                   inc=True)
                    bld.set_free(tpn, ev_am)
                    bld.set_prod(f"am_{h}", ev_am)

                def emit_intra_S(h):
                    hb = h // 8
                    hsl = slice(h * 64, (h + 1) * 64)
                    qsl = slice(h * 128, (h + 1) * 128)
                    pa = h % 2
                    if i > 0:
                        bld.op("tensor", "matmul", out=o_ps[:, hsl],
                               lhsT=qh_fm[p][:, qsl], rhs=S_bf[:, hsl],
                               start=True, stop=True,
                               waits=[bld.get_prod("S_bf"),
                                      bld.get_prod(f"qh_fm{p}_g{h // 4}"),
                                      bld.wait_free(f"o_ps{hb}")],
                               inc=False)
                    bld.op("tensor", "matmul", out=o_ps[:, hsl],
                           lhsT=A_sb[pa][:], rhs=v_tm[p][:, hsl],
                           start=(i == 0), stop=True,
                           waits=[bld.get_prod(f"am_{h}"), bld.get_prod(f"v_tm{p}_h{h // 8}"),
                                  bld.wait_free(f"o_ps{hb}") if i == 0 else None],
                           inc=False)
                    ev_mark = bld.op("tensor", "matmul", out=S_ps[:, hsl],
                                     lhsT=eh_tm[p][:, qsl], rhs=v_tm[p][:, hsl],
                                     start=False, stop=True,
                                     waits=[bld.get_prod(f"eh_tm{p}"),
                                            bld.get_prod("S_ps_init") if i == 0 and h == 0 else None,
                                            bld.wait_free("S_ps_rd") if h == 0 else None],
                                     inc=True)
                    bld.set_free(f"A_sb{pa}", ev_mark)
                    if h == 15:
                        bld.set_prod("S_upd", ev_mark)
                        bld.set_prod(f"o_done_{i}", ev_mark)
                        bld.set_free("S_bf", ev_mark)
                        bld.set_free(f"qh_fm{p}", ev_mark)
                        bld.set_free(f"eh_fm{p}", ev_mark)
                        bld.set_free(f"v_tm{p}", ev_mark)
                        bld.set_free(f"eh_tm{p}", ev_mark)

                emit_A(0)
                for h in range(16):
                    if h + 1 < 16:
                        emit_A(h + 1)
                    emit_intra_S(h)
            def tile_og(i):
                p = i % 2
                ev_od = bld.get_prod(f"o_done_{i}")
                for half in range(2):
                    sl = slice(half * 512, half * 512 + 512)
                    ev_og2 = bld.op("vector", "tensor_mul", out=og[:, sl],
                                    in0=o_ps[:, sl], in1=escr[p][:, sl],
                                    waits=[ev_od, bld.get_prod(f"gate{p}_h{half}"),
                                           bld.wait_free("og")] if half == 0 else [
                                           bld.get_prod(f"gate{p}_h1")], inc=True)
                bld.set_free(f"escr{p}", ev_og2)
                bld.set_free("o_ps0", ev_og2)
                bld.set_free("o_ps1", ev_og2)
                ev_oss = bld.op("scalar", "activation", out=sqscr[:], in_=og[:],
                                func=AF.Square, accum_out=st["oss"][p][:],
                                waits=[ev_og2, bld.wait_free("sqscr")], inc=True)
                bld.set_free("sqscr", ev_oss)
                ev_lno = bld.op("scalar", "activation", out=st["oss"][p][:], in_=st["oss"][p][:],
                                func=AF.Ln, scale=1.0 / D, bias=EPS_RMS, waits=[ev_oss], inc=True)
                ev_ro = bld.op("scalar", "activation", out=st["orr"][p][:], in_=st["oss"][p][:],
                               func=AF.Exp, scale=-0.5, waits=[ev_lno], inc=True)
                ev_ogn = bld.op("scalar", "activation", out=ogn[p][:], in_=og[:],
                                func=AF.Copy, scale=st["orr"][p][:],
                                waits=[ev_ro, bld.wait_free(f"ogn{p}")], inc=True)
                bld.set_free("og", ev_ogn)
                ev_st2 = bld.op("gpsimd", "dma_start", out=og_dram[i * 128:(i + 1) * 128, :],
                                in_=ogn[p][:], waits=[ev_ogn], inc=f"og{p}", inc_n=16)
                bld.set_free(f"ogn{p}", ev_st2)
                bld.set_prod(f"og_store_{i}", ev_st2)
                og_counts[i] = (bld.ctr["og0"], bld.ctr["og1"])

            og_counts = {}
            tile_pre1(0)
            tile_pre1(1)
            tile_x1tp(0)
            tile_x1tp(1)
            tile_pre2(0)
            tile_lrpe(0)
            for i in range(NT):
                if i + 2 < NT:
                    tile_pre1(i + 2)
                if i + 1 < NT:
                    tile_pre2(i + 1)
                if i + 2 < NT:
                    tile_x1tp(i + 2)
                tile_qe_tp(i)
                tile_gla(i)
                if i + 1 < NT:
                    tile_lrpe(i + 1)
                tile_og(i)

            ev_last_sbf = bld.get_prod("S_bf")
            ev_last_og = bld.get_prod("og_h1")
            bld.set_free("Sps_nt0", ev_last_sbf)
            bld.set_free("Sps_nt1", ev_last_sbf)
            bld.set_free("ops_nt0", ev_last_og)
            bld.set_free("ops_nt1", ev_last_og)
            bld.og_counts = og_counts
            phaseA_done = [("P", bld.ctr["P"]), ("V", bld.ctr["V"]),
                           ("A", bld.ctr["A"]),
                           ("og0", bld.ctr["og0"]), ("og1", bld.ctr["og1"]),
                           ("x0", bld.ctr["x0"]), ("x1", bld.ctr["x1"])]

        bld.new_era()
        with ExitStack() as bctx:
            sbb = lambda name, shape, dtype=F32: bld.tag(bctx.enter_context(nc.sbuf_tensor(name, shape, dtype)))
            Wo_bf = sbb("Wo_bf", [128, 8 * 1024], BF16)
            ogf = [sbb(f"ogf{p}", [128, 1024], BF16) for p in range(2)]
            xo_t = [sbb(f"xo_t{p}", [128, 1024]) for p in range(2)]
            xm_t = [sbb(f"xm_t{p}", [128, 1024]) for p in range(2)]
            yraw = [sbb(f"yraw{p}", [128, 1024]) for p in range(2)]
            x2t = sbb("x2t", [128, 1024], BF16)
            yln = sbb("yln", [128, 1024])
            sqb = sbb("sqb", [128, 1024], BF16)
            bs = {nm: [sbb(f"bs_{nm}_{p}", [128, 1]) for p in range(2)]
                  for nm in ("s1", "sq", "sqB", "m", "t0", "v", "r", "nmr")}
            hbuf = sbb("hbuf", [128, 32 * 1024], BF16)
            outt = [sbb(f"outt{p}", [128, 1024]) for p in range(2)]
            ev_ofsl = ev_consts

            with ExitStack() as wctx:
                wldb = [bld.tag(wctx.enter_context(nc.sbuf_tensor(f"wldb{q}", [128, 1024], F32)))
                        for q in range(2)]
                for c in range(8):
                    q = c % 2
                    evl = bld.op("sync", "dma_start", out=wldb[q][:],
                                 in_=Wo_h[c * 128:(c + 1) * 128, :],
                                 waits=[bld.wait_free(f"wldb{q}")], inc=f"wo{q}", inc_n=16)
                    evc = bld.op("vector", "tensor_scalar_mul",
                                 out=Wo_bf[:, c * 1024:(c + 1) * 1024], in0=wldb[q][:],
                                 scalar1=nwc[:, c:c + 1],
                                 waits=[evl, ev_consts] + (phaseA_done if c == 0 else []),
                                 inc=True)
                    bld.set_free(f"wldb{q}", evc)
            bld.new_era()
            ev_wo = ("V", bld.ctr["V"])

            regcell = {}

            def setup_reg(engine, post):
                reg = engine.alloc_register("hofs")
                engine.reg_load(reg, ofs_sb[0:1, 0:1])
                regcell["ofs"] = engine.snap(reg)
                inst = engine.nop()
                post(inst)

            ev_reg = bld.raw("gpsimd", setup_reg, waits=[ev_ofsl] + phaseA_done, inc="O")

            for hf in range(2):
                bld.new_era()
                bhctx = ExitStack()
                x2f = bld.tag(bhctx.enter_context(nc.sbuf_tensor(f"x2f{hf}", [128, 8 * 1024], BF16)))
                with ExitStack() as hctx:
                    og_sb = bld.tag(hctx.enter_context(nc.sbuf_tensor(f"og_sb{hf}", [128, 8 * 1024], BF16)))

                    def og_load(engine, post, hf=hf):
                        ofs = regcell["ofs"]
                        src = og_dram[bass.ds(ofs, HALF), :][hf * 1024: hf * 1024 + 1024, :]
                        src = src.rearrange("(t p) m -> p t m", p=128)
                        inst = engine.dma_start(
                            out=og_sb[:].rearrange("p (t m) -> p t m", t=8), in_=src)
                        post(inst)

                    ev_ogl = bld.raw("gpsimd", og_load,
                                     waits=[ev_reg,
                                            ("og0", bld.ctr["og0"]), ("og1", bld.ctr["og1"]),
                                            bld.wait_free("og_sb")],
                                     inc="ogld", inc_n=16, out_name=og_sb.name)
                    bld.set_prod("og_sb", ev_ogl)

                    def B_pe(t):
                        p = t % 2
                        gtok = hf * 1024 + t * 128
                        evx = bld.op("sync", "dma_start", out=xo_t[p][:],
                                     in_=x_own[gtok:gtok + 128, :],
                                     waits=[bld.wait_free(f"xo_t{p}")], inc=f"xo{p}", inc_n=16)
                        bld.set_prod(f"xo_{t}", evx)
                        for half in range(2):
                            tpn, tp = TP[half]
                            tpv = tp[:].bitcast(BF16)
                            ev_tp = None
                            for c in range(4):
                                cc = half * 4 + c
                                ev_tp = bld.op("tensor", "transpose",
                                               out=tpv[:, c * 128:(c + 1) * 128],
                                               in_=og_sb[:, t * 1024 + cc * 128: t * 1024 + cc * 128 + 128],
                                               identity=identb[:],
                                               waits=[ev_ogl, ev_wo, bld.wait_free(tpn)] if c == 0 else [],
                                               inc=(c == 3))
                            ev_ev = bld.op("vector", "tensor_copy",
                                           out=ogf[p][:, half * 512:(half + 1) * 512],
                                           in_=tpv[:, 0:512],
                                           waits=[ev_tp, bld.wait_free(f"ogf{p}")], inc=True)
                            bld.set_free(tpn, ev_ev)
                            bld.set_prod(f"ogf{p}_h{half}", ev_ev)
                        for half in range(2):
                            pjn, pj = PJ[half]
                            ev_mm = None
                            for c in range(8):
                                ev_mm = bld.op("tensor", "matmul", out=pj[:],
                                               lhsT=ogf[p][:, c * 128:(c + 1) * 128],
                                               rhs=Wo_bf[:, c * 1024 + half * 512: c * 1024 + half * 512 + 512],
                                               start=(c == 0), stop=(c == 7),
                                               waits=[bld.get_prod(f"ogf{p}_h{c // 4}"),
                                                      bld.wait_free(pjn)] if c in (0, 4) else [],
                                               inc=(c == 7))
                            ev_yev = bld.op("vector", "tensor_copy",
                                            out=yraw[p][:, half * 512:(half + 1) * 512], in_=pj[:],
                                            waits=[ev_mm, bld.wait_free(f"yraw{p}")], inc=True)
                            bld.set_free(pjn, ev_yev)
                            bld.set_prod(f"yraw_{t}_h{half}", ev_yev)
                        bld.set_free(f"ogf{p}", bld.get_prod("y_ps_h1") if False else ev_mm)

                    def B_ln(t):
                        p = t % 2
                        gtok = hf * 1024 + t * 128
                        evy = [bld.get_prod(f"yraw_{t}_h0"), bld.get_prod(f"yraw_{t}_h1")]
                        ev_sa = bld.op("vector", "reduce_sum", out=bs["s1"][p][:], in_=yraw[p][:],
                                       axis=AXL.X, waits=evy + [bld.wait_free(f"bstats{p}")], inc=True)
                        ev_qa = bld.op("scalar", "activation", out=sqb[:], in_=yraw[p][:],
                                       func=AF.Square, accum_out=bs["sq"][p][:],
                                       waits=evy + [bld.wait_free("sqb")], inc=True)
                        bld.set_free("sqb", ev_qa)
                        ev_m = bld.op("vector", "tensor_scalar_mul", out=bs["m"][p][:],
                                      in0=bs["s1"][p][:], scalar1=1.0 / D, waits=[ev_sa], inc=True)
                        ev_t0 = bld.op("vector", "tensor_mul", out=bs["t0"][p][:], in0=bs["m"][p][:],
                                       in1=bs["m"][p][:], waits=[ev_m], inc=True)
                        ev_v = bld.op("vector", "scalar_tensor_tensor", out=bs["v"][p][:],
                                      in0=bs["sq"][p][:], scalar=1.0 / D, in1=bs["t0"][p][:],
                                      op0=ALU.mult, op1=ALU.subtract, waits=[ev_qa, ev_t0], inc=True)
                        ev_ln = bld.op("scalar", "activation", out=bs["v"][p][:], in_=bs["v"][p][:],
                                       func=AF.Ln, bias=EPS_LN, waits=[ev_v], inc=True)
                        ev_r = bld.op("scalar", "activation", out=bs["r"][p][:], in_=bs["v"][p][:],
                                      func=AF.Exp, scale=-0.5, waits=[ev_ln], inc=True)
                        ev_yl = bld.op("vector", "tensor_scalar", out=yln[:], in0=yraw[p][:],
                                       scalar1=bs["m"][p][:], scalar2=bs["r"][p][:],
                                       op0=ALU.subtract, op1=ALU.mult,
                                       waits=[ev_r, bld.wait_free("yln")], inc=True)
                        ev_xm = bld.op("vector", "tensor_add", out=xm_t[p][:],
                                       in0=yln[:], in1=xo_t[p][:],
                                       waits=[ev_yl, bld.get_prod(f"xo_{t}"),
                                              bld.wait_free(f"xm_t{p}"),
                                              bld.wait_free(f"xm_tst{p}")], inc=True)
                        bld.set_free(f"yraw{p}", ev_yl)
                        bld.set_free(f"xo_t{p}", ev_xm)
                        bld.set_free("yln", ev_xm)
                        ev_xst = bld.op("gpsimd", "dma_start", out=xm_dram[gtok:gtok + 128, :],
                                        in_=xm_t[p][:], waits=[ev_xm], inc=f"xs{p}", inc_n=16)
                        bld.set_prod(f"xm_store_{t}", ev_xst)
                        bld.set_free(f"xm_tst{p}", ev_xst)
                        ev_s3 = bld.op("vector", "reduce_sum", out=bs["s1"][p][:], in_=xm_t[p][:],
                                       axis=AXL.X, waits=[ev_xm], inc=True)
                        ev_q3 = bld.op("scalar", "activation", out=sqb[:], in_=xm_t[p][:],
                                       func=AF.Square, accum_out=bs["sq"][p][:],
                                       waits=[ev_xm, bld.wait_free("sqb")], inc=True)
                        bld.set_free("sqb", ev_q3)
                        ev_m3 = bld.op("vector", "tensor_scalar_mul", out=bs["m"][p][:],
                                       in0=bs["s1"][p][:], scalar1=1.0 / D, waits=[ev_s3], inc=True)
                        ev_t3 = bld.op("vector", "tensor_mul", out=bs["t0"][p][:], in0=bs["m"][p][:],
                                       in1=bs["m"][p][:], waits=[ev_m3], inc=True)
                        ev_v3 = bld.op("vector", "scalar_tensor_tensor", out=bs["v"][p][:],
                                       in0=bs["sq"][p][:], scalar=1.0 / D, in1=bs["t0"][p][:],
                                       op0=ALU.mult, op1=ALU.subtract, waits=[ev_q3, ev_t3], inc=True)
                        ev_l3 = bld.op("scalar", "activation", out=bs["v"][p][:], in_=bs["v"][p][:],
                                       func=AF.Ln, bias=EPS_LN, waits=[ev_v3], inc=True)
                        ev_r3 = bld.op("scalar", "activation", out=bs["r"][p][:], in_=bs["v"][p][:],
                                       func=AF.Exp, scale=-0.5, waits=[ev_l3], inc=True)
                        ev_x2 = bld.op("vector", "tensor_scalar", out=x2t[:], in0=xm_t[p][:],
                                       scalar1=bs["m"][p][:], scalar2=bs["r"][p][:],
                                       op0=ALU.subtract, op1=ALU.mult,
                                       waits=[ev_r3, bld.wait_free("x2t")], inc=True)
                        bld.set_free(f"bstats{p}", ev_x2)
                        bld.set_free(f"xm_t{p}", ev_x2)
                        bld.set_prod(f"x2t_{t}", ev_x2)

                    def B_x2tp(t):
                        ev_x2 = bld.get_prod(f"x2t_{t}")
                        ev_e2 = None
                        for half in range(2):
                            tpn, tp = TP[half]
                            tpv = tp[:].bitcast(BF16)
                            ev_tp = None
                            for c in range(4):
                                cc = half * 4 + c
                                ev_tp = bld.op("tensor", "transpose",
                                               out=tpv[:, c * 128:(c + 1) * 128],
                                               in_=x2t[:, cc * 128:(cc + 1) * 128],
                                               identity=identb[:],
                                               waits=[ev_x2, bld.wait_free(tpn)] if c == 0 else [],
                                               inc=(c == 3))
                            for c in range(4):
                                cc = half * 4 + c
                                ev_e2 = bld.op("vector", "tensor_copy",
                                               out=x2f[:, cc * 1024 + t * 128: cc * 1024 + t * 128 + 128],
                                               in_=tpv[:, c * 128:(c + 1) * 128],
                                               waits=[ev_tp, bld.wait_free("x2f")] if c == 0 else [ev_tp],
                                               inc=(c == 3))
                            bld.set_free(tpn, ev_e2)
                        bld.set_free("x2t", ev_e2)
                        bld.set_prod(f"x2f_t{t}", ev_e2)

                    B_pe(0)
                    for t in range(8):
                        if t + 1 < 8:
                            B_pe(t + 1)
                        B_ln(t)
                        B_x2tp(t)
                    ev_x2f_all = bld.get_prod("x2f_t7")
                    bld.set_free("og_sb", ev_x2f_all)

                bld.new_era()
                ffn_barrier = bld.totals()
                with ExitStack() as c1ctx:
                    wfA = [bld.tag(c1ctx.enter_context(nc.sbuf_tensor(f"wfA{hf}_{q}", [128, 2048], F32)))
                           for q in range(2)]
                    wbA = [bld.tag(c1ctx.enter_context(nc.sbuf_tensor(f"wbA{hf}_{q}", [128, 1024], BF16)))
                           for q in range(2)]
                    wbB = [bld.tag(c1ctx.enter_context(nc.sbuf_tensor(f"wbB{hf}_{q}", [128, 1024], BF16)))
                           for q in range(2)]
                    sA = [bld.tag(c1ctx.enter_context(nc.sbuf_tensor(f"sA{hf}_{q}", [128, 512], F32)))
                          for q in range(2)]
                    for oc in range(32):
                        q = oc % 2
                        evl1 = bld.op("sync", "dma_start", out=wfA[q][:, 0:1024],
                                      in_=w1i_h[oc],
                                      waits=(ffn_barrier if oc < 2 else []) +
                                            [bld.wait_free(f"wfA{q}")], inc=f"wfA{q}", inc_n=16)
                        evl2 = bld.op("sync", "dma_start", out=wfA[q][:, 1024:2048],
                                      in_=w2i_h[oc], inc=f"wfA{q}", inc_n=16)
                        evc1 = bld.op("vector", "tensor_copy", out=wbA[q][:], in_=wfA[q][:, 0:1024],
                                      waits=[evl2, bld.wait_free(f"wbA{q}")], inc=True)
                        evc2 = bld.op("vector", "tensor_copy", out=wbB[q][:], in_=wfA[q][:, 1024:2048],
                                      waits=[evl2, bld.wait_free(f"wbB{q}")], inc=True)
                        bld.set_free(f"wfA{q}", evc2)
                        for nt in range(2):
                            psa = S_ps[:, nt * 512: nt * 512 + 512]
                            psb = o_ps[:, nt * 512: nt * 512 + 512]
                            ev_x2f_nt = ev_x2f_all
                            ev_ma = None
                            for kc in range(8):
                                ev_ma = bld.op("tensor", "matmul", out=psa,
                                               lhsT=wbA[q][:, kc * 128:(kc + 1) * 128],
                                               rhs=x2f[:, kc * 1024 + nt * 512: kc * 1024 + nt * 512 + 512],
                                               start=(kc == 0), stop=(kc == 7),
                                               waits=[evc1, ev_x2f_nt,
                                                      bld.wait_free(f"Sps_nt{nt}")] if kc == 0 else [],
                                               inc=(kc == 7))
                            ev_mb = None
                            for kc in range(8):
                                ev_mb = bld.op("tensor", "matmul", out=psb,
                                               lhsT=wbB[q][:, kc * 128:(kc + 1) * 128],
                                               rhs=x2f[:, kc * 1024 + nt * 512: kc * 1024 + nt * 512 + 512],
                                               start=(kc == 0), stop=(kc == 7),
                                               waits=[evc2, ev_x2f_nt,
                                                      bld.wait_free(f"ops_nt{nt}")] if kc == 0 else [],
                                               inc=(kc == 7))
                            if nt == 1:
                                bld.set_free(f"wbA{q}", ev_ma)
                                bld.set_free(f"wbB{q}", ev_mb)
                            ev_sl = bld.op("scalar", "activation", out=sA[nt][:], in_=psa,
                                           func=AF.Silu, scale=s1c[:, oc:oc + 1],
                                           waits=[ev_ma, bld.wait_free(f"sA{nt}")], inc=True)
                            bld.set_free(f"Sps_nt{nt}", ev_sl)
                            ev_hv = bld.op("vector", "scalar_tensor_tensor",
                                           out=hbuf[:, oc * 1024 + nt * 512: oc * 1024 + nt * 512 + 512],
                                           in0=psb, scalar=s2c[:, oc:oc + 1], in1=sA[nt][:],
                                           op0=ALU.mult, op1=ALU.mult,
                                           waits=[ev_mb, ev_sl, bld.wait_free("hbuf")], inc=True)
                            bld.set_free(f"ops_nt{nt}", ev_hv)
                            bld.set_free(f"sA{nt}", ev_hv)
                            bld.set_prod(f"h_oc{oc}", ev_hv)
                    ev_h_all = bld.get_prod("h_oc31")
                    bld.set_free("x2f", bld.get_prod("h_oc31"))

                bhctx.close()
                bld.new_era()
                c2_barrier = bld.totals()
                with ExitStack() as c2ctx:
                    hwf = [bld.tag(c2ctx.enter_context(nc.sbuf_tensor(f"hwf{hf}_{q}", [128, 2048], F32)))
                           for q in range(2)]
                    hwb = [bld.tag(c2ctx.enter_context(nc.sbuf_tensor(f"hwb{hf}_{q}", [128, 4096], BF16)))
                           for q in range(2)]
                    hp_t = [bld.tag(c2ctx.enter_context(nc.sbuf_tensor(f"hp_t{hf}_{q}", [128, 512], BF16)))
                            for q in range(2)]

                    def hlayer(img_h, scol, dst_dram, src_prod_ev, first_barrier, bg=None):
                        oc_counts = []
                        for oc in range(32):
                            if bg is not None:
                                bg(oc)
                            q = oc % 2
                            evl1 = bld.op("sync", "dma_start", out=hwf[q][:],
                                          in_=img_h[oc][:, 0:2048],
                                          waits=(first_barrier if oc < 2 else []) +
                                                [bld.wait_free(f"hwf{q}")], inc=f"hw{q}", inc_n=16)
                            evc1 = bld.op("vector", "tensor_copy", out=hwb[q][:, 0:2048],
                                          in_=hwf[q][:],
                                          waits=[evl1, bld.wait_free(f"hwb{q}")], inc=True)
                            evl2 = bld.op("sync", "dma_start", out=hwf[q][:],
                                          in_=img_h[oc][:, 2048:4096],
                                          waits=[evc1], inc=f"hw{q}", inc_n=16)
                            evc2 = bld.op("vector", "tensor_copy", out=hwb[q][:, 2048:4096],
                                          in_=hwf[q][:], waits=[evl2], inc=True)
                            bld.set_free(f"hwf{q}", evc2)
                            for nt in range(2):
                                psx = S_ps[:, nt * 512: nt * 512 + 512]
                                ev_mm = None
                                for kc in range(32):
                                    ev_mm = bld.op("tensor", "matmul", out=psx,
                                                   lhsT=hwb[q][:, kc * 128:(kc + 1) * 128],
                                                   rhs=hbuf[:, kc * 1024 + nt * 512: kc * 1024 + nt * 512 + 512],
                                                   start=(kc == 0), stop=(kc == 31),
                                                   waits=[evc1, evc2, src_prod_ev,
                                                          bld.wait_free(f"Sps_nt{nt}")] if kc == 0 else [],
                                                   inc=(kc == 31))
                                if nt == 1:
                                    bld.set_free(f"hwb{q}", ev_mm)
                                pp = nt
                                ev_g = bld.op("scalar", "activation", out=hp_t[pp][:], in_=psx,
                                              func=AF.Gelu, scale=scol[:, oc:oc + 1],
                                              waits=[ev_mm, bld.wait_free(f"hp_t{pp}")], inc=True)
                                bld.set_free(f"Sps_nt{nt}", ev_g)
                                ev_hs = bld.op("gpsimd", "dma_start",
                                               out=dst_dram[:, (oc * 2 + nt) * 512: (oc * 2 + nt) * 512 + 512],
                                               in_=hp_t[pp][:], waits=[ev_g], inc=f"hp{pp}", inc_n=16)
                                bld.set_free(f"hp_t{pp}", ev_hs)
                            oc_counts.append((bld.ctr["hp0"], bld.ctr["hp1"]))
                        return [("hp0", bld.ctr["hp0"]), ("hp1", bld.ctr["hp1"])], oc_counts

                    ev_hp, h1_counts = hlayer(h1i_h, h1c, hp_dram, ev_h_all, c2_barrier)
                    ev_rl = bld.op("gpsimd", "dma_start", out=hbuf[:], in_=hp_dram[:],
                                   waits=ev_hp + [ev_h_all, bld.wait_free("hbuf")],
                                   inc="hbr", inc_n=16)
                    bld.new_era()
                    c6ctx = ExitStack()
                    w3ld = [bld.tag(c6ctx.enter_context(nc.sbuf_tensor(f"w3ld{hf}_{q}", [128, 512], F32, side="right")))
                            for q in range(2)]
                    w3b = bld.tag(c6ctx.enter_context(nc.sbuf_tensor(f"w3b{hf}", [128, 32 * 512], BF16, side="right")))

                    def w3_pair(kc, oh):
                        q = kc % 2
                        evl = bld.op("sync", "dma_start", out=w3ld[q][:],
                                     in_=w3i_h[kc][:, oh * 512: oh * 512 + 512],
                                     waits=[bld.wait_free(f"w3ld{q}")], inc=f"w3l{q}", inc_n=16)
                        evc = bld.op("vector", "tensor_copy",
                                     out=w3b[:, kc * 512:(kc + 1) * 512], in_=w3ld[q][:],
                                     waits=[evl, bld.wait_free("w3b") if kc == 0 else None],
                                     inc=True)
                        bld.set_free(f"w3ld{q}", evc)
                        bld.set_prod(f"w3b_{kc}", evc)

                    def w3_stream(oh):
                        for kc in range(32):
                            w3_pair(kc, oh)

                    ev_hp2, h2_counts = hlayer(h2i_h, h2c, hp2_dram, ev_rl,
                                               [], bg=lambda oc: w3_pair(oc, 0))
                    ev_rl2 = bld.op("gpsimd", "dma_start", out=hbuf[:], in_=hp2_dram[:],
                                    waits=ev_hp2, inc="hbr", inc_n=16)

                bld.new_era()
                if True:
                    y3 = bld.tag(c6ctx.enter_context(nc.sbuf_tensor(f"y3_{hf}", [128, 8 * 1024], F32, side="right")))
                    def D_tile(t):

                        p = t % 2
                        gtok = hf * 1024 + t * 128
                        y3t = y3[:, t * 1024:(t + 1) * 1024]
                        evy = [bld.get_prod(f"y3_t{t}_oh0"), bld.get_prod(f"y3_t{t}_oh1")]
                        ev_xl = bld.op("sync", "dma_start", out=xm_t[p][:],
                                       in_=xm_dram[gtok:gtok + 128, :],
                                       waits=[bld.get_prod(f"xm_store_{t}"),
                                              bld.wait_free(f"xm_t{p}"),
                                              bld.wait_free(f"xm_tst{p}")], inc=f"xmr{p}", inc_n=16)
                        ev_s4 = bld.op("vector", "reduce_sum", out=bs["s1"][p][:], in_=y3t,
                                       axis=AXL.X, waits=evy + [bld.wait_free(f"bstats{p}")], inc=True)
                        ev_q4 = bld.op("scalar", "activation", out=sqb[:], in_=y3t,
                                       func=AF.Square, accum_out=bs["sq"][p][:],
                                       waits=evy + [bld.wait_free("sqb")], inc=True)
                        bld.set_free("sqb", ev_q4)
                        ev_m4 = bld.op("vector", "tensor_scalar_mul", out=bs["m"][p][:],
                                       in0=bs["s1"][p][:], scalar1=1.0 / D, waits=[ev_s4], inc=True)
                        ev_t4 = bld.op("vector", "tensor_mul", out=bs["t0"][p][:], in0=bs["m"][p][:],
                                       in1=bs["m"][p][:], waits=[ev_m4], inc=True)
                        ev_v4 = bld.op("vector", "scalar_tensor_tensor", out=bs["v"][p][:],
                                       in0=bs["sq"][p][:], scalar=1.0 / D, in1=bs["t0"][p][:],
                                       op0=ALU.mult, op1=ALU.subtract, waits=[ev_q4, ev_t4], inc=True)
                        ev_l4 = bld.op("scalar", "activation", out=bs["v"][p][:], in_=bs["v"][p][:],
                                       func=AF.Ln, bias=EPS_LN, waits=[ev_v4], inc=True)
                        ev_r4 = bld.op("scalar", "activation", out=bs["r"][p][:], in_=bs["v"][p][:],
                                       func=AF.Exp, scale=-0.5, waits=[ev_l4], inc=True)
                        ev_n4 = bld.op("vector", "scalar_tensor_tensor", out=bs["nmr"][p][:],
                                       in0=bs["m"][p][:], scalar=-1.0, in1=bs["r"][p][:],
                                       op0=ALU.mult, op1=ALU.mult, waits=[ev_m4, ev_r4], inc=True)
                        bld.set_free(f"bstats{p}", ev_n4)
                        ev_o4 = bld.op("vector", "tensor_scalar", out=outt[p][:], in0=y3t,
                                       scalar1=bs["m"][p][:], scalar2=bs["r"][p][:],
                                       op0=ALU.subtract, op1=ALU.mult,
                                       waits=[ev_r4, bld.wait_free(f"outt{p}")], inc=True)
                        ev_f = bld.op("vector", "tensor_add", out=outt[p][:], in0=outt[p][:],
                                      in1=xm_t[p][:], waits=[ev_o4, ev_xl], inc=True)
                        bld.set_free(f"xm_t{p}", ev_f)
                        if t == 7:
                            bld.set_free("y3", ev_f)
                        ev_os = bld.op("sync", "dma_start", out=out_h[gtok:gtok + 128, :],
                                       in_=outt[p][:], waits=[ev_f], inc=f"os{p}", inc_n=16)
                        bld.set_free(f"outt{p}", ev_os)

                    for oh in range(2):
                        if oh == 1:
                            w3_stream(1)
                        ev_mm_last = None
                        for t in range(8):
                            pjn, pj = PJ[t % 2]
                            ev_mm = None
                            for kc in range(32):
                                ev_mm = bld.op("tensor", "matmul", out=pj[:],
                                               lhsT=hbuf[:, kc * 1024 + t * 128: kc * 1024 + t * 128 + 128],
                                               rhs=w3b[:, kc * 512:(kc + 1) * 512],
                                               start=(kc == 0), stop=(kc == 31),
                                               waits=[ev_rl2, bld.get_prod("w3b_31"),
                                                      bld.wait_free(pjn)] if kc == 0 else [],
                                               inc=(kc == 31))
                            ev_mm_last = ev_mm
                            ev_y3 = bld.op("vector", "tensor_mul",
                                           out=y3[:, t * 1024 + oh * 512: t * 1024 + oh * 512 + 512],
                                           in0=pj[:], in1=s3bc[:, oh * 512: oh * 512 + 512],
                                           waits=[ev_mm, bld.wait_free("y3") if t == 0 and oh == 0 else None],
                                           inc=True)
                            bld.set_free(pjn, ev_y3)
                            bld.set_prod(f"y3_t{t}_oh{oh}", ev_y3)
                            if oh == 1:
                                D_tile(t)
                        bld.set_free("w3b", ev_mm_last)
                        if oh == 1:
                            bld.set_free("hbuf", ev_mm_last)
                    c6ctx.close()

            bld.op("sync", "wait_ge", bld.sem_handles["os0"], bld.ctr["os0"])
            bld.op("sync", "wait_ge", bld.sem_handles["os1"], bld.ctr["os1"])

            with nc.Block() as block:
                @block.sync
                def _(eng):
                    for f in bld.streams["sync"]:
                        f(eng)

                @block.vector
                def _(eng):
                    for f in bld.streams["vector"]:
                        f(eng)

                @block.scalar
                def _(eng):
                    for f in bld.streams["scalar"]:
                        f(eng)

                @block.tensor
                def _(eng):
                    for f in bld.streams["tensor"]:
                        f(eng)

                @block.gpsimd
                def _(eng):
                    for f in bld.streams["gpsimd"]:
                        f(eng)

    return nc


def _host_inputs(inputs):
    x = np.ascontiguousarray(inputs["L0_attn_x"])
    t = np.arange(N, dtype=np.float64)[:, None]
    j = np.arange(HD, dtype=np.float64)[None, :]
    th = t * (10000.0 ** (-2.0 * j / HD))
    cos_t = np.cos(th).astype(np.float32)
    sin_t = np.sin(th).astype(np.float32)
    cosim = np.ascontiguousarray(cos_t.reshape(NT, 128, HD).transpose(1, 0, 2).reshape(128, NT * HD))
    sinim = np.ascontiguousarray(sin_t.reshape(NT, 128, HD).transpose(1, 0, 2).reshape(128, NT * HD))
    common = dict(
        Wq=np.ascontiguousarray(inputs["L0_attn_Wq"]),
        Wk=np.ascontiguousarray(inputs["L0_attn_Wk"]),
        Wv=np.ascontiguousarray(inputs["L0_attn_Wv"]),
        Wo=np.ascontiguousarray(inputs["L0_attn_Wo"]),
        Wg1=np.ascontiguousarray(inputs["L0_attn_Wg1"]),
        Wg2=np.ascontiguousarray(inputs["L0_attn_Wg2"]),
        nwc=np.ascontiguousarray(inputs["L0_attn_norm_w"].reshape(8, 128).T),
        cosim=cosim, sinim=sinim,
        tri=np.triu(np.ones((128, 128), np.float32)),
        identb=np.eye(128, dtype=np.float32).astype(ml_dtypes.bfloat16),
        onesrow=np.ones((1, 128), np.float32),
        onescol=np.ones((128, 1), np.float32),
        zrow=np.zeros((1, 512), np.float32),
        w1i=np.ascontiguousarray(inputs["L0_ffn_w1_w"].T.reshape(8, 128, 32, 128)
                                 .transpose(2, 1, 0, 3).reshape(32, 128, 1024)),
        w2i=np.ascontiguousarray(inputs["L0_ffn_w2_w"].T.reshape(8, 128, 32, 128)
                                 .transpose(2, 1, 0, 3).reshape(32, 128, 1024)),
        h1i=np.ascontiguousarray(inputs["L0_ffn_h1_w"].T.reshape(32, 128, 32, 128)
                                 .transpose(2, 1, 0, 3).reshape(32, 128, 4096)),
        h2i=np.ascontiguousarray(inputs["L0_ffn_h2_w"].T.reshape(32, 128, 32, 128)
                                 .transpose(2, 1, 0, 3).reshape(32, 128, 4096)),
        w3i=np.ascontiguousarray(inputs["L0_ffn_w3_w"].T.reshape(32, 128, 1024)),
        s1c=np.ascontiguousarray(inputs["L0_ffn_w1_s"].reshape(32, 128).T),
        s2c=np.ascontiguousarray(inputs["L0_ffn_w2_s"].reshape(32, 128).T),
        h1c=np.ascontiguousarray(inputs["L0_ffn_h1_s"].reshape(32, 128).T),
        h2c=np.ascontiguousarray(inputs["L0_ffn_h2_s"].reshape(32, 128).T),
        s3bc=np.ascontiguousarray(np.broadcast_to(inputs["L0_ffn_w3_s"][None, :], (128, 1024))),
        cvals=np.ascontiguousarray(np.broadcast_to(np.array(CONST_VALS, np.float32)[None, :],
                                                   (128, len(CONST_VALS)))),
    )
    maps = []
    for core in range(8):
        b, hf = core // 2, core % 2
        m = dict(common)
        m["xb"] = np.ascontiguousarray(x[b])
        m["x_own"] = np.ascontiguousarray(x[b, hf * HALF:(hf + 1) * HALF])
        m["half_ofs"] = np.array([[hf * HALF]], np.int32)
        maps.append(m)
    return maps


def kernel(**inputs):
    nc = build_nc()
    in_maps = _host_inputs(inputs)
    res = run_bass_kernel_spmd(nc, in_maps, core_ids=list(range(8)))
    out = np.empty((B, N, D), np.float32)
    for core in range(8):
        b, hf = core // 2, core % 2
        out[b, hf * HALF:(hf + 1) * HALF] = res.results[core]["out"]
    return out


# version 22
# speedup vs baseline: 1.0515x; 1.0063x over previous
import numpy as np
import ml_dtypes
import concourse.bass as bass
import concourse.mybir as mybir
from concourse.bass_utils import run_bass_kernel_spmd
from contextlib import ExitStack

F32 = mybir.dt.float32
BF16 = mybir.dt.bfloat16
I32 = mybir.dt.int32
AF = mybir.ActivationFunctionType
ALU = mybir.AluOpType
AXL = mybir.AxisListType

B, N, D, H, HD = 4, 4096, 1024, 16, 64
NT = N // 128
HALF = 2048
EPS_LN = 1e-5
EPS_RMS = 1e-6
CONST_VALS = (1e-5, 1.0, 1e-6, 1.0 / 1024.0, -0.5, -1.0)
SAME_ENGINE_WAITS = True

ENGINES = ("sync", "vector", "scalar", "tensor", "gpsimd")


class Builder:
    def __init__(self, nc):
        self.nc = nc
        self.streams = {e: [] for e in ENGINES}
        self.sem_handles = {}
        self.ctr = {}
        self.last_wait = {}
        self.free_ev = {}
        self.prod_ev = {}
        self.engine_sem = {"sync": "L", "vector": "V", "scalar": "A",
                           "tensor": "P", "gpsimd": "G"}

    DMA_SEMS = ("C", "wld0", "wld1", "wld2", "wld3", "x0", "x1", "wo0", "wo1", "wfA0", "wfA1",
                "hw0", "hw1", "w3l0", "w3l1", "xo0", "xo1", "xmr0", "xmr1",
                "hbr", "ogld", "og0", "og1", "hp0", "hp1", "xs0", "xs1",
                "os0", "os1")

    def init_sems(self, ctx):
        for s in ("V", "A", "P", "G", "O") + self.DMA_SEMS:
            self.sem_handles[s] = ctx.enter_context(self.nc.semaphore("sem_" + s))
            self.ctr[s] = 0
        for s in ("L", "W", "X", "S"):
            self.ctr.setdefault(s, 0)
        self.era = 0
        self.era_barriers = {0: []}
        self.engine_era = {}
        self.tensor_era = {}

    def new_era(self):
        self.era += 1
        self.era_barriers[self.era] = [(s, v) for s, v in self.ctr.items() if v > 0]

    def tag(self, handle):
        self.tensor_era[handle.name] = self.era
        return handle

    def totals(self, sems=("P", "V", "A")):
        return [(s, self.ctr[s]) for s in sems]

    def _wait(self, eng, ev):
        sem, val = ev
        if self.engine_sem[eng] == sem:
            assert val <= self.ctr[sem], f"{eng} waiting future own event {ev}"
            if not SAME_ENGINE_WAITS:
                return None
        key = (eng, sem)
        if self.last_wait.get(key, 0) >= val:
            return None
        self.last_wait[key] = val
        return (self.sem_handles[sem], val)

    def _prep(self, eng, waits, inc, inc_n, out_name=None):
        if out_name is not None:
            te = self.tensor_era.get(out_name, 0)
            if te > self.engine_era.get(eng, 0):
                self.engine_era[eng] = te
                waits = list(waits) + self.era_barriers[te]
        wl = []
        for ev in waits:
            if ev is None:
                continue
            w = self._wait(eng, ev)
            if w is not None:
                wl.append(w)
        sem_name = inc if isinstance(inc, str) else (self.engine_sem[eng] if inc else None)
        if sem_name is not None:
            self.ctr[sem_name] += inc_n
            return wl, (sem_name, self.ctr[sem_name]), self.sem_handles[sem_name], inc_n
        return wl, None, None, 0

    def op(self, eng, method, *args, waits=(), inc=None, inc_n=1, **kwargs):
        out_ap = kwargs.get("out")
        if out_ap is None and args:
            out_ap = args[0]
        out_name = getattr(out_ap, "name", None) if out_ap is not None else None
        wl, ev_out, sem_h, k = self._prep(eng, waits, inc, inc_n, out_name)

        def fn(engine, wl=wl, method=method, args=args, kwargs=kwargs, sem_h=sem_h, k=k):
            for sem, val in wl:
                engine.wait_ge(sem, val)
            inst = getattr(engine, method)(*args, **kwargs)
            if sem_h is not None:
                inst.then_inc(sem_h, k)
            return inst

        self.streams[eng].append(fn)
        return ev_out

    def raw(self, eng, fn, waits=(), inc=None, inc_n=1, out_name=None):
        wl, ev_out, sem_h, k = self._prep(eng, waits, inc, inc_n, out_name)

        def wrapped(engine, wl=wl, fn=fn, sem_h=sem_h, k=k):
            for sem, val in wl:
                engine.wait_ge(sem, val)

            def post(inst):
                if sem_h is not None:
                    inst.then_inc(sem_h, k)
            fn(engine, post)

        self.streams[eng].append(wrapped)
        return ev_out

    def wait_free(self, name):
        return self.free_ev.get(name)

    def set_prod(self, name, ev):
        if ev is not None:
            self.prod_ev[name] = ev

    def get_prod(self, name):
        return self.prod_ev.get(name)

    def set_free(self, name, ev):
        if ev is not None:
            self.free_ev[name] = ev


def build_nc():
    nc = bass.Bass("TRN2", target_bir_lowering=False, debug=False)
    dt = nc.dram_tensor

    xb = dt("xb", [N, D], F32, kind="ExternalInput").ap()
    x_own = dt("x_own", [HALF, D], F32, kind="ExternalInput").ap()
    half_ofs = dt("half_ofs", [1, 1], I32, kind="ExternalInput").ap()
    Wq_h = dt("Wq", [D, D], F32, kind="ExternalInput").ap()
    Wk_h = dt("Wk", [D, D], F32, kind="ExternalInput").ap()
    Wv_h = dt("Wv", [D, D], F32, kind="ExternalInput").ap()
    Wo_h = dt("Wo", [D, D], F32, kind="ExternalInput").ap()
    Wg1_h = dt("Wg1", [D, H], F32, kind="ExternalInput").ap()
    Wg2_h = dt("Wg2", [H, D], F32, kind="ExternalInput").ap()
    nwc_h = dt("nwc", [128, 8], F32, kind="ExternalInput").ap()
    cosim_h = dt("cosim", [128, NT * HD], F32, kind="ExternalInput").ap()
    sinim_h = dt("sinim", [128, NT * HD], F32, kind="ExternalInput").ap()
    tri_h = dt("tri", [128, 128], F32, kind="ExternalInput").ap()
    identb_h = dt("identb", [128, 128], BF16, kind="ExternalInput").ap()
    onesrow_h = dt("onesrow", [1, 128], F32, kind="ExternalInput").ap()
    onescol_h = dt("onescol", [128, 1], F32, kind="ExternalInput").ap()
    zrow_h = dt("zrow", [1, 512], F32, kind="ExternalInput").ap()
    w1i_h = dt("w1i", [32, 128, 1024], F32, kind="ExternalInput").ap()
    w2i_h = dt("w2i", [32, 128, 1024], F32, kind="ExternalInput").ap()
    h1i_h = dt("h1i", [32, 128, 4096], F32, kind="ExternalInput").ap()
    h2i_h = dt("h2i", [32, 128, 4096], F32, kind="ExternalInput").ap()
    w3i_h = dt("w3i", [32, 128, 1024], F32, kind="ExternalInput").ap()
    s1c_h = dt("s1c", [128, 32], F32, kind="ExternalInput").ap()
    s2c_h = dt("s2c", [128, 32], F32, kind="ExternalInput").ap()
    h1c_h = dt("h1c", [128, 32], F32, kind="ExternalInput").ap()
    h2c_h = dt("h2c", [128, 32], F32, kind="ExternalInput").ap()
    s3bc_h = dt("s3bc", [128, 1024], F32, kind="ExternalInput").ap()
    cvals_h = dt("cvals", [128, 6], F32, kind="ExternalInput").ap()

    out_h = dt("out", [HALF, D], F32, kind="ExternalOutput").ap()

    og_dram = dt("og_dram", [N, D], BF16, kind="Internal").ap()
    hp_dram = dt("hp_dram", [128, 32 * 1024], BF16, kind="Internal").ap()
    hp2_dram = dt("hp2_dram", [128, 32 * 1024], BF16, kind="Internal").ap()
    xm_dram = dt("xm_dram", [HALF, D], F32, kind="Internal").ap()

    bld = Builder(nc)

    with ExitStack() as gctx:
        sb = lambda name, shape, dtype=F32: bld.tag(gctx.enter_context(nc.sbuf_tensor(name, shape, dtype)))
        bld.init_sems(gctx)

        S_ps = gctx.enter_context(nc.psum_tensor("S_ps", [128, 1024], F32))
        o_ps = gctx.enter_context(nc.psum_tensor("o_ps", [128, 1024], F32))
        tpA = gctx.enter_context(nc.psum_tensor("tpA", [128, 512], F32))
        tpB = gctx.enter_context(nc.psum_tensor("tpB", [128, 512], F32))
        pj0 = gctx.enter_context(nc.psum_tensor("pj0", [128, 512], F32))
        pj1 = gctx.enter_context(nc.psum_tensor("pj1", [128, 512], F32))
        PJ = [("pj0", pj0), ("pj1", pj1)]
        TP = [("tpA", tpA), ("tpB", tpB)]

        tri = sb("tri_s", [128, 128])
        identb = sb("identb_s", [128, 128], BF16)
        onesrow = sb("onesrow_s", [1, 128])
        onescol = sb("onescol_s", [128, 1])
        zrow = sb("zrow_s", [1, 512])
        nwc = sb("nwc_s", [128, 8])
        s1c = sb("s1c_s", [128, 32]); s2c = sb("s2c_s", [128, 32])
        h1c = sb("h1c_s", [128, 32]); h2c = sb("h2c_s", [128, 32])
        cvals = sb("cvals_s", [128, 6])
        for ci, cv in enumerate(CONST_VALS):
            nc.const_aps.register(F32, cv, cvals[:, ci:ci + 1])

        s3bc = sb("s3bc_s", [128, 1024])
        ofs_sb = sb("ofs_sb", [1, 1], I32)
        for dst, src in [(tri, tri_h), (identb, identb_h), (onesrow, onesrow_h),
                         (onescol, onescol_h), (zrow, zrow_h), (nwc, nwc_h),
                         (s1c, s1c_h), (s2c, s2c_h), (h1c, h1c_h), (h2c, h2c_h),
                         (cvals, cvals_h), (s3bc, s3bc_h), (ofs_sb, half_ofs)]:
            bld.op("sync", "dma_start", out=dst[:], in_=src[:], inc="C", inc_n=16)

        with ExitStack() as actx:
            sba = lambda name, shape, dtype=F32: bld.tag(actx.enter_context(nc.sbuf_tensor(name, shape, dtype)))
            cosim = sba("cosim_s", [128, NT * HD])
            sinim = sba("sinim_s", [128, NT * HD])
            Wq_bf = sba("Wq_bf", [128, 8 * 1024], BF16)
            Wk_bf = sba("Wk_bf", [128, 8 * 1024], BF16)
            Wv_bf = sba("Wv_bf", [128, 8 * 1024], BF16)
            Wg1_bf = sba("Wg1_bf", [128, 8 * 16], BF16)
            Wg2_bf = sba("Wg2_bf", [16, 1024], BF16)
            t1f = sba("t1f", [16, N], BF16)
            carry = sba("carry", [1, 1024])
            S_bf = sba("S_bf", [128, 1024], BF16)

            bld.op("sync", "dma_start", out=cosim[:], in_=cosim_h[:], inc="C", inc_n=16)
            bld.op("sync", "dma_start", out=sinim[:], in_=sinim_h[:], inc="C", inc_n=16)
            ev_tabs = ("C", bld.ctr["C"])
            ev_consts = ev_tabs

            x_t = [sba(f"x_t{p}", [128, 1024]) for p in range(2)]
            x1t = [sba(f"x1t{p}", [128, 1024], BF16) for p in range(2)]
            x1f = [sba(f"x1f{p}", [128, 1024], BF16) for p in range(2)]
            sqscr = sba("sqscr", [128, 1024], BF16)
            st = {nm: [sba(f"st_{nm}_{p}", [128, 1]) for p in range(2)]
                  for nm in ("s1", "sq", "m", "t0", "v", "r", "nmr", "oss", "orr")}
            qraw = [sba(f"qraw{p}", [128, 1024]) for p in range(2)]
            escr = [sba(f"escr{p}", [128, 1024]) for p in range(2)]
            E_tm = [sba(f"E_tm{p}", [128, 1024]) for p in range(2)]
            sig = [sba(f"sig{p}", [128, 1024]) for p in range(2)]
            qh_tm = [sba(f"qh_tm{p}", [128, 2048], BF16) for p in range(2)]
            eh_tm = [sba(f"eh_tm{p}", [128, 2048], BF16) for p in range(2)]
            qh_fm = [sba(f"qh_fm{p}", [128, 2048], BF16) for p in range(2)]
            eh_fm = [sba(f"eh_fm{p}", [128, 2048], BF16) for p in range(2)]
            v_tm = [sba(f"v_tm{p}", [128, 1024], BF16) for p in range(2)]
            A_sb = [sba(f"A_sb{p}", [128, 128], BF16) for p in range(2)]
            og = sba("og", [128, 1024])
            ogn = [sba(f"ogn{p}", [128, 1024], BF16) for p in range(2)]
            with ExitStack() as wctx:
                wld = [bld.tag(wctx.enter_context(nc.sbuf_tensor(f"wld{q}", [128, 1024], F32)))
                       for q in range(4)]
                for wi, (Wh, Wb) in enumerate([(Wq_h, Wq_bf), (Wk_h, Wk_bf), (Wv_h, Wv_bf)]):
                    for c in range(8):
                        q = (wi * 8 + c) % 4
                        evl = bld.op("sync", "dma_start", out=wld[q][:],
                                     in_=Wh[c * 128:(c + 1) * 128, :],
                                     waits=[bld.wait_free(f"wld{q}")], inc=f"wld{q}", inc_n=16)
                        evc = bld.op("vector", "tensor_copy",
                                     out=Wb[:, c * 1024:(c + 1) * 1024], in_=wld[q][:],
                                     waits=[evl], inc=True)
                        bld.set_free(f"wld{q}", evc)
                evl = bld.op("sync", "dma_start",
                             out=wld[0][:, 0:128].rearrange("p (c g) -> p c g", g=16),
                             in_=Wg1_h[:].rearrange("(c p) g -> p c g", p=128),
                             waits=[bld.wait_free("wld0")], inc="wld0", inc_n=16)
                evc = bld.op("vector", "tensor_copy", out=Wg1_bf[:], in_=wld[0][:, 0:128],
                             waits=[evl], inc=True)
                bld.set_free("wld0", evc)
                evl = bld.op("sync", "dma_start", out=wld[1][0:16, :], in_=Wg2_h[:],
                             waits=[bld.wait_free("wld1")], inc="wld1", inc_n=16)
                evc = bld.op("vector", "tensor_copy", out=Wg2_bf[:], in_=wld[1][0:16, :],
                             waits=[evl], inc=True)
                bld.set_free("wld1", evc)
            bld.new_era()
            ev_attw = evc


            ev_z1 = bld.op("tensor", "matmul", out=S_ps[:, 0:512], lhsT=onesrow[:],
                           rhs=zrow[:], start=True, stop=True, waits=[ev_consts], inc=True)
            ev_z2 = bld.op("tensor", "matmul", out=S_ps[:, 512:1024], lhsT=onesrow[:],
                           rhs=zrow[:], start=True, stop=True, inc=True)
            bld.set_prod("S_ps_init", ev_z2)
            ev_cz = bld.op("vector", "memset", carry[:], 0.0, waits=[ev_consts], inc=True)
            bld.set_prod("carry_h0", ev_cz)
            bld.set_prod("carry_h1", ev_cz)

            def tile_pre1(i):
                p = i % 2
                evx = bld.op("sync", "dma_start", out=x_t[p][:],
                             in_=xb[i * 128:(i + 1) * 128, :],
                             waits=[bld.wait_free(f"x_t{p}")],
                             inc=f"x{p}", inc_n=16)
                ev_s1 = bld.op("vector", "reduce_sum", out=st["s1"][p][:], in_=x_t[p][:],
                               axis=AXL.X, waits=[evx, bld.wait_free(f"stats{p}")], inc=True)
                ev_sq = bld.op("scalar", "activation", out=sqscr[:], in_=x_t[p][:],
                               func=AF.Square, accum_out=st["sq"][p][:],
                               waits=[evx, bld.wait_free("sqscr")], inc=True)
                bld.set_free("sqscr", ev_sq)
                ev_m = bld.op("vector", "tensor_scalar_mul", out=st["m"][p][:], in0=st["s1"][p][:],
                              scalar1=1.0 / D, waits=[ev_s1], inc=True)
                ev_t0 = bld.op("vector", "tensor_mul", out=st["t0"][p][:], in0=st["m"][p][:],
                               in1=st["m"][p][:], waits=[ev_m], inc=True)
                ev_v = bld.op("vector", "scalar_tensor_tensor", out=st["v"][p][:], in0=st["sq"][p][:],
                              scalar=1.0 / D, in1=st["t0"][p][:], op0=ALU.mult, op1=ALU.subtract,
                              waits=[ev_sq, ev_t0], inc=True)
                ev_ln = bld.op("scalar", "activation", out=st["v"][p][:], in_=st["v"][p][:],
                               func=AF.Ln, bias=EPS_LN,
                               waits=[ev_v, ev_consts if i < 2 else None], inc=True)
                ev_r = bld.op("scalar", "activation", out=st["r"][p][:], in_=st["v"][p][:],
                              func=AF.Exp, scale=-0.5, waits=[ev_ln], inc=True)
                ev_nmr = bld.op("vector", "scalar_tensor_tensor", out=st["nmr"][p][:],
                                in0=st["m"][p][:], scalar=-1.0, in1=st["r"][p][:],
                                op0=ALU.mult, op1=ALU.mult, waits=[ev_m, ev_r], inc=True)
                bld.set_free(f"stats{p}", ev_nmr)
                ev_x1t = bld.op("vector", "tensor_scalar", out=x1t[p][:], in0=x_t[p][:],
                                scalar1=st["m"][p][:], scalar2=st["r"][p][:],
                                op0=ALU.subtract, op1=ALU.mult,
                                waits=[ev_r, bld.wait_free(f"x1t{p}")], inc=True)
                bld.set_free(f"x_t{p}", ev_x1t)
                bld.set_prod(f"x1t_{i}", ev_x1t)

            def tile_x1tp(i):
                p = i % 2
                for half in range(2):
                    tpn, tp = TP[half]
                    tpv = tp[:].bitcast(BF16)
                    ev_tp = None
                    for c in range(4):
                        cc = half * 4 + c
                        ev_tp = bld.op("tensor", "transpose",
                                       out=tpv[:, c * 128:(c + 1) * 128],
                                       in_=x1t[p][:, cc * 128:(cc + 1) * 128],
                                       identity=identb[:],
                                       waits=[bld.get_prod(f"x1t_{i}"), bld.wait_free(tpn)] if c == 0 else [],
                                       inc=(c == 3))
                    ev_ev = bld.op("vector", "tensor_copy",
                                   out=x1f[p][:, half * 512:(half + 1) * 512], in_=tpv[:, 0:512],
                                   waits=[ev_tp, bld.wait_free(f"x1f{p}_h{half}")], inc=True)
                    bld.set_free(tpn, ev_ev)
                    bld.set_prod(f"x1f{p}_h{half}", ev_ev)
                bld.set_free(f"x1t{p}", bld.get_prod(f"x1f{p}_h1"))

            def tile_pre2(i):
                p = i % 2

                def proj(Wb):
                    evs = []
                    for half in range(2):
                        pjn, pj = PJ[half]
                        ev_mm = None
                        for c in range(8):
                            ev_mm = bld.op("tensor", "matmul", out=pj[:],
                                           lhsT=x1f[p][:, c * 128:(c + 1) * 128],
                                           rhs=Wb[:, c * 1024 + half * 512: c * 1024 + half * 512 + 512],
                                           start=(c == 0), stop=(c == 7),
                                           waits=[bld.get_prod(f"x1f{p}_h{c // 4}"),
                                                  ev_attw if i < 2 else None,
                                                  bld.wait_free(pjn)] if c in (0, 4) else [],
                                           inc=(c == 7))
                        evs.append((pjn, pj, ev_mm))
                    return evs

                for half, (pjn, pj, ev_mm) in enumerate(proj(Wq_bf)):
                    sl = slice(half * 512, half * 512 + 512)
                    ev_qr = bld.op("vector", "tensor_copy", out=qraw[p][:, sl], in_=pj[:],
                                   waits=[ev_mm, bld.wait_free(f"qraw{p}")], inc=True)
                    ev_e = bld.op("scalar", "activation", out=escr[p][:, sl], in_=pj[:],
                                  func=AF.Exp, scale=-1.0,
                                  waits=[ev_mm, ev_qr, bld.wait_free(f"escr{p}")], inc=True)
                    bld.set_free(pjn, ev_e)
                    ev_l = bld.op("scalar", "activation", out=escr[p][:, sl], in_=escr[p][:, sl],
                                  func=AF.Ln, bias=1.0, waits=[ev_e], inc=True)
                    ev_g = bld.op("scalar", "activation", out=escr[p][:, sl], in_=escr[p][:, sl],
                                  func=AF.Exp, scale=-1.0, waits=[ev_l], inc=True)
                    bld.set_prod(f"gq{p}_h{half}", ev_g)
                    bld.set_prod(f"qraw{p}_h{half}", ev_qr)
                for half, (pjn, pj, ev_mm) in enumerate(proj(Wk_bf)):
                    sl = slice(half * 512, half * 512 + 512)
                    ev_e = bld.op("scalar", "activation", out=E_tm[p][:, sl], in_=pj[:],
                                  func=AF.Exp, waits=[ev_mm, bld.wait_free(f"E_tm{p}")], inc=True)
                    bld.set_free(pjn, ev_e)
                    bld.set_prod(f"E_tm{p}_h{half}", ev_e)
                for half, (pjn, pj, ev_mm) in enumerate(proj(Wv_bf)):
                    sl = slice(half * 512, half * 512 + 512)
                    ev_v2 = bld.op("vector", "tensor_copy", out=v_tm[p][:, sl], in_=pj[:],
                                   waits=[ev_mm, bld.wait_free(f"v_tm{p}")], inc=True)
                    bld.set_free(pjn, ev_v2)
                    bld.set_prod(f"v_tm{p}_h{half}", ev_v2)
                pjn, pj = PJ[0]
                ev_mm = None
                for c in range(8):
                    ev_mm = bld.op("tensor", "matmul", out=pj[0:16, 0:128],
                                   lhsT=Wg1_bf[:, c * 16:(c + 1) * 16],
                                   rhs=x1f[p][:, c * 128:(c + 1) * 128],
                                   start=(c == 0), stop=(c == 7),
                                   waits=[bld.wait_free(pjn)] if c == 0 else [],
                                   inc=(c == 7))
                ev_t1 = bld.op("vector", "tensor_copy", out=t1f[:, i * 128:(i + 1) * 128],
                               in_=pj[0:16, 0:128], waits=[ev_mm], inc=True)
                bld.set_free(pjn, ev_t1)
                bld.set_prod(f"t1_{i}", ev_t1)
                bld.set_free(f"x1f{p}_h0", ev_mm)
                bld.set_free(f"x1f{p}_h1", ev_mm)
                ev_lnps = []
                for half in range(2):
                    pjn, pj = PJ[half]
                    sl = slice(half * 512, half * 512 + 512)
                    bld.op("tensor", "matmul", out=pj[:], lhsT=tri[:],
                           rhs=E_tm[p][:, sl], start=True, stop=False,
                           waits=[bld.get_prod(f"E_tm{p}_h{half}"), bld.get_prod(f"carry_h{half}"),
                                  bld.wait_free(pjn)], inc=False)
                    ev_p2 = bld.op("tensor", "matmul", out=pj[:], lhsT=onesrow[:],
                                   rhs=carry[:, sl], start=False, stop=True, inc=True)
                    ev_lnp = bld.op("scalar", "activation", out=sig[p][:, sl], in_=pj[:],
                                    func=AF.Ln, bias=1.0,
                                    waits=[ev_p2, bld.wait_free(f"sig{p}")], inc=True)
                    ev_sig = bld.op("scalar", "activation", out=sig[p][:, sl], in_=sig[p][:, sl],
                                    func=AF.Exp, scale=-1.0, waits=[ev_lnp], inc=True)
                    bld.set_prod(f"sig{p}_h{half}", ev_sig)
                    ev_lnps.append(ev_lnp)
                for half in range(2):
                    pjn, pj = PJ[half]
                    sl = slice(half * 512, half * 512 + 512)
                    ev_tt = bld.op("tensor", "matmul", out=pj[0:1, 0:512], lhsT=onescol[:],
                                   rhs=E_tm[p][:, sl], start=True, stop=True,
                                   waits=[ev_lnps[half]], inc=True)
                    ev_cu = bld.op("vector", "tensor_add", out=carry[:, sl], in0=carry[:, sl],
                                   in1=pj[0:1, 0:512], waits=[ev_tt], inc=True)
                    bld.set_free(pjn, ev_cu)
                    bld.set_prod(f"carry_h{half}", ev_cu)
            def tile_lrpe(i):
                p = i % 2
                ev_gs = bld.op("vector", "tensor_mul", out=escr[p][:], in0=escr[p][:], in1=sig[p][:],
                               waits=[bld.get_prod(f"gq{p}_h0"), bld.get_prod(f"gq{p}_h1"),
                                      bld.get_prod(f"sig{p}_h0"), bld.get_prod(f"sig{p}_h1")],
                               inc=True)
                bld.set_free(f"sig{p}", ev_gs)
                ev_tq = bld.op("vector", "tensor_mul", out=qraw[p][:], in0=qraw[p][:], in1=escr[p][:],
                               waits=[ev_gs], inc=True)
                bld.set_free(f"escr{p}", ev_tq)
                ct = cosim[:, i * HD:(i + 1) * HD]
                stt_ = sinim[:, i * HD:(i + 1) * HD]
                ct_b = ct.unsqueeze(1).broadcast_to((128, 16, 64))
                st_b = stt_.unsqueeze(1).broadcast_to((128, 16, 64))
                tq3 = qraw[p][:].rearrange("p (h d) -> p h d", d=64)
                E3 = E_tm[p][:].rearrange("p (h d) -> p h d", d=64)
                qh4 = qh_tm[p][:].rearrange("p (h t d) -> p h t d", h=16, t=2)
                eh4 = eh_tm[p][:].rearrange("p (h t d) -> p h t d", h=16, t=2)
                bld.op("vector", "tensor_mul", out=qh4[:, :, 0, :], in0=tq3, in1=ct_b,
                       waits=[ev_tq, ev_tabs, bld.wait_free(f"qh_tm{p}")], inc=True)
                ev_q2 = bld.op("vector", "tensor_mul", out=qh4[:, :, 1, :], in0=tq3, in1=st_b,
                               inc=True)
                bld.set_free(f"qraw{p}", ev_q2)
                ev_e1 = bld.op("gpsimd", "tensor_mul", out=eh4[:, :, 0, :], in0=E3, in1=ct_b,
                               waits=[bld.get_prod(f"E_tm{p}_h0"), bld.get_prod(f"E_tm{p}_h1"),
                                      ev_tabs, bld.wait_free(f"eh_tm{p}")], inc="O")
                ev_e2 = bld.op("gpsimd", "tensor_mul", out=eh4[:, :, 1, :], in0=E3, in1=st_b,
                               inc="O")
                bld.set_free(f"E_tm{p}", ev_e2)
                bld.set_prod(f"qh_tm{p}", ev_q2)
                bld.set_prod(f"eh_tm{p}", ev_e2)

            def tile_qe_tp(i):
                p = i % 2
                ev_q2 = bld.get_prod(f"qh_tm{p}")
                ev_e2 = bld.get_prod(f"eh_tm{p}")
                for (src, sname, dst, dname, pev) in (
                        (qh_tm, f"qh_tm{p}", qh_fm, f"qh_fm{p}", ev_q2),
                        (eh_tm, f"eh_tm{p}", eh_fm, f"eh_fm{p}", ev_e2)):
                    ev_tp_last = None
                    for grp in range(4):
                        tpn, tp = TP[grp % 2]
                        tpv = tp[:].bitcast(BF16)
                        ev_tp = None
                        for c in range(4):
                            h = grp * 4 + c
                            ev_tp = bld.op("tensor", "transpose",
                                           out=tpv[:, c * 128:(c + 1) * 128],
                                           in_=src[p][:, h * 128:(h + 1) * 128],
                                           identity=identb[:],
                                           waits=[pev, bld.wait_free(tpn)] if c == 0 else [],
                                           inc=(c == 3))
                        ev_tp_last = ev_tp
                        eng_ev = "vector" if grp % 2 == 0 else "scalar"
                        ev_ev = bld.op(eng_ev, "tensor_copy" if grp % 2 == 0 else "copy",
                                       **({"out": dst[p][:, grp * 512:(grp + 1) * 512],
                                           "in_": tpv[:, 0:512]}),
                                       waits=[ev_tp, bld.wait_free(dname)] if grp == 0 else [ev_tp],
                                       inc=True)
                        bld.set_free(tpn, ev_ev)
                        bld.set_prod(f"{dname}_g{grp}", ev_ev)
                    bld.set_prod(dname, bld.get_prod(f"{dname}_g3"))
                    if sname.startswith("qh_tm"):
                        bld.set_free(sname, ev_tp_last)

            def tile_gla(i):
                p = i % 2
                if i > 0:
                    ev_sbf = bld.op("vector", "tensor_copy", out=S_bf[:], in_=S_ps[:],
                                    waits=[bld.get_prod("S_upd"), bld.wait_free("S_bf")],
                                    inc=True)
                    bld.set_prod("S_bf", ev_sbf)
                    bld.set_free("S_ps_rd", ev_sbf)
                for half in range(2):
                    pjn, pj = PJ[half]
                    sl = slice(half * 512, half * 512 + 512)
                    ev_mm = bld.op("tensor", "matmul", out=pj[:],
                                   lhsT=t1f[:, i * 128:(i + 1) * 128],
                                   rhs=Wg2_bf[:, half * 512:half * 512 + 512],
                                   start=True, stop=True,
                                   waits=[bld.get_prod(f"t1_{i}"), bld.wait_free(pjn)],
                                   inc=True)
                    ev_e = bld.op("scalar", "activation", out=escr[p][:, sl], in_=pj[:],
                                  func=AF.Exp, scale=-1.0,
                                  waits=[ev_mm, bld.wait_free(f"escr{p}")], inc=True)
                    bld.set_free(pjn, ev_e)
                    ev_l = bld.op("scalar", "activation", out=escr[p][:, sl], in_=escr[p][:, sl],
                                  func=AF.Ln, bias=1.0, waits=[ev_e], inc=True)
                    ev_g = bld.op("scalar", "activation", out=escr[p][:, sl], in_=escr[p][:, sl],
                                  func=AF.Exp, scale=-1.0, waits=[ev_l], inc=True)
                    bld.set_prod(f"gate{p}_h{half}", ev_g)
                def emit_A(h):
                    qsl = slice(h * 128, (h + 1) * 128)
                    pa = h % 2
                    tpn, tp = TP[pa]
                    ev_a = bld.op("tensor", "matmul", out=tp[:, 0:128],
                                  lhsT=eh_fm[p][:, qsl], rhs=qh_fm[p][:, qsl],
                                  start=True, stop=True,
                                  waits=[bld.get_prod(f"eh_fm{p}_g{h // 4}"),
                                         bld.get_prod(f"qh_fm{p}_g{h // 4}"),
                                         bld.wait_free(tpn)],
                                  inc=True)
                    ev_am = bld.op("vector", "tensor_mul", out=A_sb[pa][:], in0=tp[:, 0:128],
                                   in1=tri[:], waits=[ev_a, bld.wait_free(f"A_sb{pa}")],
                                   inc=True)
                    bld.set_free(tpn, ev_am)
                    bld.set_prod(f"am_{h}", ev_am)

                def emit_intra_S(h):
                    hb = h // 8
                    hsl = slice(h * 64, (h + 1) * 64)
                    qsl = slice(h * 128, (h + 1) * 128)
                    pa = h % 2
                    if i > 0:
                        bld.op("tensor", "matmul", out=o_ps[:, hsl],
                               lhsT=qh_fm[p][:, qsl], rhs=S_bf[:, hsl],
                               start=True, stop=True,
                               waits=[bld.get_prod("S_bf"),
                                      bld.get_prod(f"qh_fm{p}_g{h // 4}"),
                                      bld.wait_free(f"o_ps{hb}")],
                               inc=False)
                    bld.op("tensor", "matmul", out=o_ps[:, hsl],
                           lhsT=A_sb[pa][:], rhs=v_tm[p][:, hsl],
                           start=(i == 0), stop=True,
                           waits=[bld.get_prod(f"am_{h}"), bld.get_prod(f"v_tm{p}_h{h // 8}"),
                                  bld.wait_free(f"o_ps{hb}") if i == 0 else None],
                           inc=False)
                    ev_mark = bld.op("tensor", "matmul", out=S_ps[:, hsl],
                                     lhsT=eh_tm[p][:, qsl], rhs=v_tm[p][:, hsl],
                                     start=False, stop=True,
                                     waits=[bld.get_prod(f"eh_tm{p}"),
                                            bld.get_prod("S_ps_init") if i == 0 and h == 0 else None,
                                            bld.wait_free("S_ps_rd") if h == 0 else None],
                                     inc=True)
                    bld.set_free(f"A_sb{pa}", ev_mark)
                    if h == 15:
                        bld.set_prod("S_upd", ev_mark)
                        bld.set_prod(f"o_done_{i}", ev_mark)
                        bld.set_free("S_bf", ev_mark)
                        bld.set_free(f"qh_fm{p}", ev_mark)
                        bld.set_free(f"eh_fm{p}", ev_mark)
                        bld.set_free(f"v_tm{p}", ev_mark)
                        bld.set_free(f"eh_tm{p}", ev_mark)

                emit_A(0)
                for h in range(16):
                    if h + 1 < 16:
                        emit_A(h + 1)
                    emit_intra_S(h)
            def tile_og(i):
                p = i % 2
                ev_od = bld.get_prod(f"o_done_{i}")
                for half in range(2):
                    sl = slice(half * 512, half * 512 + 512)
                    ev_og2 = bld.op("vector", "tensor_mul", out=og[:, sl],
                                    in0=o_ps[:, sl], in1=escr[p][:, sl],
                                    waits=[ev_od, bld.get_prod(f"gate{p}_h{half}"),
                                           bld.wait_free("og")] if half == 0 else [
                                           bld.get_prod(f"gate{p}_h1")], inc=True)
                bld.set_free(f"escr{p}", ev_og2)
                bld.set_free("o_ps0", ev_og2)
                bld.set_free("o_ps1", ev_og2)
                ev_oss = bld.op("scalar", "activation", out=sqscr[:], in_=og[:],
                                func=AF.Square, accum_out=st["oss"][p][:],
                                waits=[ev_og2, bld.wait_free("sqscr")], inc=True)
                bld.set_free("sqscr", ev_oss)
                ev_lno = bld.op("scalar", "activation", out=st["oss"][p][:], in_=st["oss"][p][:],
                                func=AF.Ln, scale=1.0 / D, bias=EPS_RMS, waits=[ev_oss], inc=True)
                ev_ro = bld.op("scalar", "activation", out=st["orr"][p][:], in_=st["oss"][p][:],
                               func=AF.Exp, scale=-0.5, waits=[ev_lno], inc=True)
                ev_ogn = bld.op("scalar", "activation", out=ogn[p][:], in_=og[:],
                                func=AF.Copy, scale=st["orr"][p][:],
                                waits=[ev_ro, bld.wait_free(f"ogn{p}")], inc=True)
                bld.set_free("og", ev_ogn)
                ev_st2 = bld.op("gpsimd", "dma_start", out=og_dram[i * 128:(i + 1) * 128, :],
                                in_=ogn[p][:], waits=[ev_ogn], inc=f"og{p}", inc_n=16)
                bld.set_free(f"ogn{p}", ev_st2)
                bld.set_prod(f"og_store_{i}", ev_st2)
                og_counts[i] = (bld.ctr["og0"], bld.ctr["og1"])

            og_counts = {}
            tile_pre1(0)
            tile_pre1(1)
            tile_x1tp(0)
            tile_x1tp(1)
            tile_pre2(0)
            tile_lrpe(0)
            for i in range(NT):
                if i + 2 < NT:
                    tile_pre1(i + 2)
                if i + 1 < NT:
                    tile_pre2(i + 1)
                if i + 2 < NT:
                    tile_x1tp(i + 2)
                tile_qe_tp(i)
                tile_gla(i)
                if i + 1 < NT:
                    tile_lrpe(i + 1)
                tile_og(i)

            ev_last_sbf = bld.get_prod("S_bf")
            ev_last_og = bld.get_prod("og_h1")
            bld.set_free("Sps_nt0", ev_last_sbf)
            bld.set_free("Sps_nt1", ev_last_sbf)
            bld.set_free("ops_nt0", ev_last_og)
            bld.set_free("ops_nt1", ev_last_og)
            bld.og_counts = og_counts
            phaseA_done = [("P", bld.ctr["P"]), ("V", bld.ctr["V"]),
                           ("A", bld.ctr["A"]),
                           ("og0", bld.ctr["og0"]), ("og1", bld.ctr["og1"]),
                           ("x0", bld.ctr["x0"]), ("x1", bld.ctr["x1"])]

        bld.new_era()
        with ExitStack() as bctx:
            sbb = lambda name, shape, dtype=F32: bld.tag(bctx.enter_context(nc.sbuf_tensor(name, shape, dtype)))
            Wo_bf = sbb("Wo_bf", [128, 8 * 1024], BF16)
            ogf = [sbb(f"ogf{p}", [128, 1024], BF16) for p in range(2)]
            xo_t = [sbb(f"xo_t{p}", [128, 1024]) for p in range(2)]
            xm_t = [sbb(f"xm_t{p}", [128, 1024]) for p in range(2)]
            yraw = [sbb(f"yraw{p}", [128, 1024]) for p in range(2)]
            x2t = sbb("x2t", [128, 1024], BF16)
            yln = sbb("yln", [128, 1024])
            sqb = sbb("sqb", [128, 1024], BF16)
            bs = {nm: [sbb(f"bs_{nm}_{p}", [128, 1]) for p in range(2)]
                  for nm in ("s1", "sq", "sqB", "m", "t0", "v", "r", "nmr")}
            hbuf = sbb("hbuf", [128, 32 * 1024], BF16)
            outt = [sbb(f"outt{p}", [128, 1024]) for p in range(2)]
            ev_ofsl = ev_consts

            with ExitStack() as wctx:
                wldb = [bld.tag(wctx.enter_context(nc.sbuf_tensor(f"wldb{q}", [128, 1024], F32)))
                        for q in range(2)]
                for c in range(8):
                    q = c % 2
                    evl = bld.op("sync", "dma_start", out=wldb[q][:],
                                 in_=Wo_h[c * 128:(c + 1) * 128, :],
                                 waits=[bld.wait_free(f"wldb{q}")], inc=f"wo{q}", inc_n=16)
                    evc = bld.op("vector", "tensor_scalar_mul",
                                 out=Wo_bf[:, c * 1024:(c + 1) * 1024], in0=wldb[q][:],
                                 scalar1=nwc[:, c:c + 1],
                                 waits=[evl, ev_consts] + (phaseA_done if c == 0 else []),
                                 inc=True)
                    bld.set_free(f"wldb{q}", evc)
            bld.new_era()
            ev_wo = ("V", bld.ctr["V"])

            regcell = {}

            def setup_reg(engine, post):
                reg = engine.alloc_register("hofs")
                engine.reg_load(reg, ofs_sb[0:1, 0:1])
                regcell["ofs"] = engine.snap(reg)
                inst = engine.nop()
                post(inst)

            ev_reg = bld.raw("gpsimd", setup_reg, waits=[ev_ofsl] + phaseA_done, inc="O")

            for hf in range(2):
                bld.new_era()
                bhctx = ExitStack()
                x2f = bld.tag(bhctx.enter_context(nc.sbuf_tensor(f"x2f{hf}", [128, 8 * 1024], BF16)))
                with ExitStack() as hctx:
                    og_sb = bld.tag(hctx.enter_context(nc.sbuf_tensor(f"og_sb{hf}", [128, 8 * 1024], BF16)))

                    def og_load(engine, post, hf=hf):
                        ofs = regcell["ofs"]
                        src = og_dram[bass.ds(ofs, HALF), :][hf * 1024: hf * 1024 + 1024, :]
                        src = src.rearrange("(t p) m -> p t m", p=128)
                        inst = engine.dma_start(
                            out=og_sb[:].rearrange("p (t m) -> p t m", t=8), in_=src)
                        post(inst)

                    ev_ogl = bld.raw("gpsimd", og_load,
                                     waits=[ev_reg,
                                            ("og0", bld.ctr["og0"]), ("og1", bld.ctr["og1"]),
                                            bld.wait_free("og_sb")],
                                     inc="ogld", inc_n=16, out_name=og_sb.name)
                    bld.set_prod("og_sb", ev_ogl)

                    def B_pe(t):
                        p = t % 2
                        gtok = hf * 1024 + t * 128
                        evx = bld.op("sync", "dma_start", out=xo_t[p][:],
                                     in_=x_own[gtok:gtok + 128, :],
                                     waits=[bld.wait_free(f"xo_t{p}")], inc=f"xo{p}", inc_n=16)
                        bld.set_prod(f"xo_{t}", evx)
                        for half in range(2):
                            tpn, tp = TP[half]
                            tpv = tp[:].bitcast(BF16)
                            ev_tp = None
                            for c in range(4):
                                cc = half * 4 + c
                                ev_tp = bld.op("tensor", "transpose",
                                               out=tpv[:, c * 128:(c + 1) * 128],
                                               in_=og_sb[:, t * 1024 + cc * 128: t * 1024 + cc * 128 + 128],
                                               identity=identb[:],
                                               waits=[ev_ogl, ev_wo, bld.wait_free(tpn)] if c == 0 else [],
                                               inc=(c == 3))
                            ev_ev = bld.op("vector", "tensor_copy",
                                           out=ogf[p][:, half * 512:(half + 1) * 512],
                                           in_=tpv[:, 0:512],
                                           waits=[ev_tp, bld.wait_free(f"ogf{p}")], inc=True)
                            bld.set_free(tpn, ev_ev)
                            bld.set_prod(f"ogf{p}_h{half}", ev_ev)
                        for half in range(2):
                            pjn, pj = PJ[half]
                            ev_mm = None
                            for c in range(8):
                                ev_mm = bld.op("tensor", "matmul", out=pj[:],
                                               lhsT=ogf[p][:, c * 128:(c + 1) * 128],
                                               rhs=Wo_bf[:, c * 1024 + half * 512: c * 1024 + half * 512 + 512],
                                               start=(c == 0), stop=(c == 7),
                                               waits=[bld.get_prod(f"ogf{p}_h{c // 4}"),
                                                      bld.wait_free(pjn)] if c in (0, 4) else [],
                                               inc=(c == 7))
                            ev_yev = bld.op("vector", "tensor_copy",
                                            out=yraw[p][:, half * 512:(half + 1) * 512], in_=pj[:],
                                            waits=[ev_mm, bld.wait_free(f"yraw{p}")], inc=True)
                            bld.set_free(pjn, ev_yev)
                            bld.set_prod(f"yraw_{t}_h{half}", ev_yev)
                        bld.set_free(f"ogf{p}", bld.get_prod("y_ps_h1") if False else ev_mm)

                    def B_ln(t):
                        p = t % 2
                        gtok = hf * 1024 + t * 128
                        evy = [bld.get_prod(f"yraw_{t}_h0"), bld.get_prod(f"yraw_{t}_h1")]
                        ev_sa = bld.op("vector", "reduce_sum", out=bs["s1"][p][:], in_=yraw[p][:],
                                       axis=AXL.X, waits=evy + [bld.wait_free(f"bstats{p}")], inc=True)
                        ev_qa = bld.op("scalar", "activation", out=sqb[:], in_=yraw[p][:],
                                       func=AF.Square, accum_out=bs["sq"][p][:],
                                       waits=evy + [bld.wait_free("sqb")], inc=True)
                        bld.set_free("sqb", ev_qa)
                        ev_m = bld.op("vector", "tensor_scalar_mul", out=bs["m"][p][:],
                                      in0=bs["s1"][p][:], scalar1=1.0 / D, waits=[ev_sa], inc=True)
                        ev_t0 = bld.op("vector", "tensor_mul", out=bs["t0"][p][:], in0=bs["m"][p][:],
                                       in1=bs["m"][p][:], waits=[ev_m], inc=True)
                        ev_v = bld.op("vector", "scalar_tensor_tensor", out=bs["v"][p][:],
                                      in0=bs["sq"][p][:], scalar=1.0 / D, in1=bs["t0"][p][:],
                                      op0=ALU.mult, op1=ALU.subtract, waits=[ev_qa, ev_t0], inc=True)
                        ev_ln = bld.op("scalar", "activation", out=bs["v"][p][:], in_=bs["v"][p][:],
                                       func=AF.Ln, bias=EPS_LN, waits=[ev_v], inc=True)
                        ev_r = bld.op("scalar", "activation", out=bs["r"][p][:], in_=bs["v"][p][:],
                                      func=AF.Exp, scale=-0.5, waits=[ev_ln], inc=True)
                        ev_yl = bld.op("vector", "tensor_scalar", out=yln[:], in0=yraw[p][:],
                                       scalar1=bs["m"][p][:], scalar2=bs["r"][p][:],
                                       op0=ALU.subtract, op1=ALU.mult,
                                       waits=[ev_r, bld.wait_free("yln")], inc=True)
                        ev_xm = bld.op("vector", "tensor_add", out=xm_t[p][:],
                                       in0=yln[:], in1=xo_t[p][:],
                                       waits=[ev_yl, bld.get_prod(f"xo_{t}"),
                                              bld.wait_free(f"xm_t{p}"),
                                              bld.wait_free(f"xm_tst{p}")], inc=True)
                        bld.set_free(f"yraw{p}", ev_yl)
                        bld.set_free(f"xo_t{p}", ev_xm)
                        bld.set_free("yln", ev_xm)
                        ev_xst = bld.op("gpsimd", "dma_start", out=xm_dram[gtok:gtok + 128, :],
                                        in_=xm_t[p][:], waits=[ev_xm], inc=f"xs{p}", inc_n=16)
                        bld.set_prod(f"xm_store_{t}", ev_xst)
                        bld.set_free(f"xm_tst{p}", ev_xst)
                        ev_s3 = bld.op("vector", "reduce_sum", out=bs["s1"][p][:], in_=xm_t[p][:],
                                       axis=AXL.X, waits=[ev_xm], inc=True)
                        ev_q3 = bld.op("scalar", "activation", out=sqb[:], in_=xm_t[p][:],
                                       func=AF.Square, accum_out=bs["sq"][p][:],
                                       waits=[ev_xm, bld.wait_free("sqb")], inc=True)
                        bld.set_free("sqb", ev_q3)
                        ev_m3 = bld.op("vector", "tensor_scalar_mul", out=bs["m"][p][:],
                                       in0=bs["s1"][p][:], scalar1=1.0 / D, waits=[ev_s3], inc=True)
                        ev_t3 = bld.op("vector", "tensor_mul", out=bs["t0"][p][:], in0=bs["m"][p][:],
                                       in1=bs["m"][p][:], waits=[ev_m3], inc=True)
                        ev_v3 = bld.op("vector", "scalar_tensor_tensor", out=bs["v"][p][:],
                                       in0=bs["sq"][p][:], scalar=1.0 / D, in1=bs["t0"][p][:],
                                       op0=ALU.mult, op1=ALU.subtract, waits=[ev_q3, ev_t3], inc=True)
                        ev_l3 = bld.op("scalar", "activation", out=bs["v"][p][:], in_=bs["v"][p][:],
                                       func=AF.Ln, bias=EPS_LN, waits=[ev_v3], inc=True)
                        ev_r3 = bld.op("scalar", "activation", out=bs["r"][p][:], in_=bs["v"][p][:],
                                       func=AF.Exp, scale=-0.5, waits=[ev_l3], inc=True)
                        ev_x2 = bld.op("vector", "tensor_scalar", out=x2t[:], in0=xm_t[p][:],
                                       scalar1=bs["m"][p][:], scalar2=bs["r"][p][:],
                                       op0=ALU.subtract, op1=ALU.mult,
                                       waits=[ev_r3, bld.wait_free("x2t")], inc=True)
                        bld.set_free(f"bstats{p}", ev_x2)
                        bld.set_free(f"xm_t{p}", ev_x2)
                        bld.set_prod(f"x2t_{t}", ev_x2)

                    def B_x2tp(t):
                        ev_x2 = bld.get_prod(f"x2t_{t}")
                        ev_e2 = None
                        for half in range(2):
                            tpn, tp = TP[half]
                            tpv = tp[:].bitcast(BF16)
                            ev_tp = None
                            for c in range(4):
                                cc = half * 4 + c
                                ev_tp = bld.op("tensor", "transpose",
                                               out=tpv[:, c * 128:(c + 1) * 128],
                                               in_=x2t[:, cc * 128:(cc + 1) * 128],
                                               identity=identb[:],
                                               waits=[ev_x2, bld.wait_free(tpn)] if c == 0 else [],
                                               inc=(c == 3))
                            for c in range(4):
                                cc = half * 4 + c
                                ev_e2 = bld.op("vector", "tensor_copy",
                                               out=x2f[:, cc * 1024 + t * 128: cc * 1024 + t * 128 + 128],
                                               in_=tpv[:, c * 128:(c + 1) * 128],
                                               waits=[ev_tp, bld.wait_free("x2f")] if c == 0 else [ev_tp],
                                               inc=(c == 3))
                            bld.set_free(tpn, ev_e2)
                        bld.set_free("x2t", ev_e2)
                        bld.set_prod(f"x2f_t{t}", ev_e2)

                    B_pe(0)
                    for t in range(8):
                        if t + 1 < 8:
                            B_pe(t + 1)
                        B_ln(t)
                        B_x2tp(t)
                    ev_x2f_all = bld.get_prod("x2f_t7")
                    bld.set_free("og_sb", ev_x2f_all)

                bld.new_era()
                ffn_barrier = bld.totals()
                with ExitStack() as c1ctx:
                    wfA = [bld.tag(c1ctx.enter_context(nc.sbuf_tensor(f"wfA{hf}_{q}", [128, 2048], F32)))
                           for q in range(2)]
                    wbA = [bld.tag(c1ctx.enter_context(nc.sbuf_tensor(f"wbA{hf}_{q}", [128, 1024], BF16)))
                           for q in range(2)]
                    wbB = [bld.tag(c1ctx.enter_context(nc.sbuf_tensor(f"wbB{hf}_{q}", [128, 1024], BF16)))
                           for q in range(2)]
                    sA = [bld.tag(c1ctx.enter_context(nc.sbuf_tensor(f"sA{hf}_{q}", [128, 512], F32)))
                          for q in range(2)]
                    for oc in range(32):
                        q = oc % 2
                        evl1 = bld.op("sync", "dma_start", out=wfA[q][:, 0:1024],
                                      in_=w1i_h[oc],
                                      waits=(ffn_barrier if oc < 2 else []) +
                                            [bld.wait_free(f"wfA{q}")], inc=f"wfA{q}", inc_n=16)
                        evl2 = bld.op("sync", "dma_start", out=wfA[q][:, 1024:2048],
                                      in_=w2i_h[oc], inc=f"wfA{q}", inc_n=16)
                        evc1 = bld.op("vector", "tensor_copy", out=wbA[q][:], in_=wfA[q][:, 0:1024],
                                      waits=[evl2, bld.wait_free(f"wbA{q}")], inc=True)
                        evc2 = bld.op("vector", "tensor_copy", out=wbB[q][:], in_=wfA[q][:, 1024:2048],
                                      waits=[evl2, bld.wait_free(f"wbB{q}")], inc=True)
                        bld.set_free(f"wfA{q}", evc2)
                        for nt in range(2):
                            psa = S_ps[:, nt * 512: nt * 512 + 512]
                            psb = o_ps[:, nt * 512: nt * 512 + 512]
                            ev_x2f_nt = ev_x2f_all
                            ev_ma = None
                            for kc in range(8):
                                ev_ma = bld.op("tensor", "matmul", out=psa,
                                               lhsT=wbA[q][:, kc * 128:(kc + 1) * 128],
                                               rhs=x2f[:, kc * 1024 + nt * 512: kc * 1024 + nt * 512 + 512],
                                               start=(kc == 0), stop=(kc == 7),
                                               waits=[evc1, ev_x2f_nt,
                                                      bld.wait_free(f"Sps_nt{nt}")] if kc == 0 else [],
                                               inc=(kc == 7))
                            ev_mb = None
                            for kc in range(8):
                                ev_mb = bld.op("tensor", "matmul", out=psb,
                                               lhsT=wbB[q][:, kc * 128:(kc + 1) * 128],
                                               rhs=x2f[:, kc * 1024 + nt * 512: kc * 1024 + nt * 512 + 512],
                                               start=(kc == 0), stop=(kc == 7),
                                               waits=[evc2, ev_x2f_nt,
                                                      bld.wait_free(f"ops_nt{nt}")] if kc == 0 else [],
                                               inc=(kc == 7))
                            if nt == 1:
                                bld.set_free(f"wbA{q}", ev_ma)
                                bld.set_free(f"wbB{q}", ev_mb)
                            ev_sl = bld.op("scalar", "activation", out=sA[nt][:], in_=psa,
                                           func=AF.Silu, scale=s1c[:, oc:oc + 1],
                                           waits=[ev_ma, bld.wait_free(f"sA{nt}")], inc=True)
                            bld.set_free(f"Sps_nt{nt}", ev_sl)
                            ev_hv = bld.op("vector", "scalar_tensor_tensor",
                                           out=hbuf[:, oc * 1024 + nt * 512: oc * 1024 + nt * 512 + 512],
                                           in0=psb, scalar=s2c[:, oc:oc + 1], in1=sA[nt][:],
                                           op0=ALU.mult, op1=ALU.mult,
                                           waits=[ev_mb, ev_sl, bld.wait_free("hbuf")], inc=True)
                            bld.set_free(f"ops_nt{nt}", ev_hv)
                            bld.set_free(f"sA{nt}", ev_hv)
                            bld.set_prod(f"h_oc{oc}", ev_hv)
                    ev_h_all = bld.get_prod("h_oc31")
                    bld.set_free("x2f", bld.get_prod("h_oc31"))

                bhctx.close()
                bld.new_era()
                c2_barrier = bld.totals()
                with ExitStack() as c2ctx:
                    hwf = [bld.tag(c2ctx.enter_context(nc.sbuf_tensor(f"hwf{hf}_{q}", [128, 2048], F32)))
                           for q in range(2)]
                    hwb = [bld.tag(c2ctx.enter_context(nc.sbuf_tensor(f"hwb{hf}_{q}", [128, 4096], BF16)))
                           for q in range(2)]
                    hp_t = [bld.tag(c2ctx.enter_context(nc.sbuf_tensor(f"hp_t{hf}_{q}", [128, 512], BF16)))
                            for q in range(2)]

                    def hlayer(img_h, scol, dst_dram, src_prod_ev, first_barrier, bg=None):
                        oc_counts = []
                        for oc in range(32):
                            if bg is not None:
                                bg(oc)
                            q = oc % 2
                            evl1 = bld.op("sync", "dma_start", out=hwf[q][:],
                                          in_=img_h[oc][:, 0:2048],
                                          waits=(first_barrier if oc < 2 else []) +
                                                [bld.wait_free(f"hwf{q}")], inc=f"hw{q}", inc_n=16)
                            evc1 = bld.op("vector", "tensor_copy", out=hwb[q][:, 0:2048],
                                          in_=hwf[q][:],
                                          waits=[evl1, bld.wait_free(f"hwb{q}")], inc=True)
                            evl2 = bld.op("sync", "dma_start", out=hwf[q][:],
                                          in_=img_h[oc][:, 2048:4096],
                                          waits=[evc1], inc=f"hw{q}", inc_n=16)
                            evc2 = bld.op("vector", "tensor_copy", out=hwb[q][:, 2048:4096],
                                          in_=hwf[q][:], waits=[evl2], inc=True)
                            bld.set_free(f"hwf{q}", evc2)
                            for nt in range(2):
                                psx = S_ps[:, nt * 512: nt * 512 + 512]
                                ev_mm = None
                                for kc in range(32):
                                    ev_mm = bld.op("tensor", "matmul", out=psx,
                                                   lhsT=hwb[q][:, kc * 128:(kc + 1) * 128],
                                                   rhs=hbuf[:, kc * 1024 + nt * 512: kc * 1024 + nt * 512 + 512],
                                                   start=(kc == 0), stop=(kc == 31),
                                                   waits=[evc1, evc2, src_prod_ev,
                                                          bld.wait_free(f"Sps_nt{nt}")] if kc == 0 else [],
                                                   inc=(kc == 31))
                                if nt == 1:
                                    bld.set_free(f"hwb{q}", ev_mm)
                                pp = nt
                                ev_g = bld.op("scalar", "activation", out=hp_t[pp][:], in_=psx,
                                              func=AF.Gelu, scale=scol[:, oc:oc + 1],
                                              waits=[ev_mm, bld.wait_free(f"hp_t{pp}")], inc=True)
                                bld.set_free(f"Sps_nt{nt}", ev_g)
                                ev_hs = bld.op("gpsimd", "dma_start",
                                               out=dst_dram[:, (oc * 2 + nt) * 512: (oc * 2 + nt) * 512 + 512],
                                               in_=hp_t[pp][:], waits=[ev_g], inc=f"hp{pp}", inc_n=16)
                                bld.set_free(f"hp_t{pp}", ev_hs)
                            oc_counts.append((bld.ctr["hp0"], bld.ctr["hp1"]))
                        return [("hp0", bld.ctr["hp0"]), ("hp1", bld.ctr["hp1"])], oc_counts

                    ev_hp, h1_counts = hlayer(h1i_h, h1c, hp_dram, ev_h_all, c2_barrier)
                    ev_rl = bld.op("gpsimd", "dma_start", out=hbuf[:], in_=hp_dram[:],
                                   waits=ev_hp + [ev_h_all, bld.wait_free("hbuf")],
                                   inc="hbr", inc_n=16)
                    bld.new_era()
                    c6ctx = ExitStack()
                    w3ld = [bld.tag(c6ctx.enter_context(nc.sbuf_tensor(f"w3ld{hf}_{q}", [128, 512], F32, side="right")))
                            for q in range(2)]
                    w3b = bld.tag(c6ctx.enter_context(nc.sbuf_tensor(f"w3b{hf}", [128, 32 * 512], BF16, side="right")))

                    def w3_pair(kc, oh):
                        q = kc % 2
                        evl = bld.op("sync", "dma_start", out=w3ld[q][:],
                                     in_=w3i_h[kc][:, oh * 512: oh * 512 + 512],
                                     waits=[bld.wait_free(f"w3ld{q}")], inc=f"w3l{q}", inc_n=16)
                        evc = bld.op("vector", "tensor_copy",
                                     out=w3b[:, kc * 512:(kc + 1) * 512], in_=w3ld[q][:],
                                     waits=[evl, bld.wait_free("w3b") if kc == 0 else None],
                                     inc=True)
                        bld.set_free(f"w3ld{q}", evc)
                        bld.set_prod(f"w3b_{kc}", evc)

                    def w3_stream(oh):
                        for kc in range(32):
                            w3_pair(kc, oh)

                    ev_hp2, h2_counts = hlayer(h2i_h, h2c, hp2_dram, ev_rl,
                                               [], bg=lambda oc: w3_pair(oc, 0))
                    ev_rl2 = bld.op("gpsimd", "dma_start", out=hbuf[:], in_=hp2_dram[:],
                                    waits=ev_hp2, inc="hbr", inc_n=16)

                bld.new_era()
                if True:
                    y3 = bld.tag(c6ctx.enter_context(nc.sbuf_tensor(f"y3_{hf}", [128, 8 * 1024], F32, side="right")))
                    def D_tile(t):

                        p = t % 2
                        gtok = hf * 1024 + t * 128
                        y3t = y3[:, t * 1024:(t + 1) * 1024]
                        evy = [bld.get_prod(f"y3_t{t}_oh0"), bld.get_prod(f"y3_t{t}_oh1")]
                        ev_xl = bld.op("sync", "dma_start", out=xm_t[p][:],
                                       in_=xm_dram[gtok:gtok + 128, :],
                                       waits=[bld.get_prod(f"xm_store_{t}"),
                                              bld.wait_free(f"xm_t{p}"),
                                              bld.wait_free(f"xm_tst{p}")], inc=f"xmr{p}", inc_n=16)
                        ev_s4 = bld.op("vector", "reduce_sum", out=bs["s1"][p][:], in_=y3t,
                                       axis=AXL.X, waits=evy + [bld.wait_free(f"bstats{p}")], inc=True)
                        ev_q4 = bld.op("scalar", "activation", out=sqb[:], in_=y3t,
                                       func=AF.Square, accum_out=bs["sq"][p][:],
                                       waits=evy + [bld.wait_free("sqb")], inc=True)
                        bld.set_free("sqb", ev_q4)
                        ev_m4 = bld.op("vector", "tensor_scalar_mul", out=bs["m"][p][:],
                                       in0=bs["s1"][p][:], scalar1=1.0 / D, waits=[ev_s4], inc=True)
                        ev_t4 = bld.op("vector", "tensor_mul", out=bs["t0"][p][:], in0=bs["m"][p][:],
                                       in1=bs["m"][p][:], waits=[ev_m4], inc=True)
                        ev_v4 = bld.op("vector", "scalar_tensor_tensor", out=bs["v"][p][:],
                                       in0=bs["sq"][p][:], scalar=1.0 / D, in1=bs["t0"][p][:],
                                       op0=ALU.mult, op1=ALU.subtract, waits=[ev_q4, ev_t4], inc=True)
                        ev_l4 = bld.op("scalar", "activation", out=bs["v"][p][:], in_=bs["v"][p][:],
                                       func=AF.Ln, bias=EPS_LN, waits=[ev_v4], inc=True)
                        ev_r4 = bld.op("scalar", "activation", out=bs["r"][p][:], in_=bs["v"][p][:],
                                       func=AF.Exp, scale=-0.5, waits=[ev_l4], inc=True)
                        ev_n4 = bld.op("vector", "scalar_tensor_tensor", out=bs["nmr"][p][:],
                                       in0=bs["m"][p][:], scalar=-1.0, in1=bs["r"][p][:],
                                       op0=ALU.mult, op1=ALU.mult, waits=[ev_m4, ev_r4], inc=True)
                        bld.set_free(f"bstats{p}", ev_n4)
                        ev_o4 = bld.op("vector", "tensor_scalar", out=outt[p][:], in0=y3t,
                                       scalar1=bs["m"][p][:], scalar2=bs["r"][p][:],
                                       op0=ALU.subtract, op1=ALU.mult,
                                       waits=[ev_r4, bld.wait_free(f"outt{p}")], inc=True)
                        ev_f = bld.op("vector", "tensor_add", out=outt[p][:], in0=outt[p][:],
                                      in1=xm_t[p][:], waits=[ev_o4, ev_xl], inc=True)
                        bld.set_free(f"xm_t{p}", ev_f)
                        if t == 7:
                            bld.set_free("y3", ev_f)
                        ev_os = bld.op("sync", "dma_start", out=out_h[gtok:gtok + 128, :],
                                       in_=outt[p][:], waits=[ev_f], inc=f"os{p}", inc_n=16)
                        bld.set_free(f"outt{p}", ev_os)

                    for oh in range(2):
                        if oh == 1:
                            w3_stream(1)
                        ev_mm_last = None
                        for t in range(8):
                            pjn, pj = PJ[t % 2]
                            ev_mm = None
                            for kc in range(32):
                                ev_mm = bld.op("tensor", "matmul", out=pj[:],
                                               lhsT=hbuf[:, kc * 1024 + t * 128: kc * 1024 + t * 128 + 128],
                                               rhs=w3b[:, kc * 512:(kc + 1) * 512],
                                               start=(kc == 0), stop=(kc == 31),
                                               waits=[ev_rl2, bld.get_prod("w3b_31"),
                                                      bld.wait_free(pjn)] if kc == 0 else [],
                                               inc=(kc == 31))
                            ev_mm_last = ev_mm
                            ev_y3 = bld.op("vector", "tensor_mul",
                                           out=y3[:, t * 1024 + oh * 512: t * 1024 + oh * 512 + 512],
                                           in0=pj[:], in1=s3bc[:, oh * 512: oh * 512 + 512],
                                           waits=[ev_mm, bld.wait_free("y3") if t == 0 and oh == 0 else None],
                                           inc=True)
                            bld.set_free(pjn, ev_y3)
                            bld.set_prod(f"y3_t{t}_oh{oh}", ev_y3)
                            if oh == 1:
                                D_tile(t)
                        bld.set_free("w3b", ev_mm_last)
                        if oh == 1:
                            bld.set_free("hbuf", ev_mm_last)
                    c6ctx.close()

            bld.op("sync", "wait_ge", bld.sem_handles["os0"], bld.ctr["os0"])
            bld.op("sync", "wait_ge", bld.sem_handles["os1"], bld.ctr["os1"])

            with nc.Block() as block:
                @block.sync
                def _(eng):
                    for f in bld.streams["sync"]:
                        f(eng)

                @block.vector
                def _(eng):
                    for f in bld.streams["vector"]:
                        f(eng)

                @block.scalar
                def _(eng):
                    for f in bld.streams["scalar"]:
                        f(eng)

                @block.tensor
                def _(eng):
                    for f in bld.streams["tensor"]:
                        f(eng)

                @block.gpsimd
                def _(eng):
                    for f in bld.streams["gpsimd"]:
                        f(eng)

    return nc


def _host_inputs(inputs):
    x = np.ascontiguousarray(inputs["L0_attn_x"])
    t = np.arange(N, dtype=np.float64)[:, None]
    j = np.arange(HD, dtype=np.float64)[None, :]
    th = t * (10000.0 ** (-2.0 * j / HD))
    cos_t = np.cos(th).astype(np.float32)
    sin_t = np.sin(th).astype(np.float32)
    cosim = np.ascontiguousarray(cos_t.reshape(NT, 128, HD).transpose(1, 0, 2).reshape(128, NT * HD))
    sinim = np.ascontiguousarray(sin_t.reshape(NT, 128, HD).transpose(1, 0, 2).reshape(128, NT * HD))
    common = dict(
        Wq=np.ascontiguousarray(inputs["L0_attn_Wq"]),
        Wk=np.ascontiguousarray(inputs["L0_attn_Wk"]),
        Wv=np.ascontiguousarray(inputs["L0_attn_Wv"]),
        Wo=np.ascontiguousarray(inputs["L0_attn_Wo"]),
        Wg1=np.ascontiguousarray(inputs["L0_attn_Wg1"]),
        Wg2=np.ascontiguousarray(inputs["L0_attn_Wg2"]),
        nwc=np.ascontiguousarray(inputs["L0_attn_norm_w"].reshape(8, 128).T),
        cosim=cosim, sinim=sinim,
        tri=np.triu(np.ones((128, 128), np.float32)),
        identb=np.eye(128, dtype=np.float32).astype(ml_dtypes.bfloat16),
        onesrow=np.ones((1, 128), np.float32),
        onescol=np.ones((128, 1), np.float32),
        zrow=np.zeros((1, 512), np.float32),
        w1i=np.ascontiguousarray(inputs["L0_ffn_w1_w"].T.reshape(8, 128, 32, 128)
                                 .transpose(2, 1, 0, 3).reshape(32, 128, 1024)),
        w2i=np.ascontiguousarray(inputs["L0_ffn_w2_w"].T.reshape(8, 128, 32, 128)
                                 .transpose(2, 1, 0, 3).reshape(32, 128, 1024)),
        h1i=np.ascontiguousarray(inputs["L0_ffn_h1_w"].T.reshape(32, 128, 32, 128)
                                 .transpose(2, 1, 0, 3).reshape(32, 128, 4096)),
        h2i=np.ascontiguousarray(inputs["L0_ffn_h2_w"].T.reshape(32, 128, 32, 128)
                                 .transpose(2, 1, 0, 3).reshape(32, 128, 4096)),
        w3i=np.ascontiguousarray(inputs["L0_ffn_w3_w"].T.reshape(32, 128, 1024)),
        s1c=np.ascontiguousarray(inputs["L0_ffn_w1_s"].reshape(32, 128).T),
        s2c=np.ascontiguousarray(inputs["L0_ffn_w2_s"].reshape(32, 128).T),
        h1c=np.ascontiguousarray(inputs["L0_ffn_h1_s"].reshape(32, 128).T),
        h2c=np.ascontiguousarray(inputs["L0_ffn_h2_s"].reshape(32, 128).T),
        s3bc=np.ascontiguousarray(np.broadcast_to(inputs["L0_ffn_w3_s"][None, :], (128, 1024))),
        cvals=np.ascontiguousarray(np.broadcast_to(np.array(CONST_VALS, np.float32)[None, :],
                                                   (128, len(CONST_VALS)))),
    )
    maps = []
    for core in range(8):
        b, hf = core // 2, core % 2
        m = dict(common)
        m["xb"] = np.ascontiguousarray(x[b])
        m["x_own"] = np.ascontiguousarray(x[b, hf * HALF:(hf + 1) * HALF])
        m["half_ofs"] = np.array([[hf * HALF]], np.int32)
        maps.append(m)
    return maps


def kernel(**inputs):
    nc = build_nc()
    in_maps = _host_inputs(inputs)
    res = run_bass_kernel_spmd(nc, in_maps, core_ids=list(range(8)))
    out = np.empty((B, N, D), np.float32)
    for core in range(8):
        b, hf = core // 2, core % 2
        out[b, hf * HALF:(hf + 1) * HALF] = res.results[core]["out"]
    return out


# version 23
# speedup vs baseline: 1.0535x; 1.0019x over previous
import numpy as np
import ml_dtypes
import concourse.bass as bass
import concourse.mybir as mybir
from concourse.bass_utils import run_bass_kernel_spmd
from contextlib import ExitStack

F32 = mybir.dt.float32
BF16 = mybir.dt.bfloat16
I32 = mybir.dt.int32
AF = mybir.ActivationFunctionType
ALU = mybir.AluOpType
AXL = mybir.AxisListType

B, N, D, H, HD = 4, 4096, 1024, 16, 64
NT = N // 128
HALF = 2048
EPS_LN = 1e-5
EPS_RMS = 1e-6
CONST_VALS = (1e-5, 1.0, 1e-6, 1.0 / 1024.0, -0.5, -1.0)
SAME_ENGINE_WAITS = True

ENGINES = ("sync", "vector", "scalar", "tensor", "gpsimd")


class Builder:
    def __init__(self, nc):
        self.nc = nc
        self.streams = {e: [] for e in ENGINES}
        self.sem_handles = {}
        self.ctr = {}
        self.last_wait = {}
        self.free_ev = {}
        self.prod_ev = {}
        self.engine_sem = {"sync": "L", "vector": "V", "scalar": "A",
                           "tensor": "P", "gpsimd": "G"}

    DMA_SEMS = ("C", "wld0", "wld1", "wld2", "wld3", "x0", "x1", "wo0", "wo1", "wfA0", "wfA1",
                "hw0", "hw1", "w3l0", "w3l1", "xo0", "xo1", "xmr0", "xmr1",
                "hbr", "ogld", "og0", "og1", "hp0", "hp1", "xs0", "xs1",
                "os0", "os1")

    def init_sems(self, ctx):
        for s in ("V", "A", "P", "G", "O") + self.DMA_SEMS:
            self.sem_handles[s] = ctx.enter_context(self.nc.semaphore("sem_" + s))
            self.ctr[s] = 0
        for s in ("L", "W", "X", "S"):
            self.ctr.setdefault(s, 0)
        self.era = 0
        self.era_barriers = {0: []}
        self.engine_era = {}
        self.tensor_era = {}

    def new_era(self):
        self.era += 1
        self.era_barriers[self.era] = [(s, v) for s, v in self.ctr.items() if v > 0]

    def tag(self, handle):
        self.tensor_era[handle.name] = self.era
        return handle

    def totals(self, sems=("P", "V", "A")):
        return [(s, self.ctr[s]) for s in sems]

    def _wait(self, eng, ev):
        sem, val = ev
        if self.engine_sem[eng] == sem:
            assert val <= self.ctr[sem], f"{eng} waiting future own event {ev}"
            if not SAME_ENGINE_WAITS:
                return None
        key = (eng, sem)
        if self.last_wait.get(key, 0) >= val:
            return None
        self.last_wait[key] = val
        return (self.sem_handles[sem], val)

    def _prep(self, eng, waits, inc, inc_n, out_name=None):
        if out_name is not None:
            te = self.tensor_era.get(out_name, 0)
            if te > self.engine_era.get(eng, 0):
                self.engine_era[eng] = te
                waits = list(waits) + self.era_barriers[te]
        wl = []
        for ev in waits:
            if ev is None:
                continue
            w = self._wait(eng, ev)
            if w is not None:
                wl.append(w)
        sem_name = inc if isinstance(inc, str) else (self.engine_sem[eng] if inc else None)
        if sem_name is not None:
            self.ctr[sem_name] += inc_n
            return wl, (sem_name, self.ctr[sem_name]), self.sem_handles[sem_name], inc_n
        return wl, None, None, 0

    def op(self, eng, method, *args, waits=(), inc=None, inc_n=1, **kwargs):
        out_ap = kwargs.get("out")
        if out_ap is None and args:
            out_ap = args[0]
        out_name = getattr(out_ap, "name", None) if out_ap is not None else None
        wl, ev_out, sem_h, k = self._prep(eng, waits, inc, inc_n, out_name)

        def fn(engine, wl=wl, method=method, args=args, kwargs=kwargs, sem_h=sem_h, k=k):
            for sem, val in wl:
                engine.wait_ge(sem, val)
            inst = getattr(engine, method)(*args, **kwargs)
            if sem_h is not None:
                inst.then_inc(sem_h, k)
            return inst

        self.streams[eng].append(fn)
        return ev_out

    def raw(self, eng, fn, waits=(), inc=None, inc_n=1, out_name=None):
        wl, ev_out, sem_h, k = self._prep(eng, waits, inc, inc_n, out_name)

        def wrapped(engine, wl=wl, fn=fn, sem_h=sem_h, k=k):
            for sem, val in wl:
                engine.wait_ge(sem, val)

            def post(inst):
                if sem_h is not None:
                    inst.then_inc(sem_h, k)
            fn(engine, post)

        self.streams[eng].append(wrapped)
        return ev_out

    def wait_free(self, name):
        return self.free_ev.get(name)

    def set_prod(self, name, ev):
        if ev is not None:
            self.prod_ev[name] = ev

    def get_prod(self, name):
        return self.prod_ev.get(name)

    def set_free(self, name, ev):
        if ev is not None:
            self.free_ev[name] = ev


def build_nc():
    nc = bass.Bass("TRN2", target_bir_lowering=False, debug=False)
    dt = nc.dram_tensor

    xb = dt("xb", [N, D], F32, kind="ExternalInput").ap()
    x_own = dt("x_own", [HALF, D], F32, kind="ExternalInput").ap()
    half_ofs = dt("half_ofs", [1, 1], I32, kind="ExternalInput").ap()
    Wq_h = dt("Wq", [D, D], F32, kind="ExternalInput").ap()
    Wk_h = dt("Wk", [D, D], F32, kind="ExternalInput").ap()
    Wv_h = dt("Wv", [D, D], F32, kind="ExternalInput").ap()
    Wo_h = dt("Wo", [D, D], F32, kind="ExternalInput").ap()
    Wg1_h = dt("Wg1", [D, H], F32, kind="ExternalInput").ap()
    Wg2_h = dt("Wg2", [H, D], F32, kind="ExternalInput").ap()
    nwc_h = dt("nwc", [128, 8], F32, kind="ExternalInput").ap()
    cosim_h = dt("cosim", [128, NT * HD], F32, kind="ExternalInput").ap()
    sinim_h = dt("sinim", [128, NT * HD], F32, kind="ExternalInput").ap()
    tri_h = dt("tri", [128, 128], F32, kind="ExternalInput").ap()
    identb_h = dt("identb", [128, 128], BF16, kind="ExternalInput").ap()
    onesrow_h = dt("onesrow", [1, 128], F32, kind="ExternalInput").ap()
    onescol_h = dt("onescol", [128, 1], F32, kind="ExternalInput").ap()
    zrow_h = dt("zrow", [1, 512], F32, kind="ExternalInput").ap()
    w1i_h = dt("w1i", [32, 128, 1024], F32, kind="ExternalInput").ap()
    w2i_h = dt("w2i", [32, 128, 1024], F32, kind="ExternalInput").ap()
    h1i_h = dt("h1i", [32, 128, 4096], F32, kind="ExternalInput").ap()
    h2i_h = dt("h2i", [32, 128, 4096], F32, kind="ExternalInput").ap()
    w3i_h = dt("w3i", [32, 128, 1024], F32, kind="ExternalInput").ap()
    s1c_h = dt("s1c", [128, 32], F32, kind="ExternalInput").ap()
    s2c_h = dt("s2c", [128, 32], F32, kind="ExternalInput").ap()
    h1c_h = dt("h1c", [128, 32], F32, kind="ExternalInput").ap()
    h2c_h = dt("h2c", [128, 32], F32, kind="ExternalInput").ap()
    s3bc_h = dt("s3bc", [128, 1024], F32, kind="ExternalInput").ap()
    cvals_h = dt("cvals", [128, 6], F32, kind="ExternalInput").ap()

    out_h = dt("out", [HALF, D], F32, kind="ExternalOutput").ap()

    og_dram = dt("og_dram", [N, D], BF16, kind="Internal").ap()
    hp_dram = dt("hp_dram", [128, 32 * 1024], BF16, kind="Internal").ap()
    hp2_dram = dt("hp2_dram", [128, 32 * 1024], BF16, kind="Internal").ap()
    xm_dram = dt("xm_dram", [HALF, D], F32, kind="Internal").ap()

    bld = Builder(nc)

    with ExitStack() as gctx:
        sb = lambda name, shape, dtype=F32: bld.tag(gctx.enter_context(nc.sbuf_tensor(name, shape, dtype)))
        bld.init_sems(gctx)

        S_ps = gctx.enter_context(nc.psum_tensor("S_ps", [128, 1024], F32))
        o_ps = gctx.enter_context(nc.psum_tensor("o_ps", [128, 1024], F32))
        tpA = gctx.enter_context(nc.psum_tensor("tpA", [128, 512], F32))
        tpB = gctx.enter_context(nc.psum_tensor("tpB", [128, 512], F32))
        pj0 = gctx.enter_context(nc.psum_tensor("pj0", [128, 512], F32))
        pj1 = gctx.enter_context(nc.psum_tensor("pj1", [128, 512], F32))
        PJ = [("pj0", pj0), ("pj1", pj1)]
        TP = [("tpA", tpA), ("tpB", tpB)]

        tri = sb("tri_s", [128, 128])
        identb = sb("identb_s", [128, 128], BF16)
        onesrow = sb("onesrow_s", [1, 128])
        onescol = sb("onescol_s", [128, 1])
        zrow = sb("zrow_s", [1, 512])
        nwc = sb("nwc_s", [128, 8])
        s1c = sb("s1c_s", [128, 32]); s2c = sb("s2c_s", [128, 32])
        h1c = sb("h1c_s", [128, 32]); h2c = sb("h2c_s", [128, 32])
        cvals = sb("cvals_s", [128, 6])
        for ci, cv in enumerate(CONST_VALS):
            nc.const_aps.register(F32, cv, cvals[:, ci:ci + 1])

        s3bc = sb("s3bc_s", [128, 1024])
        ofs_sb = sb("ofs_sb", [1, 1], I32)
        for dst, src in [(tri, tri_h), (identb, identb_h), (onesrow, onesrow_h),
                         (onescol, onescol_h), (zrow, zrow_h), (nwc, nwc_h),
                         (s1c, s1c_h), (s2c, s2c_h), (h1c, h1c_h), (h2c, h2c_h),
                         (cvals, cvals_h), (s3bc, s3bc_h), (ofs_sb, half_ofs)]:
            bld.op("sync", "dma_start", out=dst[:], in_=src[:], inc="C", inc_n=16)

        with ExitStack() as actx:
            sba = lambda name, shape, dtype=F32: bld.tag(actx.enter_context(nc.sbuf_tensor(name, shape, dtype)))
            cosim = sba("cosim_s", [128, NT * HD])
            sinim = sba("sinim_s", [128, NT * HD])
            Wq_bf = sba("Wq_bf", [128, 8 * 1024], BF16)
            Wk_bf = sba("Wk_bf", [128, 8 * 1024], BF16)
            Wv_bf = sba("Wv_bf", [128, 8 * 1024], BF16)
            Wg1_bf = sba("Wg1_bf", [128, 8 * 16], BF16)
            Wg2_bf = sba("Wg2_bf", [16, 1024], BF16)
            t1f = sba("t1f", [16, N], BF16)
            carry = sba("carry", [1, 1024])
            S_bf = sba("S_bf", [128, 1024], BF16)

            bld.op("sync", "dma_start", out=cosim[:], in_=cosim_h[:], inc="C", inc_n=16)
            bld.op("sync", "dma_start", out=sinim[:], in_=sinim_h[:], inc="C", inc_n=16)
            ev_tabs = ("C", bld.ctr["C"])
            ev_consts = ev_tabs

            x_t = [sba(f"x_t{p}", [128, 1024]) for p in range(2)]
            x1t = [sba(f"x1t{p}", [128, 1024], BF16) for p in range(2)]
            x1f = [sba(f"x1f{p}", [128, 1024], BF16) for p in range(2)]
            sqscr = sba("sqscr", [128, 1024], BF16)
            st = {nm: [sba(f"st_{nm}_{p}", [128, 1]) for p in range(2)]
                  for nm in ("s1", "sq", "m", "t0", "v", "r", "nmr", "oss", "orr")}
            qraw = [sba(f"qraw{p}", [128, 1024]) for p in range(2)]
            escr = [sba(f"escr{p}", [128, 1024]) for p in range(2)]
            E_tm = [sba(f"E_tm{p}", [128, 1024]) for p in range(2)]
            sig = [sba(f"sig{p}", [128, 1024]) for p in range(2)]
            qh_tm = [sba(f"qh_tm{p}", [128, 2048], BF16) for p in range(2)]
            eh_tm = [sba(f"eh_tm{p}", [128, 2048], BF16) for p in range(2)]
            qh_fm = [sba(f"qh_fm{p}", [128, 2048], BF16) for p in range(2)]
            eh_fm = [sba(f"eh_fm{p}", [128, 2048], BF16) for p in range(2)]
            v_tm = [sba(f"v_tm{p}", [128, 1024], BF16) for p in range(2)]
            A_sb = [sba(f"A_sb{p}", [128, 128], BF16) for p in range(2)]
            og = sba("og", [128, 1024])
            ogn = [sba(f"ogn{p}", [128, 1024], BF16) for p in range(2)]
            with ExitStack() as wctx:
                wld = [bld.tag(wctx.enter_context(nc.sbuf_tensor(f"wld{q}", [128, 1024], F32)))
                       for q in range(4)]
                for wi, (Wh, Wb) in enumerate([(Wq_h, Wq_bf), (Wk_h, Wk_bf), (Wv_h, Wv_bf)]):
                    for c in range(8):
                        q = (wi * 8 + c) % 4
                        evl = bld.op("sync", "dma_start", out=wld[q][:],
                                     in_=Wh[c * 128:(c + 1) * 128, :],
                                     waits=[bld.wait_free(f"wld{q}")], inc=f"wld{q}", inc_n=16)
                        evc = bld.op("vector", "tensor_copy",
                                     out=Wb[:, c * 1024:(c + 1) * 1024], in_=wld[q][:],
                                     waits=[evl], inc=True)
                        bld.set_free(f"wld{q}", evc)
                evl = bld.op("sync", "dma_start",
                             out=wld[0][:, 0:128].rearrange("p (c g) -> p c g", g=16),
                             in_=Wg1_h[:].rearrange("(c p) g -> p c g", p=128),
                             waits=[bld.wait_free("wld0")], inc="wld0", inc_n=16)
                evc = bld.op("vector", "tensor_copy", out=Wg1_bf[:], in_=wld[0][:, 0:128],
                             waits=[evl], inc=True)
                bld.set_free("wld0", evc)
                evl = bld.op("sync", "dma_start", out=wld[1][0:16, :], in_=Wg2_h[:],
                             waits=[bld.wait_free("wld1")], inc="wld1", inc_n=16)
                evc = bld.op("vector", "tensor_copy", out=Wg2_bf[:], in_=wld[1][0:16, :],
                             waits=[evl], inc=True)
                bld.set_free("wld1", evc)
            bld.new_era()
            ev_attw = evc


            ev_z1 = bld.op("tensor", "matmul", out=S_ps[:, 0:512], lhsT=onesrow[:],
                           rhs=zrow[:], start=True, stop=True, waits=[ev_consts], inc=True)
            ev_z2 = bld.op("tensor", "matmul", out=S_ps[:, 512:1024], lhsT=onesrow[:],
                           rhs=zrow[:], start=True, stop=True, inc=True)
            bld.set_prod("S_ps_init", ev_z2)
            ev_cz = bld.op("vector", "memset", carry[:], 0.0, waits=[ev_consts], inc=True)
            bld.set_prod("carry_h0", ev_cz)
            bld.set_prod("carry_h1", ev_cz)

            def tile_pre1(i):
                p = i % 2
                evx = bld.op("sync", "dma_start", out=x_t[p][:],
                             in_=xb[i * 128:(i + 1) * 128, :],
                             waits=[bld.wait_free(f"x_t{p}")],
                             inc=f"x{p}", inc_n=16)
                ev_s1 = bld.op("vector", "reduce_sum", out=st["s1"][p][:], in_=x_t[p][:],
                               axis=AXL.X, waits=[evx, bld.wait_free(f"stats{p}")], inc=True)
                ev_sq = bld.op("scalar", "activation", out=sqscr[:], in_=x_t[p][:],
                               func=AF.Square, accum_out=st["sq"][p][:],
                               waits=[evx, bld.wait_free("sqscr")], inc=True)
                bld.set_free("sqscr", ev_sq)
                ev_m = bld.op("vector", "tensor_scalar_mul", out=st["m"][p][:], in0=st["s1"][p][:],
                              scalar1=1.0 / D, waits=[ev_s1], inc=True)
                ev_t0 = bld.op("vector", "tensor_mul", out=st["t0"][p][:], in0=st["m"][p][:],
                               in1=st["m"][p][:], waits=[ev_m], inc=True)
                ev_v = bld.op("vector", "scalar_tensor_tensor", out=st["v"][p][:], in0=st["sq"][p][:],
                              scalar=1.0 / D, in1=st["t0"][p][:], op0=ALU.mult, op1=ALU.subtract,
                              waits=[ev_sq, ev_t0], inc=True)
                ev_ln = bld.op("scalar", "activation", out=st["v"][p][:], in_=st["v"][p][:],
                               func=AF.Ln, bias=EPS_LN,
                               waits=[ev_v, ev_consts if i < 2 else None], inc=True)
                ev_r = bld.op("scalar", "activation", out=st["r"][p][:], in_=st["v"][p][:],
                              func=AF.Exp, scale=-0.5, waits=[ev_ln], inc=True)
                ev_nmr = bld.op("vector", "scalar_tensor_tensor", out=st["nmr"][p][:],
                                in0=st["m"][p][:], scalar=-1.0, in1=st["r"][p][:],
                                op0=ALU.mult, op1=ALU.mult, waits=[ev_m, ev_r], inc=True)
                bld.set_free(f"stats{p}", ev_nmr)
                ev_x1t = bld.op("vector", "tensor_scalar", out=x1t[p][:], in0=x_t[p][:],
                                scalar1=st["m"][p][:], scalar2=st["r"][p][:],
                                op0=ALU.subtract, op1=ALU.mult,
                                waits=[ev_r, bld.wait_free(f"x1t{p}")], inc=True)
                bld.set_free(f"x_t{p}", ev_x1t)
                bld.set_prod(f"x1t_{i}", ev_x1t)

            def tile_x1tp(i):
                p = i % 2
                for half in range(2):
                    tpn, tp = TP[half]
                    tpv = tp[:].bitcast(BF16)
                    ev_tp = None
                    for c in range(4):
                        cc = half * 4 + c
                        ev_tp = bld.op("tensor", "transpose",
                                       out=tpv[:, c * 128:(c + 1) * 128],
                                       in_=x1t[p][:, cc * 128:(cc + 1) * 128],
                                       identity=identb[:],
                                       waits=[bld.get_prod(f"x1t_{i}"), bld.wait_free(tpn)] if c == 0 else [],
                                       inc=(c == 3))
                    ev_ev = bld.op("vector", "tensor_copy",
                                   out=x1f[p][:, half * 512:(half + 1) * 512], in_=tpv[:, 0:512],
                                   waits=[ev_tp, bld.wait_free(f"x1f{p}_h{half}")], inc=True)
                    bld.set_free(tpn, ev_ev)
                    bld.set_prod(f"x1f{p}_h{half}", ev_ev)
                bld.set_free(f"x1t{p}", bld.get_prod(f"x1f{p}_h1"))

            def tile_pre2(i):
                p = i % 2

                def proj(Wb):
                    evs = []
                    for half in range(2):
                        pjn, pj = PJ[half]
                        ev_mm = None
                        for c in range(8):
                            ev_mm = bld.op("tensor", "matmul", out=pj[:],
                                           lhsT=x1f[p][:, c * 128:(c + 1) * 128],
                                           rhs=Wb[:, c * 1024 + half * 512: c * 1024 + half * 512 + 512],
                                           start=(c == 0), stop=(c == 7),
                                           waits=[bld.get_prod(f"x1f{p}_h{c // 4}"),
                                                  ev_attw if i < 2 else None,
                                                  bld.wait_free(pjn)] if c in (0, 4) else [],
                                           inc=(c == 7))
                        evs.append((pjn, pj, ev_mm))
                    return evs

                for half, (pjn, pj, ev_mm) in enumerate(proj(Wq_bf)):
                    sl = slice(half * 512, half * 512 + 512)
                    ev_qr = bld.op("vector", "tensor_copy", out=qraw[p][:, sl], in_=pj[:],
                                   waits=[ev_mm, bld.wait_free(f"qraw{p}")], inc=True)
                    ev_e = bld.op("scalar", "activation", out=escr[p][:, sl], in_=pj[:],
                                  func=AF.Exp, scale=-1.0,
                                  waits=[ev_mm, ev_qr, bld.wait_free(f"escr{p}")], inc=True)
                    bld.set_free(pjn, ev_e)
                    ev_l = bld.op("scalar", "activation", out=escr[p][:, sl], in_=escr[p][:, sl],
                                  func=AF.Ln, bias=1.0, waits=[ev_e], inc=True)
                    ev_g = bld.op("scalar", "activation", out=escr[p][:, sl], in_=escr[p][:, sl],
                                  func=AF.Exp, scale=-1.0, waits=[ev_l], inc=True)
                    bld.set_prod(f"gq{p}_h{half}", ev_g)
                    bld.set_prod(f"qraw{p}_h{half}", ev_qr)
                for half, (pjn, pj, ev_mm) in enumerate(proj(Wk_bf)):
                    sl = slice(half * 512, half * 512 + 512)
                    ev_e = bld.op("scalar", "activation", out=E_tm[p][:, sl], in_=pj[:],
                                  func=AF.Exp, waits=[ev_mm, bld.wait_free(f"E_tm{p}")], inc=True)
                    bld.set_free(pjn, ev_e)
                    bld.set_prod(f"E_tm{p}_h{half}", ev_e)
                for half, (pjn, pj, ev_mm) in enumerate(proj(Wv_bf)):
                    sl = slice(half * 512, half * 512 + 512)
                    ev_v2 = bld.op("vector", "tensor_copy", out=v_tm[p][:, sl], in_=pj[:],
                                   waits=[ev_mm, bld.wait_free(f"v_tm{p}")], inc=True)
                    bld.set_free(pjn, ev_v2)
                    bld.set_prod(f"v_tm{p}_h{half}", ev_v2)
                pjn, pj = PJ[0]
                ev_mm = None
                for c in range(8):
                    ev_mm = bld.op("tensor", "matmul", out=pj[0:16, 0:128],
                                   lhsT=Wg1_bf[:, c * 16:(c + 1) * 16],
                                   rhs=x1f[p][:, c * 128:(c + 1) * 128],
                                   start=(c == 0), stop=(c == 7),
                                   waits=[bld.wait_free(pjn)] if c == 0 else [],
                                   inc=(c == 7))
                ev_t1 = bld.op("vector", "tensor_copy", out=t1f[:, i * 128:(i + 1) * 128],
                               in_=pj[0:16, 0:128], waits=[ev_mm], inc=True)
                bld.set_free(pjn, ev_t1)
                bld.set_prod(f"t1_{i}", ev_t1)
                bld.set_free(f"x1f{p}_h0", ev_mm)
                bld.set_free(f"x1f{p}_h1", ev_mm)
                ev_lnps = []
                for half in range(2):
                    pjn, pj = PJ[half]
                    sl = slice(half * 512, half * 512 + 512)
                    bld.op("tensor", "matmul", out=pj[:], lhsT=tri[:],
                           rhs=E_tm[p][:, sl], start=True, stop=False,
                           waits=[bld.get_prod(f"E_tm{p}_h{half}"), bld.get_prod(f"carry_h{half}"),
                                  bld.wait_free(pjn)], inc=False)
                    ev_p2 = bld.op("tensor", "matmul", out=pj[:], lhsT=onesrow[:],
                                   rhs=carry[:, sl], start=False, stop=True, inc=True)
                    ev_lnp = bld.op("scalar", "activation", out=sig[p][:, sl], in_=pj[:],
                                    func=AF.Ln, bias=1.0,
                                    waits=[ev_p2, bld.wait_free(f"sig{p}")], inc=True)
                    ev_sig = bld.op("scalar", "activation", out=sig[p][:, sl], in_=sig[p][:, sl],
                                    func=AF.Exp, scale=-1.0, waits=[ev_lnp], inc=True)
                    bld.set_prod(f"sig{p}_h{half}", ev_sig)
                    ev_lnps.append(ev_lnp)
                for half in range(2):
                    pjn, pj = PJ[half]
                    sl = slice(half * 512, half * 512 + 512)
                    ev_tt = bld.op("tensor", "matmul", out=pj[0:1, 0:512], lhsT=onescol[:],
                                   rhs=E_tm[p][:, sl], start=True, stop=True,
                                   waits=[ev_lnps[half]], inc=True)
                    ev_cu = bld.op("vector", "tensor_add", out=carry[:, sl], in0=carry[:, sl],
                                   in1=pj[0:1, 0:512], waits=[ev_tt], inc=True)
                    bld.set_free(pjn, ev_cu)
                    bld.set_prod(f"carry_h{half}", ev_cu)
            def tile_lrpe(i):
                p = i % 2
                ev_gs = bld.op("vector", "tensor_mul", out=escr[p][:], in0=escr[p][:], in1=sig[p][:],
                               waits=[bld.get_prod(f"gq{p}_h0"), bld.get_prod(f"gq{p}_h1"),
                                      bld.get_prod(f"sig{p}_h0"), bld.get_prod(f"sig{p}_h1")],
                               inc=True)
                bld.set_free(f"sig{p}", ev_gs)
                ev_tq = bld.op("vector", "tensor_mul", out=qraw[p][:], in0=qraw[p][:], in1=escr[p][:],
                               waits=[ev_gs], inc=True)
                bld.set_free(f"escr{p}", ev_tq)
                ct = cosim[:, i * HD:(i + 1) * HD]
                stt_ = sinim[:, i * HD:(i + 1) * HD]
                ct_b = ct.unsqueeze(1).broadcast_to((128, 16, 64))
                st_b = stt_.unsqueeze(1).broadcast_to((128, 16, 64))
                tq3 = qraw[p][:].rearrange("p (h d) -> p h d", d=64)
                E3 = E_tm[p][:].rearrange("p (h d) -> p h d", d=64)
                qh4 = qh_tm[p][:].rearrange("p (h t d) -> p h t d", h=16, t=2)
                eh4 = eh_tm[p][:].rearrange("p (h t d) -> p h t d", h=16, t=2)
                bld.op("vector", "tensor_mul", out=qh4[:, :, 0, :], in0=tq3, in1=ct_b,
                       waits=[ev_tq, ev_tabs, bld.wait_free(f"qh_tm{p}")], inc=True)
                ev_q2 = bld.op("vector", "tensor_mul", out=qh4[:, :, 1, :], in0=tq3, in1=st_b,
                               inc=True)
                bld.set_free(f"qraw{p}", ev_q2)
                ev_e1 = bld.op("gpsimd", "tensor_mul", out=eh4[:, :, 0, :], in0=E3, in1=ct_b,
                               waits=[bld.get_prod(f"E_tm{p}_h0"), bld.get_prod(f"E_tm{p}_h1"),
                                      ev_tabs, bld.wait_free(f"eh_tm{p}")], inc="O")
                ev_e2 = bld.op("gpsimd", "tensor_mul", out=eh4[:, :, 1, :], in0=E3, in1=st_b,
                               inc="O")
                bld.set_free(f"E_tm{p}", ev_e2)
                bld.set_prod(f"qh_tm{p}", ev_q2)
                bld.set_prod(f"eh_tm{p}", ev_e2)

            def tile_qe_tp(i):
                p = i % 2
                ev_q2 = bld.get_prod(f"qh_tm{p}")
                ev_e2 = bld.get_prod(f"eh_tm{p}")
                for (src, sname, dst, dname, pev) in (
                        (qh_tm, f"qh_tm{p}", qh_fm, f"qh_fm{p}", ev_q2),
                        (eh_tm, f"eh_tm{p}", eh_fm, f"eh_fm{p}", ev_e2)):
                    ev_tp_last = None
                    for grp in range(4):
                        tpn, tp = TP[grp % 2]
                        tpv = tp[:].bitcast(BF16)
                        ev_tp = None
                        for c in range(4):
                            h = grp * 4 + c
                            ev_tp = bld.op("tensor", "transpose",
                                           out=tpv[:, c * 128:(c + 1) * 128],
                                           in_=src[p][:, h * 128:(h + 1) * 128],
                                           identity=identb[:],
                                           waits=[pev, bld.wait_free(tpn)] if c == 0 else [],
                                           inc=(c == 3))
                        ev_tp_last = ev_tp
                        eng_ev = "vector" if grp % 2 == 0 else "scalar"
                        ev_ev = bld.op(eng_ev, "tensor_copy" if grp % 2 == 0 else "copy",
                                       **({"out": dst[p][:, grp * 512:(grp + 1) * 512],
                                           "in_": tpv[:, 0:512]}),
                                       waits=[ev_tp, bld.wait_free(dname)] if grp == 0 else [ev_tp],
                                       inc=True)
                        bld.set_free(tpn, ev_ev)
                        bld.set_prod(f"{dname}_g{grp}", ev_ev)
                    bld.set_prod(dname, bld.get_prod(f"{dname}_g3"))
                    if sname.startswith("qh_tm"):
                        bld.set_free(sname, ev_tp_last)

            def tile_gla(i):
                p = i % 2
                if i > 0:
                    ev_sbf = bld.op("vector", "tensor_copy", out=S_bf[:], in_=S_ps[:],
                                    waits=[bld.get_prod("S_upd"), bld.wait_free("S_bf")],
                                    inc=True)
                    bld.set_prod("S_bf", ev_sbf)
                    bld.set_free("S_ps_rd", ev_sbf)
                for half in range(2):
                    pjn, pj = PJ[half]
                    sl = slice(half * 512, half * 512 + 512)
                    ev_mm = bld.op("tensor", "matmul", out=pj[:],
                                   lhsT=t1f[:, i * 128:(i + 1) * 128],
                                   rhs=Wg2_bf[:, half * 512:half * 512 + 512],
                                   start=True, stop=True,
                                   waits=[bld.get_prod(f"t1_{i}"), bld.wait_free(pjn)],
                                   inc=True)
                    ev_e = bld.op("scalar", "activation", out=escr[p][:, sl], in_=pj[:],
                                  func=AF.Exp, scale=-1.0,
                                  waits=[ev_mm, bld.wait_free(f"escr{p}")], inc=True)
                    bld.set_free(pjn, ev_e)
                    ev_l = bld.op("scalar", "activation", out=escr[p][:, sl], in_=escr[p][:, sl],
                                  func=AF.Ln, bias=1.0, waits=[ev_e], inc=True)
                    ev_g = bld.op("scalar", "activation", out=escr[p][:, sl], in_=escr[p][:, sl],
                                  func=AF.Exp, scale=-1.0, waits=[ev_l], inc=True)
                    bld.set_prod(f"gate{p}_h{half}", ev_g)
                def emit_A(h):
                    qsl = slice(h * 128, (h + 1) * 128)
                    pa = h % 2
                    tpn, tp = TP[pa]
                    ev_a = bld.op("tensor", "matmul", out=tp[:, 0:128],
                                  lhsT=eh_fm[p][:, qsl], rhs=qh_fm[p][:, qsl],
                                  start=True, stop=True,
                                  waits=[bld.get_prod(f"eh_fm{p}_g{h // 4}"),
                                         bld.get_prod(f"qh_fm{p}_g{h // 4}"),
                                         bld.wait_free(tpn)],
                                  inc=True)
                    ev_am = bld.op("vector", "tensor_mul", out=A_sb[pa][:], in0=tp[:, 0:128],
                                   in1=tri[:], waits=[ev_a, bld.wait_free(f"A_sb{pa}")],
                                   inc=True)
                    bld.set_free(tpn, ev_am)
                    bld.set_prod(f"am_{h}", ev_am)

                def emit_intra_S(h):
                    hb = h // 8
                    hsl = slice(h * 64, (h + 1) * 64)
                    qsl = slice(h * 128, (h + 1) * 128)
                    pa = h % 2
                    if i > 0:
                        bld.op("tensor", "matmul", out=o_ps[:, hsl],
                               lhsT=qh_fm[p][:, qsl], rhs=S_bf[:, hsl],
                               start=True, stop=True,
                               waits=[bld.get_prod("S_bf"),
                                      bld.get_prod(f"qh_fm{p}_g{h // 4}"),
                                      bld.wait_free(f"o_ps{hb}")],
                               inc=False)
                    bld.op("tensor", "matmul", out=o_ps[:, hsl],
                           lhsT=A_sb[pa][:], rhs=v_tm[p][:, hsl],
                           start=(i == 0), stop=True,
                           waits=[bld.get_prod(f"am_{h}"), bld.get_prod(f"v_tm{p}_h{h // 8}"),
                                  bld.wait_free(f"o_ps{hb}") if i == 0 else None],
                           inc=False)
                    ev_mark = bld.op("tensor", "matmul", out=S_ps[:, hsl],
                                     lhsT=eh_tm[p][:, qsl], rhs=v_tm[p][:, hsl],
                                     start=False, stop=True,
                                     waits=[bld.get_prod(f"eh_tm{p}"),
                                            bld.get_prod("S_ps_init") if i == 0 and h == 0 else None,
                                            bld.wait_free("S_ps_rd") if h == 0 else None],
                                     inc=True)
                    bld.set_free(f"A_sb{pa}", ev_mark)
                    if h == 15:
                        bld.set_prod("S_upd", ev_mark)
                        bld.set_prod(f"o_done_{i}", ev_mark)
                        bld.set_free("S_bf", ev_mark)
                        bld.set_free(f"qh_fm{p}", ev_mark)
                        bld.set_free(f"eh_fm{p}", ev_mark)
                        bld.set_free(f"v_tm{p}", ev_mark)
                        bld.set_free(f"eh_tm{p}", ev_mark)

                emit_A(0)
                for h in range(16):
                    if h + 1 < 16:
                        emit_A(h + 1)
                    emit_intra_S(h)
            def tile_og(i):
                p = i % 2
                ev_od = bld.get_prod(f"o_done_{i}")
                for half in range(2):
                    sl = slice(half * 512, half * 512 + 512)
                    ev_og2 = bld.op("vector", "tensor_mul", out=og[:, sl],
                                    in0=o_ps[:, sl], in1=escr[p][:, sl],
                                    waits=[ev_od, bld.get_prod(f"gate{p}_h{half}"),
                                           bld.wait_free("og")] if half == 0 else [
                                           bld.get_prod(f"gate{p}_h1")], inc=True)
                bld.set_free(f"escr{p}", ev_og2)
                bld.set_free("o_ps0", ev_og2)
                bld.set_free("o_ps1", ev_og2)
                ev_oss = bld.op("scalar", "activation", out=sqscr[:], in_=og[:],
                                func=AF.Square, accum_out=st["oss"][p][:],
                                waits=[ev_og2, bld.wait_free("sqscr")], inc=True)
                bld.set_free("sqscr", ev_oss)
                ev_lno = bld.op("scalar", "activation", out=st["oss"][p][:], in_=st["oss"][p][:],
                                func=AF.Ln, scale=1.0 / D, bias=EPS_RMS, waits=[ev_oss], inc=True)
                ev_ro = bld.op("scalar", "activation", out=st["orr"][p][:], in_=st["oss"][p][:],
                               func=AF.Exp, scale=-0.5, waits=[ev_lno], inc=True)
                ev_ogn = bld.op("scalar", "activation", out=ogn[p][:], in_=og[:],
                                func=AF.Copy, scale=st["orr"][p][:],
                                waits=[ev_ro, bld.wait_free(f"ogn{p}")], inc=True)
                bld.set_free("og", ev_ogn)
                ev_st2 = bld.op("gpsimd", "dma_start", out=og_dram[i * 128:(i + 1) * 128, :],
                                in_=ogn[p][:], waits=[ev_ogn], inc=f"og{p}", inc_n=16)
                bld.set_free(f"ogn{p}", ev_st2)
                bld.set_prod(f"og_store_{i}", ev_st2)
                og_counts[i] = (bld.ctr["og0"], bld.ctr["og1"])

            og_counts = {}
            tile_pre1(0)
            tile_pre1(1)
            tile_x1tp(0)
            tile_x1tp(1)
            tile_pre2(0)
            tile_lrpe(0)
            for i in range(NT):
                if i + 2 < NT:
                    tile_pre1(i + 2)
                if i + 1 < NT:
                    tile_pre2(i + 1)
                if i + 2 < NT:
                    tile_x1tp(i + 2)
                tile_qe_tp(i)
                tile_gla(i)
                if i + 1 < NT:
                    tile_lrpe(i + 1)
                tile_og(i)

            ev_last_sbf = bld.get_prod("S_bf")
            ev_last_og = bld.get_prod("og_h1")
            bld.set_free("Sps_nt0", ev_last_sbf)
            bld.set_free("Sps_nt1", ev_last_sbf)
            bld.set_free("ops_nt0", ev_last_og)
            bld.set_free("ops_nt1", ev_last_og)
            bld.og_counts = og_counts
            phaseA_done = [("P", bld.ctr["P"]), ("V", bld.ctr["V"]),
                           ("A", bld.ctr["A"]),
                           ("og0", bld.ctr["og0"]), ("og1", bld.ctr["og1"]),
                           ("x0", bld.ctr["x0"]), ("x1", bld.ctr["x1"])]

        bld.new_era()
        with ExitStack() as bctx:
            sbb = lambda name, shape, dtype=F32: bld.tag(bctx.enter_context(nc.sbuf_tensor(name, shape, dtype)))
            Wo_bf = sbb("Wo_bf", [128, 8 * 1024], BF16)
            ogf = [sbb(f"ogf{p}", [128, 1024], BF16) for p in range(2)]
            xo_t = [sbb(f"xo_t{p}", [128, 1024]) for p in range(2)]
            xm_t = [sbb(f"xm_t{p}", [128, 1024]) for p in range(2)]
            yraw = [sbb(f"yraw{p}", [128, 1024]) for p in range(2)]
            x2t = sbb("x2t", [128, 1024], BF16)
            yln = sbb("yln", [128, 1024])
            sqb = sbb("sqb", [128, 1024], BF16)
            bs = {nm: [sbb(f"bs_{nm}_{p}", [128, 1]) for p in range(2)]
                  for nm in ("s1", "sq", "sqB", "m", "t0", "v", "r", "nmr")}
            hbuf = sbb("hbuf", [128, 32 * 1024], BF16)
            outt = [sbb(f"outt{p}", [128, 1024]) for p in range(2)]
            ev_ofsl = ev_consts

            with ExitStack() as wctx:
                wldb = [bld.tag(wctx.enter_context(nc.sbuf_tensor(f"wldb{q}", [128, 1024], F32)))
                        for q in range(2)]
                for c in range(8):
                    q = c % 2
                    evl = bld.op("sync", "dma_start", out=wldb[q][:],
                                 in_=Wo_h[c * 128:(c + 1) * 128, :],
                                 waits=[bld.wait_free(f"wldb{q}")], inc=f"wo{q}", inc_n=16)
                    evc = bld.op("vector", "tensor_scalar_mul",
                                 out=Wo_bf[:, c * 1024:(c + 1) * 1024], in0=wldb[q][:],
                                 scalar1=nwc[:, c:c + 1],
                                 waits=[evl, ev_consts] + (phaseA_done if c == 0 else []),
                                 inc=True)
                    bld.set_free(f"wldb{q}", evc)
            bld.new_era()
            ev_wo = ("V", bld.ctr["V"])

            regcell = {}

            def setup_reg(engine, post):
                reg = engine.alloc_register("hofs")
                engine.reg_load(reg, ofs_sb[0:1, 0:1])
                regcell["ofs"] = engine.snap(reg)
                inst = engine.nop()
                post(inst)

            ev_reg = bld.raw("gpsimd", setup_reg, waits=[ev_ofsl] + phaseA_done, inc="O")

            for hf in range(2):
                bld.new_era()
                bhctx = ExitStack()
                x2f = bld.tag(bhctx.enter_context(nc.sbuf_tensor(f"x2f{hf}", [128, 8 * 1024], BF16)))
                with ExitStack() as hctx:
                    og_sb = bld.tag(hctx.enter_context(nc.sbuf_tensor(f"og_sb{hf}", [128, 8 * 1024], BF16)))

                    def og_load(engine, post, hf=hf):
                        ofs = regcell["ofs"]
                        src = og_dram[bass.ds(ofs, HALF), :][hf * 1024: hf * 1024 + 1024, :]
                        src = src.rearrange("(t p) m -> p t m", p=128)
                        inst = engine.dma_start(
                            out=og_sb[:].rearrange("p (t m) -> p t m", t=8), in_=src)
                        post(inst)

                    need_tile = 23 if hf == 0 else 31
                    oc0, oc1 = bld.og_counts[need_tile]
                    ev_ogl = bld.raw("gpsimd", og_load,
                                     waits=[ev_reg, ("og0", oc0), ("og1", oc1),
                                            bld.wait_free("og_sb")],
                                     inc="ogld", inc_n=16, out_name=og_sb.name)
                    bld.set_prod("og_sb", ev_ogl)

                    def B_pe(t):
                        p = t % 2
                        gtok = hf * 1024 + t * 128
                        evx = bld.op("sync", "dma_start", out=xo_t[p][:],
                                     in_=x_own[gtok:gtok + 128, :],
                                     waits=[bld.wait_free(f"xo_t{p}")], inc=f"xo{p}", inc_n=16)
                        bld.set_prod(f"xo_{t}", evx)
                        for half in range(2):
                            tpn, tp = TP[half]
                            tpv = tp[:].bitcast(BF16)
                            ev_tp = None
                            for c in range(4):
                                cc = half * 4 + c
                                ev_tp = bld.op("tensor", "transpose",
                                               out=tpv[:, c * 128:(c + 1) * 128],
                                               in_=og_sb[:, t * 1024 + cc * 128: t * 1024 + cc * 128 + 128],
                                               identity=identb[:],
                                               waits=[ev_ogl, ev_wo, bld.wait_free(tpn)] if c == 0 else [],
                                               inc=(c == 3))
                            ev_ev = bld.op("vector", "tensor_copy",
                                           out=ogf[p][:, half * 512:(half + 1) * 512],
                                           in_=tpv[:, 0:512],
                                           waits=[ev_tp, bld.wait_free(f"ogf{p}")], inc=True)
                            bld.set_free(tpn, ev_ev)
                            bld.set_prod(f"ogf{p}_h{half}", ev_ev)
                        for half in range(2):
                            pjn, pj = PJ[half]
                            ev_mm = None
                            for c in range(8):
                                ev_mm = bld.op("tensor", "matmul", out=pj[:],
                                               lhsT=ogf[p][:, c * 128:(c + 1) * 128],
                                               rhs=Wo_bf[:, c * 1024 + half * 512: c * 1024 + half * 512 + 512],
                                               start=(c == 0), stop=(c == 7),
                                               waits=[bld.get_prod(f"ogf{p}_h{c // 4}"),
                                                      bld.wait_free(pjn)] if c in (0, 4) else [],
                                               inc=(c == 7))
                            ev_yev = bld.op("vector", "tensor_copy",
                                            out=yraw[p][:, half * 512:(half + 1) * 512], in_=pj[:],
                                            waits=[ev_mm, bld.wait_free(f"yraw{p}")], inc=True)
                            bld.set_free(pjn, ev_yev)
                            bld.set_prod(f"yraw_{t}_h{half}", ev_yev)
                        bld.set_free(f"ogf{p}", bld.get_prod("y_ps_h1") if False else ev_mm)

                    def B_ln(t):
                        p = t % 2
                        gtok = hf * 1024 + t * 128
                        evy = [bld.get_prod(f"yraw_{t}_h0"), bld.get_prod(f"yraw_{t}_h1")]
                        ev_sa = bld.op("vector", "reduce_sum", out=bs["s1"][p][:], in_=yraw[p][:],
                                       axis=AXL.X, waits=evy + [bld.wait_free(f"bstats{p}")], inc=True)
                        ev_qa = bld.op("scalar", "activation", out=sqb[:], in_=yraw[p][:],
                                       func=AF.Square, accum_out=bs["sq"][p][:],
                                       waits=evy + [bld.wait_free("sqb")], inc=True)
                        bld.set_free("sqb", ev_qa)
                        ev_m = bld.op("vector", "tensor_scalar_mul", out=bs["m"][p][:],
                                      in0=bs["s1"][p][:], scalar1=1.0 / D, waits=[ev_sa], inc=True)
                        ev_t0 = bld.op("vector", "tensor_mul", out=bs["t0"][p][:], in0=bs["m"][p][:],
                                       in1=bs["m"][p][:], waits=[ev_m], inc=True)
                        ev_v = bld.op("vector", "scalar_tensor_tensor", out=bs["v"][p][:],
                                      in0=bs["sq"][p][:], scalar=1.0 / D, in1=bs["t0"][p][:],
                                      op0=ALU.mult, op1=ALU.subtract, waits=[ev_qa, ev_t0], inc=True)
                        ev_ln = bld.op("scalar", "activation", out=bs["v"][p][:], in_=bs["v"][p][:],
                                       func=AF.Ln, bias=EPS_LN, waits=[ev_v], inc=True)
                        ev_r = bld.op("scalar", "activation", out=bs["r"][p][:], in_=bs["v"][p][:],
                                      func=AF.Exp, scale=-0.5, waits=[ev_ln], inc=True)
                        ev_yl = bld.op("vector", "tensor_scalar", out=yln[:], in0=yraw[p][:],
                                       scalar1=bs["m"][p][:], scalar2=bs["r"][p][:],
                                       op0=ALU.subtract, op1=ALU.mult,
                                       waits=[ev_r, bld.wait_free("yln")], inc=True)
                        ev_xm = bld.op("vector", "tensor_add", out=xm_t[p][:],
                                       in0=yln[:], in1=xo_t[p][:],
                                       waits=[ev_yl, bld.get_prod(f"xo_{t}"),
                                              bld.wait_free(f"xm_t{p}"),
                                              bld.wait_free(f"xm_tst{p}")], inc=True)
                        bld.set_free(f"yraw{p}", ev_yl)
                        bld.set_free(f"xo_t{p}", ev_xm)
                        bld.set_free("yln", ev_xm)
                        ev_xst = bld.op("gpsimd", "dma_start", out=xm_dram[gtok:gtok + 128, :],
                                        in_=xm_t[p][:], waits=[ev_xm], inc=f"xs{p}", inc_n=16)
                        bld.set_prod(f"xm_store_{t}", ev_xst)
                        bld.set_free(f"xm_tst{p}", ev_xst)
                        ev_s3 = bld.op("vector", "reduce_sum", out=bs["s1"][p][:], in_=xm_t[p][:],
                                       axis=AXL.X, waits=[ev_xm], inc=True)
                        ev_q3 = bld.op("scalar", "activation", out=sqb[:], in_=xm_t[p][:],
                                       func=AF.Square, accum_out=bs["sq"][p][:],
                                       waits=[ev_xm, bld.wait_free("sqb")], inc=True)
                        bld.set_free("sqb", ev_q3)
                        ev_m3 = bld.op("vector", "tensor_scalar_mul", out=bs["m"][p][:],
                                       in0=bs["s1"][p][:], scalar1=1.0 / D, waits=[ev_s3], inc=True)
                        ev_t3 = bld.op("vector", "tensor_mul", out=bs["t0"][p][:], in0=bs["m"][p][:],
                                       in1=bs["m"][p][:], waits=[ev_m3], inc=True)
                        ev_v3 = bld.op("vector", "scalar_tensor_tensor", out=bs["v"][p][:],
                                       in0=bs["sq"][p][:], scalar=1.0 / D, in1=bs["t0"][p][:],
                                       op0=ALU.mult, op1=ALU.subtract, waits=[ev_q3, ev_t3], inc=True)
                        ev_l3 = bld.op("scalar", "activation", out=bs["v"][p][:], in_=bs["v"][p][:],
                                       func=AF.Ln, bias=EPS_LN, waits=[ev_v3], inc=True)
                        ev_r3 = bld.op("scalar", "activation", out=bs["r"][p][:], in_=bs["v"][p][:],
                                       func=AF.Exp, scale=-0.5, waits=[ev_l3], inc=True)
                        ev_x2 = bld.op("vector", "tensor_scalar", out=x2t[:], in0=xm_t[p][:],
                                       scalar1=bs["m"][p][:], scalar2=bs["r"][p][:],
                                       op0=ALU.subtract, op1=ALU.mult,
                                       waits=[ev_r3, bld.wait_free("x2t")], inc=True)
                        bld.set_free(f"bstats{p}", ev_x2)
                        bld.set_free(f"xm_t{p}", ev_x2)
                        bld.set_prod(f"x2t_{t}", ev_x2)

                    def B_x2tp(t):
                        ev_x2 = bld.get_prod(f"x2t_{t}")
                        ev_e2 = None
                        for half in range(2):
                            tpn, tp = TP[half]
                            tpv = tp[:].bitcast(BF16)
                            ev_tp = None
                            for c in range(4):
                                cc = half * 4 + c
                                ev_tp = bld.op("tensor", "transpose",
                                               out=tpv[:, c * 128:(c + 1) * 128],
                                               in_=x2t[:, cc * 128:(cc + 1) * 128],
                                               identity=identb[:],
                                               waits=[ev_x2, bld.wait_free(tpn)] if c == 0 else [],
                                               inc=(c == 3))
                            for c in range(4):
                                cc = half * 4 + c
                                ev_e2 = bld.op("vector", "tensor_copy",
                                               out=x2f[:, cc * 1024 + t * 128: cc * 1024 + t * 128 + 128],
                                               in_=tpv[:, c * 128:(c + 1) * 128],
                                               waits=[ev_tp, bld.wait_free("x2f")] if c == 0 else [ev_tp],
                                               inc=(c == 3))
                            bld.set_free(tpn, ev_e2)
                        bld.set_free("x2t", ev_e2)
                        bld.set_prod(f"x2f_t{t}", ev_e2)

                    B_pe(0)
                    for t in range(8):
                        if t + 1 < 8:
                            B_pe(t + 1)
                        B_ln(t)
                        B_x2tp(t)
                    ev_x2f_all = bld.get_prod("x2f_t7")
                    bld.set_free("og_sb", ev_x2f_all)

                bld.new_era()
                ffn_barrier = bld.totals()
                with ExitStack() as c1ctx:
                    wfA = [bld.tag(c1ctx.enter_context(nc.sbuf_tensor(f"wfA{hf}_{q}", [128, 2048], F32)))
                           for q in range(2)]
                    wbA = [bld.tag(c1ctx.enter_context(nc.sbuf_tensor(f"wbA{hf}_{q}", [128, 1024], BF16)))
                           for q in range(2)]
                    wbB = [bld.tag(c1ctx.enter_context(nc.sbuf_tensor(f"wbB{hf}_{q}", [128, 1024], BF16)))
                           for q in range(2)]
                    sA = [bld.tag(c1ctx.enter_context(nc.sbuf_tensor(f"sA{hf}_{q}", [128, 512], F32)))
                          for q in range(2)]
                    for oc in range(32):
                        q = oc % 2
                        evl1 = bld.op("sync", "dma_start", out=wfA[q][:, 0:1024],
                                      in_=w1i_h[oc],
                                      waits=(ffn_barrier if oc < 2 else []) +
                                            [bld.wait_free(f"wfA{q}")], inc=f"wfA{q}", inc_n=16)
                        evl2 = bld.op("sync", "dma_start", out=wfA[q][:, 1024:2048],
                                      in_=w2i_h[oc], inc=f"wfA{q}", inc_n=16)
                        evc1 = bld.op("vector", "tensor_copy", out=wbA[q][:], in_=wfA[q][:, 0:1024],
                                      waits=[evl2, bld.wait_free(f"wbA{q}")], inc=True)
                        evc2 = bld.op("vector", "tensor_copy", out=wbB[q][:], in_=wfA[q][:, 1024:2048],
                                      waits=[evl2, bld.wait_free(f"wbB{q}")], inc=True)
                        bld.set_free(f"wfA{q}", evc2)
                        for nt in range(2):
                            psa = S_ps[:, nt * 512: nt * 512 + 512]
                            psb = o_ps[:, nt * 512: nt * 512 + 512]
                            ev_x2f_nt = bld.get_prod("x2f_t3") if nt == 0 else ev_x2f_all
                            ev_ma = None
                            for kc in range(8):
                                ev_ma = bld.op("tensor", "matmul", out=psa,
                                               lhsT=wbA[q][:, kc * 128:(kc + 1) * 128],
                                               rhs=x2f[:, kc * 1024 + nt * 512: kc * 1024 + nt * 512 + 512],
                                               start=(kc == 0), stop=(kc == 7),
                                               waits=[evc1, ev_x2f_nt,
                                                      bld.wait_free(f"Sps_nt{nt}")] if kc == 0 else [],
                                               inc=(kc == 7))
                            ev_mb = None
                            for kc in range(8):
                                ev_mb = bld.op("tensor", "matmul", out=psb,
                                               lhsT=wbB[q][:, kc * 128:(kc + 1) * 128],
                                               rhs=x2f[:, kc * 1024 + nt * 512: kc * 1024 + nt * 512 + 512],
                                               start=(kc == 0), stop=(kc == 7),
                                               waits=[evc2, ev_x2f_nt,
                                                      bld.wait_free(f"ops_nt{nt}")] if kc == 0 else [],
                                               inc=(kc == 7))
                            if nt == 1:
                                bld.set_free(f"wbA{q}", ev_ma)
                                bld.set_free(f"wbB{q}", ev_mb)
                            ev_sl = bld.op("scalar", "activation", out=sA[nt][:], in_=psa,
                                           func=AF.Silu, scale=s1c[:, oc:oc + 1],
                                           waits=[ev_ma, bld.wait_free(f"sA{nt}")], inc=True)
                            bld.set_free(f"Sps_nt{nt}", ev_sl)
                            ev_hv = bld.op("vector", "scalar_tensor_tensor",
                                           out=hbuf[:, oc * 1024 + nt * 512: oc * 1024 + nt * 512 + 512],
                                           in0=psb, scalar=s2c[:, oc:oc + 1], in1=sA[nt][:],
                                           op0=ALU.mult, op1=ALU.mult,
                                           waits=[ev_mb, ev_sl, bld.wait_free("hbuf")], inc=True)
                            bld.set_free(f"ops_nt{nt}", ev_hv)
                            bld.set_free(f"sA{nt}", ev_hv)
                            bld.set_prod(f"h_oc{oc}", ev_hv)
                    ev_h_all = bld.get_prod("h_oc31")
                    bld.set_free("x2f", bld.get_prod("h_oc31"))

                bhctx.close()
                bld.new_era()
                c2_barrier = bld.totals()
                with ExitStack() as c2ctx:
                    hwf = [bld.tag(c2ctx.enter_context(nc.sbuf_tensor(f"hwf{hf}_{q}", [128, 2048], F32)))
                           for q in range(2)]
                    hwb = [bld.tag(c2ctx.enter_context(nc.sbuf_tensor(f"hwb{hf}_{q}", [128, 4096], BF16)))
                           for q in range(2)]
                    hp_t = [bld.tag(c2ctx.enter_context(nc.sbuf_tensor(f"hp_t{hf}_{q}", [128, 512], BF16)))
                            for q in range(2)]

                    def hlayer(img_h, scol, dst_dram, src_prod_ev, first_barrier, bg=None):
                        oc_counts = []
                        for oc in range(32):
                            if bg is not None:
                                bg(oc)
                            q = oc % 2
                            evl1 = bld.op("sync", "dma_start", out=hwf[q][:],
                                          in_=img_h[oc][:, 0:2048],
                                          waits=(first_barrier if oc < 2 else []) +
                                                [bld.wait_free(f"hwf{q}")], inc=f"hw{q}", inc_n=16)
                            evc1 = bld.op("vector", "tensor_copy", out=hwb[q][:, 0:2048],
                                          in_=hwf[q][:],
                                          waits=[evl1, bld.wait_free(f"hwb{q}")], inc=True)
                            evl2 = bld.op("sync", "dma_start", out=hwf[q][:],
                                          in_=img_h[oc][:, 2048:4096],
                                          waits=[evc1], inc=f"hw{q}", inc_n=16)
                            evc2 = bld.op("vector", "tensor_copy", out=hwb[q][:, 2048:4096],
                                          in_=hwf[q][:], waits=[evl2], inc=True)
                            bld.set_free(f"hwf{q}", evc2)
                            for nt in range(2):
                                psx = S_ps[:, nt * 512: nt * 512 + 512]
                                ev_mm = None
                                for kc in range(32):
                                    ev_mm = bld.op("tensor", "matmul", out=psx,
                                                   lhsT=hwb[q][:, kc * 128:(kc + 1) * 128],
                                                   rhs=hbuf[:, kc * 1024 + nt * 512: kc * 1024 + nt * 512 + 512],
                                                   start=(kc == 0), stop=(kc == 31),
                                                   waits=[evc1, evc2, src_prod_ev,
                                                          bld.wait_free(f"Sps_nt{nt}")] if kc == 0 else [],
                                                   inc=(kc == 31))
                                if nt == 1:
                                    bld.set_free(f"hwb{q}", ev_mm)
                                pp = nt
                                ev_g = bld.op("scalar", "activation", out=hp_t[pp][:], in_=psx,
                                              func=AF.Gelu, scale=scol[:, oc:oc + 1],
                                              waits=[ev_mm, bld.wait_free(f"hp_t{pp}")], inc=True)
                                bld.set_free(f"Sps_nt{nt}", ev_g)
                                ev_hs = bld.op("gpsimd", "dma_start",
                                               out=dst_dram[:, (oc * 2 + nt) * 512: (oc * 2 + nt) * 512 + 512],
                                               in_=hp_t[pp][:], waits=[ev_g], inc=f"hp{pp}", inc_n=16)
                                bld.set_free(f"hp_t{pp}", ev_hs)
                            oc_counts.append((bld.ctr["hp0"], bld.ctr["hp1"]))
                        return [("hp0", bld.ctr["hp0"]), ("hp1", bld.ctr["hp1"])], oc_counts

                    ev_hp, h1_counts = hlayer(h1i_h, h1c, hp_dram, ev_h_all, c2_barrier)
                    ev_rl = bld.op("gpsimd", "dma_start", out=hbuf[:], in_=hp_dram[:],
                                   waits=ev_hp + [ev_h_all, bld.wait_free("hbuf")],
                                   inc="hbr", inc_n=16)
                    bld.new_era()
                    c6ctx = ExitStack()
                    w3ld = [bld.tag(c6ctx.enter_context(nc.sbuf_tensor(f"w3ld{hf}_{q}", [128, 512], F32, side="right")))
                            for q in range(2)]
                    w3b = bld.tag(c6ctx.enter_context(nc.sbuf_tensor(f"w3b{hf}", [128, 32 * 512], BF16, side="right")))

                    def w3_pair(kc, oh):
                        q = kc % 2
                        evl = bld.op("sync", "dma_start", out=w3ld[q][:],
                                     in_=w3i_h[kc][:, oh * 512: oh * 512 + 512],
                                     waits=[bld.wait_free(f"w3ld{q}")], inc=f"w3l{q}", inc_n=16)
                        evc = bld.op("vector", "tensor_copy",
                                     out=w3b[:, kc * 512:(kc + 1) * 512], in_=w3ld[q][:],
                                     waits=[evl, bld.wait_free("w3b") if kc == 0 else None],
                                     inc=True)
                        bld.set_free(f"w3ld{q}", evc)
                        bld.set_prod(f"w3b_{kc}", evc)

                    def w3_stream(oh):
                        for kc in range(32):
                            w3_pair(kc, oh)

                    ev_hp2, h2_counts = hlayer(h2i_h, h2c, hp2_dram, ev_rl,
                                               [], bg=lambda oc: w3_pair(oc, 0))
                    ev_rl2 = bld.op("gpsimd", "dma_start", out=hbuf[:], in_=hp2_dram[:],
                                    waits=ev_hp2, inc="hbr", inc_n=16)

                bld.new_era()
                if True:
                    y3 = bld.tag(c6ctx.enter_context(nc.sbuf_tensor(f"y3_{hf}", [128, 8 * 1024], F32, side="right")))
                    def D_tile(t):

                        p = t % 2
                        gtok = hf * 1024 + t * 128
                        y3t = y3[:, t * 1024:(t + 1) * 1024]
                        evy = [bld.get_prod(f"y3_t{t}_oh0"), bld.get_prod(f"y3_t{t}_oh1")]
                        ev_xl = bld.op("sync", "dma_start", out=xm_t[p][:],
                                       in_=xm_dram[gtok:gtok + 128, :],
                                       waits=[bld.get_prod(f"xm_store_{t}"),
                                              bld.wait_free(f"xm_t{p}"),
                                              bld.wait_free(f"xm_tst{p}")], inc=f"xmr{p}", inc_n=16)
                        ev_s4 = bld.op("vector", "reduce_sum", out=bs["s1"][p][:], in_=y3t,
                                       axis=AXL.X, waits=evy + [bld.wait_free(f"bstats{p}")], inc=True)
                        ev_q4 = bld.op("scalar", "activation", out=sqb[:], in_=y3t,
                                       func=AF.Square, accum_out=bs["sq"][p][:],
                                       waits=evy + [bld.wait_free("sqb")], inc=True)
                        bld.set_free("sqb", ev_q4)
                        ev_m4 = bld.op("vector", "tensor_scalar_mul", out=bs["m"][p][:],
                                       in0=bs["s1"][p][:], scalar1=1.0 / D, waits=[ev_s4], inc=True)
                        ev_t4 = bld.op("vector", "tensor_mul", out=bs["t0"][p][:], in0=bs["m"][p][:],
                                       in1=bs["m"][p][:], waits=[ev_m4], inc=True)
                        ev_v4 = bld.op("vector", "scalar_tensor_tensor", out=bs["v"][p][:],
                                       in0=bs["sq"][p][:], scalar=1.0 / D, in1=bs["t0"][p][:],
                                       op0=ALU.mult, op1=ALU.subtract, waits=[ev_q4, ev_t4], inc=True)
                        ev_l4 = bld.op("scalar", "activation", out=bs["v"][p][:], in_=bs["v"][p][:],
                                       func=AF.Ln, bias=EPS_LN, waits=[ev_v4], inc=True)
                        ev_r4 = bld.op("scalar", "activation", out=bs["r"][p][:], in_=bs["v"][p][:],
                                       func=AF.Exp, scale=-0.5, waits=[ev_l4], inc=True)
                        ev_n4 = bld.op("vector", "scalar_tensor_tensor", out=bs["nmr"][p][:],
                                       in0=bs["m"][p][:], scalar=-1.0, in1=bs["r"][p][:],
                                       op0=ALU.mult, op1=ALU.mult, waits=[ev_m4, ev_r4], inc=True)
                        bld.set_free(f"bstats{p}", ev_n4)
                        ev_o4 = bld.op("vector", "tensor_scalar", out=outt[p][:], in0=y3t,
                                       scalar1=bs["m"][p][:], scalar2=bs["r"][p][:],
                                       op0=ALU.subtract, op1=ALU.mult,
                                       waits=[ev_r4, bld.wait_free(f"outt{p}")], inc=True)
                        ev_f = bld.op("vector", "tensor_add", out=outt[p][:], in0=outt[p][:],
                                      in1=xm_t[p][:], waits=[ev_o4, ev_xl], inc=True)
                        bld.set_free(f"xm_t{p}", ev_f)
                        if t == 7:
                            bld.set_free("y3", ev_f)
                        ev_os = bld.op("sync", "dma_start", out=out_h[gtok:gtok + 128, :],
                                       in_=outt[p][:], waits=[ev_f], inc=f"os{p}", inc_n=16)
                        bld.set_free(f"outt{p}", ev_os)

                    for oh in range(2):
                        if oh == 1:
                            w3_stream(1)
                        ev_mm_last = None
                        for t in range(8):
                            pjn, pj = PJ[t % 2]
                            ev_mm = None
                            for kc in range(32):
                                ev_mm = bld.op("tensor", "matmul", out=pj[:],
                                               lhsT=hbuf[:, kc * 1024 + t * 128: kc * 1024 + t * 128 + 128],
                                               rhs=w3b[:, kc * 512:(kc + 1) * 512],
                                               start=(kc == 0), stop=(kc == 31),
                                               waits=[ev_rl2, bld.get_prod("w3b_31"),
                                                      bld.wait_free(pjn)] if kc == 0 else [],
                                               inc=(kc == 31))
                            ev_mm_last = ev_mm
                            ev_y3 = bld.op("vector", "tensor_mul",
                                           out=y3[:, t * 1024 + oh * 512: t * 1024 + oh * 512 + 512],
                                           in0=pj[:], in1=s3bc[:, oh * 512: oh * 512 + 512],
                                           waits=[ev_mm, bld.wait_free("y3") if t == 0 and oh == 0 else None],
                                           inc=True)
                            bld.set_free(pjn, ev_y3)
                            bld.set_prod(f"y3_t{t}_oh{oh}", ev_y3)
                            if oh == 1:
                                D_tile(t)
                        bld.set_free("w3b", ev_mm_last)
                        if oh == 1:
                            bld.set_free("hbuf", ev_mm_last)
                    c6ctx.close()

            bld.op("sync", "wait_ge", bld.sem_handles["os0"], bld.ctr["os0"])
            bld.op("sync", "wait_ge", bld.sem_handles["os1"], bld.ctr["os1"])

            with nc.Block() as block:
                @block.sync
                def _(eng):
                    for f in bld.streams["sync"]:
                        f(eng)

                @block.vector
                def _(eng):
                    for f in bld.streams["vector"]:
                        f(eng)

                @block.scalar
                def _(eng):
                    for f in bld.streams["scalar"]:
                        f(eng)

                @block.tensor
                def _(eng):
                    for f in bld.streams["tensor"]:
                        f(eng)

                @block.gpsimd
                def _(eng):
                    for f in bld.streams["gpsimd"]:
                        f(eng)

    return nc


def _host_inputs(inputs):
    x = np.ascontiguousarray(inputs["L0_attn_x"])
    t = np.arange(N, dtype=np.float64)[:, None]
    j = np.arange(HD, dtype=np.float64)[None, :]
    th = t * (10000.0 ** (-2.0 * j / HD))
    cos_t = np.cos(th).astype(np.float32)
    sin_t = np.sin(th).astype(np.float32)
    cosim = np.ascontiguousarray(cos_t.reshape(NT, 128, HD).transpose(1, 0, 2).reshape(128, NT * HD))
    sinim = np.ascontiguousarray(sin_t.reshape(NT, 128, HD).transpose(1, 0, 2).reshape(128, NT * HD))
    common = dict(
        Wq=np.ascontiguousarray(inputs["L0_attn_Wq"]),
        Wk=np.ascontiguousarray(inputs["L0_attn_Wk"]),
        Wv=np.ascontiguousarray(inputs["L0_attn_Wv"]),
        Wo=np.ascontiguousarray(inputs["L0_attn_Wo"]),
        Wg1=np.ascontiguousarray(inputs["L0_attn_Wg1"]),
        Wg2=np.ascontiguousarray(inputs["L0_attn_Wg2"]),
        nwc=np.ascontiguousarray(inputs["L0_attn_norm_w"].reshape(8, 128).T),
        cosim=cosim, sinim=sinim,
        tri=np.triu(np.ones((128, 128), np.float32)),
        identb=np.eye(128, dtype=np.float32).astype(ml_dtypes.bfloat16),
        onesrow=np.ones((1, 128), np.float32),
        onescol=np.ones((128, 1), np.float32),
        zrow=np.zeros((1, 512), np.float32),
        w1i=np.ascontiguousarray(inputs["L0_ffn_w1_w"].T.reshape(8, 128, 32, 128)
                                 .transpose(2, 1, 0, 3).reshape(32, 128, 1024)),
        w2i=np.ascontiguousarray(inputs["L0_ffn_w2_w"].T.reshape(8, 128, 32, 128)
                                 .transpose(2, 1, 0, 3).reshape(32, 128, 1024)),
        h1i=np.ascontiguousarray(inputs["L0_ffn_h1_w"].T.reshape(32, 128, 32, 128)
                                 .transpose(2, 1, 0, 3).reshape(32, 128, 4096)),
        h2i=np.ascontiguousarray(inputs["L0_ffn_h2_w"].T.reshape(32, 128, 32, 128)
                                 .transpose(2, 1, 0, 3).reshape(32, 128, 4096)),
        w3i=np.ascontiguousarray(inputs["L0_ffn_w3_w"].T.reshape(32, 128, 1024)),
        s1c=np.ascontiguousarray(inputs["L0_ffn_w1_s"].reshape(32, 128).T),
        s2c=np.ascontiguousarray(inputs["L0_ffn_w2_s"].reshape(32, 128).T),
        h1c=np.ascontiguousarray(inputs["L0_ffn_h1_s"].reshape(32, 128).T),
        h2c=np.ascontiguousarray(inputs["L0_ffn_h2_s"].reshape(32, 128).T),
        s3bc=np.ascontiguousarray(np.broadcast_to(inputs["L0_ffn_w3_s"][None, :], (128, 1024))),
        cvals=np.ascontiguousarray(np.broadcast_to(np.array(CONST_VALS, np.float32)[None, :],
                                                   (128, len(CONST_VALS)))),
    )
    maps = []
    for core in range(8):
        b, hf = core // 2, core % 2
        m = dict(common)
        m["xb"] = np.ascontiguousarray(x[b])
        m["x_own"] = np.ascontiguousarray(x[b, hf * HALF:(hf + 1) * HALF])
        m["half_ofs"] = np.array([[hf * HALF]], np.int32)
        maps.append(m)
    return maps


def kernel(**inputs):
    nc = build_nc()
    in_maps = _host_inputs(inputs)
    res = run_bass_kernel_spmd(nc, in_maps, core_ids=list(range(8)))
    out = np.empty((B, N, D), np.float32)
    for core in range(8):
        b, hf = core // 2, core % 2
        out[b, hf * HALF:(hf + 1) * HALF] = res.results[core]["out"]
    return out
